# version 05
# speedup vs baseline: 1.1487x; 1.0355x over previous
import numpy as np
import ml_dtypes
import concourse.bass as bass
import concourse.mybir as mybir
from concourse.bass_utils import run_bass_kernel_spmd

F32 = mybir.dt.float32
BF16 = mybir.dt.bfloat16
U8 = mybir.dt.uint8
AF = mybir.ActivationFunctionType
ALU = mybir.AluOpType
AX = mybir.AxisListType
BF = ml_dtypes.bfloat16

P = 128
SEQ = 2048
NSEQ = 2
D = 1024
DC = 8
NT = 4
MAGIC = 12582912.0
EPS = 1e-5


class Sem:
    __slots__ = ("h", "name", "val")

    def __init__(self, h, name):
        self.h = h; self.name = name; self.val = 0

    def inc(self, k=1):
        self.val += k
        return self.val


class KB:
    def __init__(self, nc):
        self.nc = nc
        self.q = {"pe": [], "act": [], "dve": [], "pool": [], "sp": []}
        self.nops = 0
        self.selfsem = None
        self.cur = "init"

    def mksem(self, name):
        return Sem(self.nc.semaphore(name).__enter__(), name)

    def op(self, eng, build, waits=(), inc=None, inck=1):
        waits = list(waits)
        if eng in ("dve", "act") and self.selfsem is not None:
            own = self.selfsem[eng]
            waits.append((own, own.val))
        waits = [(s, v) for (s, v) in waits if v > 0]
        for s, v in waits:
            assert v <= s.val, f"wait on future {s.name}: {v} > {s.val}"
        if eng in ("dve", "act") and self.selfsem is not None:
            if inc is None:
                inc = self.selfsem[eng]
                inck = 1
            else:
                assert inc is self.selfsem[eng], f"{eng} op must inc its own sem"
        if inc is not None:
            inc.inc(inck)
        self.nops += 1

        def fn(e, waits=tuple(waits), inc=inc, inck=inck, build=build):
            for s, v in waits:
                e.wait_ge(s.h, v)
            ins = build(e)
            if inc is not None:
                if ins is None:
                    ins = e.nop()
                ins.then_inc(inc.h, inck)
        self.q[eng].append((fn, self.cur))
        return inc.val if inc is not None else None


class StopBuild(Exception):
    pass


class DmaQ:

    def __init__(self, kb, eng, name, R=4):
        self.kb = kb
        self.eng = eng
        self.sems = [kb.mksem(f"{name}{i}") for i in range(R)]
        self.i = 0

    def start(self, build, waits=()):
        s = self.sems[self.i % len(self.sems)]
        self.i += 1
        w = list(waits) + [(s, s.val)]
        self.kb.op(self.eng, build, waits=w, inc=s, inck=16)
        return (s, s.val)

    def drain_waits(self):
        return [(s, s.val) for s in self.sems]


def build(dev_phases="ABCD", dev_dump=None, dev_stage=None, dev_stop=None):
    nc = bass.Bass("TRN2", target_bir_lowering=False, debug=False)
    g = {}

    def din(name, shape, dt=F32):
        g[name] = nc.dram_tensor(name, shape, dt, kind="ExternalInput").ap()

    din("x_f", [NSEQ, DC, P, SEQ])
    din("l0_wq", [D, D]); din("l0_wk", [D, D]); din("l0_wv", [D, D]); din("l0_wo", [D, D])
    din("l0_attn_norm_w", [P, DC]); din("l0_mlp_norm_w", [P, DC]); din("l0_out_norm_w", [P, 1])
    din("l0_wgate", [D, 4096]); din("l0_wup", [D, 4096]); din("l0_wdown", [4096, D])
    din("f_router_w1", [D, 256]); din("f_router_b1", [P, 2])
    din("f_router_w2", [256, 8]); din("f_router_b2", [1, 8])
    din("f_halt_w1", [D, 256]); din("f_halt_b1", [P, 2])
    din("f_halt_w2", [P, 2]); din("f_halt_b2", [1, 1])
    din("f_w1all", [2048, 512]); din("f_b1all", [P, 4])
    din("f_w2cat", [520, 3072])
    din("f_shared_w1", [2048, 64]); din("f_shared_b1", [P, 1])
    din("f_shared_w2", [64, 3072]); din("f_shared_b2", [P, 24])
    din("f_shared_weight", [1, 1])
    din("f_refine_w1", [D, 2048]); din("f_refine_b1", [P, 16])
    din("f_refine_w2", [2048, D]); din("f_refine_b2", [P, DC])
    din("f_mu", [P, DC]); din("f_iw", [P, DC])
    din("a_wq", [D, D]); din("a_wk", [D, D]); din("a_wv", [D, D]); din("a_wu", [D, D])
    din("a_qln_w", [1, D]); din("a_qln_b", [1, D]); din("a_kln_w", [1, D]); din("a_kln_b", [1, D])
    din("b_wgate", [D, 4096]); din("b_wup", [D, 4096]); din("b_wdown", [4096, D])
    din("c_ident", [P, P])
    din("c_triu", [P, P])
    din("c_trilq", [P, P])
    din("c_trilneg", [P, P])
    din("c_e8", [8, 512])
    g["y"] = nc.dram_tensor("y", [NSEQ, SEQ, D], F32, kind="ExternalOutput").ap()
    g["dbg"] = nc.dram_tensor("dbg", [DC, P, SEQ], F32, kind="ExternalOutput").ap()
    g["scr_wq"] = nc.dram_tensor("scr_wq", [3, 4096 * D], BF16, kind="Internal").ap()
    g["scr_w8"] = nc.dram_tensor("scr_w8", [8, 128], BF16, kind="Internal").ap()

    kb = KB(nc)
    ctxs = []

    def sbuf(name, shape, dt):
        t = nc.sbuf_tensor(name, shape, dt)
        ctxs.append(t)
        return t.__enter__()

    def psum(name, shape, dt=F32):
        t = nc.psum_tensor(name, shape, dt)
        ctxs.append(t)
        return t.__enter__()

    sb = {}
    sb["STREAM"] = sbuf("STREAM", [P, DC, SEQ], F32)
    sb["BIGA"] = sbuf("BIGA", [P, DC, SEQ], BF16)
    sb["BIGB"] = sbuf("BIGB", [P, DC, SEQ], BF16)
    sb["WBUF"] = sbuf("WBUF", [P, 18688], BF16)
    sb["SB2"] = sbuf("SB2", [P, 8192], BF16)
    sb["SMALL"] = sbuf("SMALL", [P, 4096], F32)
    sb["ident_b"] = sbuf("ident_b", [P, P], BF16)
    sb["triu_b"] = sbuf("triu_b", [P, P], BF16)
    sb["trilq_b"] = sbuf("trilq_b", [P, P], BF16)
    sb["trilneg_b"] = sbuf("trilneg_b", [P, P], BF16)
    sb["ones_b"] = sbuf("ones_b", [P, P], BF16)
    sb["ones_f"] = sbuf("ones_f", [P, P], F32)
    sb["e8_b"] = sbuf("e8_b", [8, 512], BF16)
    sb["ident_f"] = sbuf("ident_f", [P, P], F32)
    sb["CSTAT"] = sbuf("CSTAT", [P, 512], F32)
    for nm, w in [("l0_attn_norm_w", DC), ("l0_mlp_norm_w", DC), ("l0_out_norm_w", 1),
                  ("f_mu", DC), ("f_iw", DC), ("f_router_b1", 2), ("f_halt_b1", 2),
                  ("f_b1all", 4), ("f_shared_b1", 1), ("f_shared_b2", 24),
                  ("f_refine_b1", 16), ("f_refine_b2", DC), ("f_router_b2", 8),
                  ("f_halt_b2", 1), ("f_shared_weight", 1)]:
        sb[nm] = sbuf("s_" + nm, [P, w], F32)
    pb = [psum(f"pb{i}", [P, 512], F32) for i in range(8)]
    sb["pb"] = pb

    S = {n: kb.mksem(n) for n in ["pe", "act", "dve", "pool", "sp"]}
    kb.selfsem = {"dve": S["dve"], "act": S["act"]}
    S["qp"] = DmaQ(kb, "pool", "dma_p")
    S["qs"] = DmaQ(kb, "sp", "dma_s")

    S["qp"].start(lambda e: e.dma_start(out=sb["ident_b"][:], in_=g["c_ident"][:, :]))
    S["qp"].start(lambda e: e.dma_start(out=sb["triu_b"][:], in_=g["c_triu"][:, :]))
    S["qp"].start(lambda e: e.dma_start(out=sb["trilq_b"][:], in_=g["c_trilq"][:, :]))
    S["qp"].start(lambda e: e.dma_start(out=sb["trilneg_b"][:], in_=g["c_trilneg"][:, :]))
    S["qp"].start(lambda e: e.dma_start(out=sb["e8_b"][:], in_=g["c_e8"][:, :]))
    S["qs"].start(lambda e: e.dma_start(out=sb["ident_f"][:], in_=g["c_ident"][:, :]))
    for nm in ["l0_attn_norm_w", "l0_mlp_norm_w", "l0_out_norm_w", "f_mu", "f_iw",
               "f_router_b1", "f_halt_b1", "f_b1all", "f_shared_b1", "f_shared_b2",
               "f_refine_b1", "f_refine_b2"]:
        S["qs"].start(lambda e, nm=nm: e.dma_start(out=sb[nm][:], in_=g[nm][:, :]))
    for nm in ["f_halt_b2", "f_shared_weight"]:
        S["qs"].start(lambda e, nm=nm: e.dma_start(out=sb[nm][:], in_=g[nm].broadcast_to([P, 1])))
    S["qs"].start(lambda e: e.dma_start(out=sb["f_router_b2"][:], in_=g["f_router_b2"].broadcast_to([P, 8])))
    kb.op("dve", lambda e: e.memset(sb["ones_b"][:], 1.0))
    kb.op("dve", lambda e: e.memset(sb["ones_f"][:], 1.0),
          waits=S["qp"].drain_waits() + S["qs"].drain_waits(), inc=S["dve"])
    sb["sw_col"] = sbuf("sw_col", [P, 4], F32)
    kb.op("act", lambda e: e.activation(sb["sw_col"][:, 0:1], sb["f_shared_weight"][:, 0:1], AF.Sigmoid),
          waits=[(S["dve"], S["dve"].val)], inc=S["act"])
    kb.op("dve", lambda e: e.tensor_scalar(sb["sw_col"][:, 1:2], sb["sw_col"][:, 0:1], -1.0, 1.0,
                                           ALU.mult, ALU.add),
          waits=[(S["act"], S["act"].val)], inc=S["dve"])
    kb.op("pe", lambda e: None, waits=[(S["dve"], S["dve"].val)], inc=S["pe"])
    kb.op("act", lambda e: None, waits=[(S["dve"], S["dve"].val)], inc=S["act"])
    kb.op("pool", lambda e: None, waits=[(S["dve"], S["dve"].val)], inc=S["pool"])
    kb.op("sp", lambda e: None, waits=[(S["dve"], S["dve"].val)], inc=S["sp"])

    E = Emitters(kb, g, sb, S)
    E.dev_stage = dev_stage
    E.dev_stop = dev_stop
    done = False
    try:
        for s in range(NSEQ):
            for ph in "ABCD":
                if ph in dev_phases:
                    kb.cur = f"{ph}{s}"
                    getattr(E, "phase_" + ph)(s)
                    if dev_dump == (ph, s):
                        E.dump_stream()
                        done = True
                        break
            if done:
                break
    except StopBuild:
        pass

    kb.op("sp", lambda e: None, waits=S["qs"].drain_waits() + S["qp"].drain_waits())

    def emit(t, key):
        cur, ctx = None, None
        for fn, lab in kb.q[key]:
            if lab != cur:
                if ctx is not None:
                    ctx.__exit__(None, None, None)
                ctx = nc.named_scope(lab)
                ctx.__enter__()
                cur = lab
            fn(t)
        if ctx is not None:
            ctx.__exit__(None, None, None)

    with nc.Block() as block:
        @block.tensor
        def _(t):
            emit(t, "pe")

        @block.scalar
        def _(t):
            emit(t, "act")

        @block.vector
        def _(t):
            emit(t, "dve")

        @block.gpsimd
        def _(t):
            emit(t, "pool")

        @block.sync
        def _(t):
            emit(t, "sp")

    for t in reversed(ctxs):
        t.__exit__(None, None, None)
    return nc, g, kb


class Emitters:
    def __init__(self, kb, g, sb, S):
        self.kb = kb; self.g = g; self.sb = sb; self.S = S
        self.Wg = sb["WBUF"].ap()
        self.dev_stage = None
        self.stage_hit = False

    def mark(self):
        self.nmark = getattr(self, "nmark", 0) + 1
        if getattr(self, "dev_stop", None) is not None and self.nmark >= self.dev_stop:
            raise StopBuild()

    def stage_dump(self, name, src_ap=None):
        if self.dev_stage != name or self.stage_hit:
            return
        self.stage_hit = True
        self._dump_body(src_ap)
        raise StopBuild()

    def _dump_body(self, src_ap):
        kb, g, sb, S = self.kb, self.g, self.sb, self.S
        kb, g, sb, S = self.kb, self.g, self.sb, self.S
        src_ap = src_ap if src_ap is not None else sb["STREAM"]
        for c in range(DC):
            S["qp"].start(lambda e, c=c: e.dma_start(out=g["dbg"][c, :, :], in_=src_ap[:, c, :]),
                          waits=[(S["dve"], S["dve"].val), (S["act"], S["act"].val), (S["pe"], S["pe"].val)])

    def dump_stream(self):
        kb, g, sb, S = self.kb, self.g, self.sb, self.S
        for c in range(DC):
            S["qs"].start(lambda e, c=c: e.dma_start(out=g["dbg"][c, :, :], in_=sb["STREAM"][:, c, :]),
                          waits=[(S["dve"], S["dve"].val), (S["act"], S["act"].val), (S["pe"], S["pe"].val)])

    def norm_stream(self, kind, w_col=None, residual=None, out_bf=None):
        kb, g, sb, S = self.kb, self.g, self.sb, self.S
        ST, pb, rows = sb["STREAM"], sb["pb"], sb["SMALL"]
        sq = lambda c: sb["SB2"][:, (c % 2) * 2048:(c % 2) * 2048 + 2048]
        tb = lambda c: sb["SB2"][:, 4096 + (c % 2) * 2048:4096 + (c % 2) * 2048 + 2048]

        if residual is not None:
            for c in range(DC):
                wts, ap, mode = residual(c)
                op = ALU.add
                if mode == "x2add":
                    kb.op("dve", lambda e, c=c, ap=ap: e.scalar_tensor_tensor(
                        ST[:, c, :], ST[:, c, :], 2.0, ap, ALU.mult, ALU.add), waits=wts, inc=S["dve"])
                else:
                    kb.op("dve", lambda e, c=c, ap=ap: e.tensor_tensor(
                        ST[:, c, :], ST[:, c, :], ap, ALU.add), waits=wts, inc=S["dve"])
        add_done = S["dve"].val

        pe_hist = []
        for c in range(DC):
            w = [(S["dve"], add_done)]
            if len(pe_hist) >= 2:
                w.append((S["pe"], pe_hist[-2]))
            kb.op("act", lambda e, c=c: e.activation(sq(c), ST[:, c, :], AF.Square), waits=w, inc=S["act"])
            if kind == "ln":
                kb.op("act", lambda e, c=c: e.copy(tb(c), ST[:, c, :]), inc=S["act"])
            a_now = S["act"].val

            def mm(e, c=c):
                last = None
                for n in range(NT):
                    last = e.matmul(pb[4 + n][0:1, :], sb["ones_b"][:, 0:1], sq(c)[:, 512 * n:512 * n + 512],
                                    start=(c == 0), stop=(c == DC - 1))
                    if kind == "ln":
                        last = e.matmul(pb[n][0:1, :], sb["ones_b"][:, 0:1], tb(c)[:, 512 * n:512 * n + 512],
                                        start=(c == 0), stop=(c == DC - 1))
                return last
            pe_hist.append(kb.op("pe", mm, waits=[(S["act"], a_now)], inc=S["pe"]))
        stats_done = pe_hist[-1]

        if kind == "rms":
            def rowmath(e):
                last = None
                for n in range(NT):
                    sl = slice(512 * n, 512 * n + 512)
                    last = e.tensor_scalar(rows[0:1, sl], pb[4 + n][0:1, :], 1.0 / D, EPS, ALU.mult, ALU.add)
                return last
            kb.op("dve", rowmath, waits=[(S["pe"], stats_done)], inc=S["dve"])
        else:
            def rm1(e):
                last = None
                for n in range(NT):
                    last = e.tensor_scalar(rows[0:1, 2048 + 512 * n:2048 + 512 * n + 512],
                                           pb[n][0:1, :], 1.0 / D, None, ALU.mult)
                return last
            kb.op("dve", rm1, waits=[(S["pe"], stats_done)], inc=S["dve"])
            def rm2(e):
                last = None
                for n in range(NT):
                    mr = rows[0:1, 2048 + 512 * n:2048 + 512 * n + 512]
                    last = e.tensor_tensor(rows[0:1, 512 * n:512 * n + 512], mr, mr, ALU.mult)
                return last
            kb.op("dve", rm2, inc=S["dve"])
            def rm3(e):
                last = None
                for n in range(NT):
                    rr = rows[0:1, 512 * n:512 * n + 512]
                    last = e.scalar_tensor_tensor(rr, pb[4 + n][0:1, :], 1.0 / D, rr, ALU.mult, ALU.subtract)
                return last
            kb.op("dve", rm3, inc=S["dve"])
            def rm4(e):
                last = None
                for n in range(NT):
                    rr = rows[0:1, 512 * n:512 * n + 512]
                    last = e.tensor_scalar(rr, rr, EPS, None, ALU.add)
                return last
            kb.op("dve", rm4, inc=S["dve"])

        if getattr(self, "dev_stage", None) == "hssq" and not self.stage_hit:
            self.stage_hit = True
            S["qs"].start(lambda e: e.dma_start(out=g["dbg"][0, 0:1, :], in_=rows[0:1, 0:2048]),
                          waits=[(S["dve"], S["dve"].val)])
            raise StopBuild()
        kb.op("act", lambda e: e.activation(rows[0:1, 0:2048], rows[0:1, 0:2048], AF.Ln),
              waits=[(S["dve"], S["dve"].val)], inc=S["act"])
        kb.op("act", lambda e: e.activation(rows[0:1, 0:2048], rows[0:1, 0:2048], AF.Exp, scale=-0.5),
              inc=S["act"])
        rows_done = S["dve"].val
        if kind == "ln":
            kb.op("dve", lambda e: e.tensor_tensor(rows[0:1, 2048:4096], rows[0:1, 2048:4096],
                                                   rows[0:1, 0:2048], ALU.mult),
                  waits=[(S["act"], S["act"].val)], inc=S["dve"])
            rows_done = S["dve"].val
        else:
            kb.op("dve", lambda e: None, waits=[(S["act"], S["act"].val)], inc=S["dve"])
            rows_done = S["dve"].val

        def reps(e):
            last = None
            for n in range(NT):
                sl = slice(512 * n, 512 * n + 512)
                last = e.matmul(pb[4 + n][:, :], sb["ones_f"][0:1, :], rows[0:1, sl], start=True, stop=True)
                if kind == "ln":
                    last = e.matmul(pb[n][:, :], sb["ones_f"][0:1, :], rows[0:1, 2048 + 512 * n:2048 + 512 * n + 512], start=True, stop=True)
            return last
        kb.op("pe", reps, waits=[(S["dve"], rows_done)], inc=S["pe"])
        reps_done = S["pe"].val

        for c in range(DC):
            if kind == "rms":
                def apply(e, c=c):
                    last = None
                    for n in range(NT):
                        sl = slice(512 * n, 512 * n + 512)
                        last = e.scalar_tensor_tensor(out_bf(c)[:, sl], ST[:, c, sl], w_col[:, c:c + 1],
                                                      pb[4 + n][:, :], ALU.mult, ALU.mult)
                    return last
                kb.op("dve", apply, waits=[(S["pe"], reps_done)], inc=S["dve"])
            else:
                def ap1(e, c=c):
                    last = None
                    for n in range(NT):
                        sl = slice(512 * n, 512 * n + 512)
                        last = e.tensor_tensor(ST[:, c, sl], ST[:, c, sl], pb[4 + n][:, :], ALU.mult)
                    return last
                kb.op("dve", ap1, waits=[(S["pe"], reps_done)], inc=S["dve"])
                def ap2(e, c=c):
                    last = None
                    for n in range(NT):
                        sl = slice(512 * n, 512 * n + 512)
                        last = e.tensor_tensor(ST[:, c, sl], ST[:, c, sl], pb[n][:, :], ALU.subtract)
                    return last
                kb.op("dve", ap2, inc=S["dve"])
            if out_bf is not None and kind == "ln":
                kb.op("act", lambda e, c=c: e.copy(out_bf(c), ST[:, c, :]),
                      waits=[(S["dve"], S["dve"].val)], inc=S["act"])
        return S["dve"].val, S["act"].val

    def phase_A(self, s):
        kb, g, sb, S = self.kb, self.g, self.sb, self.S
        ST, BA, BB, SB2, SM = sb["STREAM"], sb["BIGA"], sb["BIGB"], sb["SB2"], sb["SMALL"]
        pb = sb["pb"]
        Wg = self.Wg

        ytk = list(getattr(self, "_y_toks", {}).values())
        if ytk:
            kb.op("act", lambda e: None, waits=ytk, inc=S["act"])
            kb.op("dve", lambda e: None, waits=ytk, inc=S["dve"])
            self._y_toks = {}

        x_tok = S["qs"].start(lambda e: e.dma_start(out=ST[:, :, :], in_=g["x_f"][s].transpose([1, 0, 2])),
                              waits=[(S["dve"], S["dve"].val), (S["act"], S["act"].val), (S["pe"], S["pe"].val)])
        kb.op("dve", lambda e: None, waits=[x_tok], inc=S["dve"])
        kb.op("act", lambda e: None, waits=[x_tok], inc=S["act"])

        pe_now = S["pe"].val
        qk_toks = []
        for i, nm in enumerate(["l0_wq", "l0_wk"]):
            qk_toks.append(S["qp"].start(lambda e, i=i, nm=nm: e.dma_start(
                out=Wg[:, 8192 * i:8192 * i + 8192].rearrange("p (c o) -> p c o", o=D),
                in_=g[nm].rearrange("(c p) o -> p c o", p=P)),
                waits=[(S["pe"], pe_now)]))

        h_dve, _ = self.norm_stream("rms", w_col=sb["l0_attn_norm_w"], out_bf=lambda c: BA[:, c, :])
        self.stage_dump("h", sb["BIGA"])
        if self.dev_stage == "hrows" and not self.stage_hit:
            self.stage_hit = True
            S["qs"].start(lambda e: e.dma_start(
                out=g["dbg"][0, 0:1, :], in_=sb["SMALL"][0:1, 0:2048]),
                waits=[(S["dve"], S["dve"].val), (S["act"], S["act"].val), (S["pe"], S["pe"].val)])
            raise StopBuild()

        TIb = SM.ap().bitcast(BF16)
        attnT, S_sb, osq = TIb[:, 4096:4224], TIb[:, 4224:4352], TIb[:, 4352:4480]
        rs_col = SM[0:1, 2304:2432]
        rep_f = SM[:, 2560:2688]
        S_f32 = SM[:, 2944:3072]
        for h in range(8):
            vofs = 16384 + (h % 2) * 1024
            v_tokk = S["qp"].start(lambda e, vofs=vofs, h=h: e.dma_start(
                out=Wg[:, vofs:vofs + 1024].rearrange("p (c o) -> p c o", o=128),
                in_=g["l0_wv"].rearrange("(c p) o -> p c o", p=P)[:, :, 128 * h:128 * h + 128]),
                waits=[(S["pe"], S["pe"].val)])
            for j in range(3):
                for n in range(NT):
                    bank = pb[(j * NT + n) % 2]
                    if j < 2:
                        lhs = (lambda c, j=j, h=h: Wg[:, 8192 * j + 1024 * c + 128 * h:8192 * j + 1024 * c + 128 * h + 128])
                        wwait = qk_toks[j]
                    else:
                        lhs = (lambda c, vofs=vofs: Wg[:, vofs + 128 * c:vofs + 128 * c + 128])
                        wwait = v_tokk

                    def proj(e, lhs=lhs, n=n, bank=bank):
                        last = None
                        for c in range(DC):
                            last = e.matmul(bank[:, :], lhs(c), BA[:, c, 512 * n:512 * n + 512],
                                            start=(c == 0), stop=(c == DC - 1))
                        return last
                    kb.op("pe", proj, waits=[wwait, (S["dve"], S["dve"].val),
                                             (S["act"], S["act"].val)], inc=S["pe"])
                    kb.op("act", lambda e, j=j, n=n, bank=bank: e.copy(
                        SB2[:, 2048 * j + 512 * n:2048 * j + 512 * n + 512], bank[:, :]),
                        waits=[(S["pe"], S["pe"].val)], inc=S["act"])
            qkv_done = S["act"].val
            t1 = S["qs"].start(lambda e: e.dma_start(
                out=SB2[:, 6144:8192].rearrange("p (t c) -> p t c", c=128),
                in_=SB2[:, 4096:6144], transpose=True),
                waits=[(S["act"], qkv_done)])
            t2 = S["qs"].start(lambda e: e.dma_start(
                out=SB2[:, 4096:6144].rearrange("p (t c) -> p t c", c=128),
                in_=SB2[:, 2048:4096], transpose=True),
                waits=[t1])
            kb.op("pe", lambda e: None, waits=[t1, t2], inc=S["pe"])

            qT = SB2[:, 0:2048]
            kT = SB2[:, 2048:4096]
            k_tok = SB2[:, 4096:6144].rearrange("p (t c) -> p t c", c=128)
            v_tok = SB2[:, 6144:8192].rearrange("p (t c) -> p t c", c=128)
            for c in range(16):
                cs = slice(128 * c, 128 * c + 128)
                kb.op("pe", lambda e, cs=cs: e.matmul(pb[3][:, 0:128], kT[:, cs], qT[:, cs],
                                                      start=True, stop=True), inc=S["pe"])
                kb.op("dve", lambda e: e.tensor_tensor(attnT, pb[3][:, 0:128], sb["triu_b"][:], ALU.mult),
                      waits=[(S["pe"], S["pe"].val)], inc=S["dve"])
                mask_done = S["dve"].val

                def omm(e, c=c, cs=cs):
                    e.matmul(pb[4][:, 0:128], v_tok[:, c, :], attnT, start=True, stop=(c == 0))
                    if c > 0:
                        e.matmul(pb[4][:, 0:128], S_sb, qT[:, cs], start=False, stop=True)
                    return e.matmul(pb[2][:, 0:128], k_tok[:, c, :], v_tok[:, c, :], start=True, stop=True)
                kb.op("pe", omm, waits=[(S["dve"], mask_done)], inc=S["pe"])
                o_done = S["pe"].val
                if c == 0:
                    kb.op("dve", lambda e: e.tensor_copy(S_f32, pb[2][:, 0:128]),
                          waits=[(S["pe"], o_done)], inc=S["dve"])
                else:
                    kb.op("dve", lambda e: e.tensor_tensor(S_f32, S_f32, pb[2][:, 0:128], ALU.add),
                          waits=[(S["pe"], o_done)], inc=S["dve"])
                kb.op("dve", lambda e: e.tensor_copy(S_sb, S_f32), inc=S["dve"])
                kb.op("act", lambda e, h=h, cs=cs: e.copy(BB[:, h, cs], pb[4][:, 0:128]),
                      waits=[(S["pe"], o_done)], inc=S["act"])
                kb.op("pe", lambda e: None, waits=[(S["act"], S["act"].val)], inc=S["pe"])
            row_f = SM[0:1, 1024:1536]
            for nb4 in range(4):
                bsl = slice(512 * nb4, 512 * nb4 + 512)
                sqb = TIb[:, 512 * (nb4 % 2):512 * (nb4 % 2) + 512]
                kb.op("act", lambda e, h=h, bsl=bsl, sqb=sqb: e.activation(
                    sqb, BB[:, h, bsl], AF.Square),
                    waits=[(S["dve"], S["dve"].val), (S["pe"], S["pe"].val)], inc=S["act"])
                kb.op("pe", lambda e, sqb=sqb: e.matmul(pb[5][0:1, 0:512], sb["ones_b"][:, 0:1],
                                                        sqb, start=True, stop=True),
                      waits=[(S["act"], S["act"].val)], inc=S["pe"])
                kb.op("dve", lambda e: e.tensor_scalar(row_f, pb[5][0:1, 0:512], 1.0 / P, EPS,
                                                       ALU.mult, ALU.add),
                      waits=[(S["pe"], S["pe"].val)], inc=S["dve"])
                kb.op("act", lambda e: e.activation(row_f, row_f, AF.Ln),
                      waits=[(S["dve"], S["dve"].val)], inc=S["act"])
                kb.op("act", lambda e: e.activation(row_f, row_f, AF.Exp, scale=-0.5),
                      inc=S["act"])
                kb.op("pe", lambda e: e.matmul(pb[5][:, 0:512], sb["ones_f"][0:1, :], row_f,
                                               start=True, stop=True),
                      waits=[(S["act"], S["act"].val)], inc=S["pe"])
                kb.op("dve", lambda e, h=h, bsl=bsl: e.scalar_tensor_tensor(
                    BB[:, h, bsl], pb[5][:, 0:512], sb["l0_out_norm_w"][:, 0:1], BB[:, h, bsl],
                    ALU.mult, ALU.mult),
                    waits=[(S["pe"], S["pe"].val)], inc=S["dve"])
            kb.op("pe", lambda e: None, waits=[(S["dve"], S["dve"].val)], inc=S["pe"])

        wo_tok = S["qp"].start(lambda e: e.dma_start(
            out=Wg[:, 0:8192].rearrange("p (c o) -> p c o", o=D),
            in_=g["l0_wo"].rearrange("(c p) o -> p c o", p=P)),
            waits=[(S["pe"], S["pe"].val)])
        for n in range(NT):
            for m in range(DC):
                bank = pb[(n * DC + m) % 2]

                def womm(e, n=n, m=m, bank=bank):
                    last = None
                    for k in range(DC):
                        last = e.matmul(bank[:, :], Wg[:, 1024 * k + 128 * m:1024 * k + 128 * m + 128],
                                        BB[:, k, 512 * n:512 * n + 512], start=(k == 0), stop=(k == DC - 1))
                    return last
                kb.op("pe", womm, waits=[wo_tok, (S["dve"], S["dve"].val)], inc=S["pe"])
                kb.op("dve", lambda e, n=n, m=m, bank=bank: e.tensor_tensor(
                    ST[:, m, 512 * n:512 * n + 512], ST[:, m, 512 * n:512 * n + 512], bank[:, :], ALU.add),
                    waits=[(S["pe"], S["pe"].val)], inc=S["dve"])

        self.stage_dump("x1")
        self.stage_dump("oT", sb["BIGB"])
        h2_done, _ = self.norm_stream("rms", w_col=sb["l0_mlp_norm_w"], out_bf=lambda c: BA[:, c, :])

        guT = BB.ap().rearrange("p c t -> p (c t)").rearrange("p (k n) -> p k n", n=512)
        silu0 = SM.ap().bitcast(BF16)[:, 0:512]
        silu1 = SM.ap().bitcast(BF16)[:, 512:1024]
        for n in range(NT):
            for oc in range(16):
                gofs = (oc % 2) * 4096
                uofs = 8192 + (oc % 2) * 4096
                pe_now = S["pe"].val
                g_tok = S["qp"].start(lambda e, oc=oc, gofs=gofs: e.dma_start(
                    out=Wg[:, gofs:gofs + 2048].rearrange("p (c o) -> p c o", o=256),
                    in_=g["l0_wgate"].rearrange("(c p) o -> p c o", p=P)[:, :, 256 * oc:256 * oc + 256]),
                    waits=[(S["pe"], pe_now)])
                u_tok = S["qp"].start(lambda e, oc=oc, uofs=uofs: e.dma_start(
                    out=Wg[:, uofs:uofs + 2048].rearrange("p (c o) -> p c o", o=256),
                    in_=g["l0_wup"].rearrange("(c p) o -> p c o", p=P)[:, :, 256 * oc:256 * oc + 256]))
                for m2 in range(2):
                    mg = oc * 2 + m2
                    gb, ub = pb[m2], pb[2 + m2]
                    silu = silu0 if m2 == 0 else silu1

                    def gmm(e, n=n, m2=m2, gofs=gofs, gb=gb):
                        last = None
                        for c in range(DC):
                            last = e.matmul(gb[:, :], Wg[:, gofs + 256 * c + 128 * m2:gofs + 256 * c + 128 * m2 + 128],
                                            BA[:, c, 512 * n:512 * n + 512], start=(c == 0), stop=(c == DC - 1))
                        return last
                    kb.op("pe", gmm, waits=[g_tok, u_tok, (S["dve"], h2_done),
                                            (S["dve"], S["dve"].val)], inc=S["pe"])
                    kb.op("act", lambda e, gb=gb, silu=silu: e.activation(silu, gb[:, :], AF.Silu),
                          waits=[(S["pe"], S["pe"].val), (S["dve"], S["dve"].val)], inc=S["act"])
                    silu_done = S["act"].val

                    def umm(e, n=n, m2=m2, uofs=uofs, ub=ub):
                        last = None
                        for c in range(DC):
                            last = e.matmul(ub[:, :], Wg[:, uofs + 256 * c + 128 * m2:uofs + 256 * c + 128 * m2 + 128],
                                            BA[:, c, 512 * n:512 * n + 512], start=(c == 0), stop=(c == DC - 1))
                        return last
                    kb.op("pe", umm, inc=S["pe"])
                    kb.op("dve", lambda e, mg=mg, ub=ub, silu=silu: e.tensor_tensor(
                        guT[:, mg, :], silu, ub[:, :], ALU.mult),
                        waits=[(S["pe"], S["pe"].val), (S["act"], silu_done)], inc=S["dve"])
            gu_done = S["dve"].val
            for m in range(DC):
                dofs = (m % 2) * 4096
                d_tok = S["qp"].start(lambda e, m=m, dofs=dofs: e.dma_start(
                    out=Wg[:, dofs:dofs + 4096].rearrange("p (k c) -> p k c", c=128),
                    in_=g["l0_wdown"].rearrange("(k p) o -> p k o", p=P)[:, :, 128 * m:128 * m + 128]),
                    waits=[(S["pe"], S["pe"].val)])
                bank = pb[4 + m % 2]

                def dmm(e, dofs=dofs, bank=bank):
                    last = None
                    for k in range(32):
                        last = e.matmul(bank[:, :], Wg[:, dofs + 128 * k:dofs + 128 * k + 128],
                                        guT[:, k, :], start=(k == 0), stop=(k == 31))
                    return last
                kb.op("pe", dmm, waits=[d_tok, (S["dve"], S["dve"].val)], inc=S["pe"])
                kb.op("dve", lambda e, n=n, m=m, bank=bank: e.tensor_tensor(
                    ST[:, m, 512 * n:512 * n + 512], ST[:, m, 512 * n:512 * n + 512], bank[:, :], ALU.add),
                    waits=[(S["pe"], S["pe"].val)], inc=S["dve"])

        self.stage_dump("x2")
        ld_state = {}

        entry_dve = S["dve"].val

        def residual(c):
            slot = 2048 * (c % 2)
            guard = [(S["dve"], ld_state[c - 2])] if (c - 2) in ld_state else [(S["dve"], entry_dve)]
            tok = S["qs"].start(lambda e, c=c, slot=slot: e.dma_start(out=SM[:, slot:slot + 2048],
                                                                      in_=g["x_f"][s, c]),
                                waits=guard)
            ld_state[c] = S["dve"].val + 1
            return [tok], SM[:, slot:slot + 2048], "add"
        self.norm_stream("ln", residual=residual, out_bf=lambda c: BA[:, c, :])

    def phase_B(self, s):
        kb, g, sb, S = self.kb, self.g, self.sb, self.S
        ST, BA, BB, SB2, SM = sb["STREAM"], sb["BIGA"], sb["BIGB"], sb["SB2"], sb["SMALL"]
        pb = sb["pb"]
        Wg = self.Wg
        Q = S["qp"]
        SMb = SM.ap().bitcast(BF16)
        BBf = BB.ap().rearrange("p c t -> p (c t)")
        SB2f = SB2.ap()

        self._B_g = {}
        self.Bv = dict(
            INT=lambda j: BBf[:, 512 * j:512 * j + 512],
            V=lambda j: BBf[:, 4096 + 512 * j:4096 + 512 * j + 512],
            VN=lambda j: BBf[:, 8192 + 512 * j:8192 + 512 * j + 512],
            U=lambda j: BBf[:, 12288 + 512 * j:12288 + 512 * j + 512],
            hm=lambda r, m: SB2f[:, 2048 * r + 512 * m:2048 * r + 512 * m + 512],
            hall=lambda m: SB2f[:, 4096 + 512 * m:4096 + 512 * m + 512],
            sp=lambda j: SB2f[:, 4096 + 512 * j:4096 + 512 * j + 512],
            shh=lambda c: SB2f[0:64, 6144 + 256 * c:6144 + 256 * c + 256],
            logits=SM[:, 0:32].rearrange("p (a b) -> p a b", b=8),
            m8=SM[:, 32:64].rearrange("p (a b) -> p a b", b=8),
            wcols=SM[:, 64:72],
            hrow=SM[0:1, 512:1024],
            maskT=[SMb[0:8, 2560:3072], SMb[0:8, 3072:3584]],
            masks=[SMb[:, 4096:4160].rearrange("p (a b) -> p a b", b=8),
                   SMb[:, 4160:4224].rearrange("p (a b) -> p a b", b=8)],
            stage=SMb[:, 4352:4480],
            stageT=SMb[:, 4480:4608],
            wrep=[SMb[:, 4608:5120], SMb[:, 5120:5632]],
            haltrep=SMb[:, 5632:6144],
            T=[SMb[:, 6144 + 512 * i:6144 + 512 * i + 512] for i in range(4)],
        )
        B = self.Bv

        pe_now = S["pe"].val
        tok_res = []
        tok_res.append(Q.start(lambda e: e.dma_start(
            out=Wg[:, 0:2048].rearrange("p (c o) -> p c o", o=256),
            in_=g["f_router_w1"].rearrange("(c p) o -> p c o", p=P)), waits=[(S["pe"], pe_now)]))
        tok_res.append(Q.start(lambda e: e.dma_start(
            out=Wg[:, 2048:2064].rearrange("p (c o) -> p c o", o=8),
            in_=g["f_router_w2"].rearrange("(c p) o -> p c o", p=P))))
        tok_res.append(Q.start(lambda e: e.dma_start(
            out=Wg[:, 2064:4112].rearrange("p (c o) -> p c o", o=256),
            in_=g["f_halt_w1"].rearrange("(c p) o -> p c o", p=P))))
        tok_res.append(Q.start(lambda e: e.dma_start(out=Wg[:, 4112:4114], in_=g["f_halt_w2"][:, :])))
        tok_res.append(Q.start(lambda e: e.dma_start(
            out=Wg[:, 4114:5138].rearrange("p (c o) -> p c o", o=64),
            in_=g["f_shared_w1"].rearrange("(c p) o -> p c o", p=P))))
        tok_res.append(Q.start(lambda e: e.dma_start(
            out=Wg[0:64, 5138:8210], in_=g["f_shared_w2"][:, :])))
        tok_res.append(Q.start(lambda e: e.dma_start(
            out=Wg[:, 8224:16416].rearrange("p (c o) -> p c o", o=512),
            in_=g["f_w1all"].rearrange("(c p) o -> p c o", p=P))))
        self.B_w = dict(r1=0, r2=2048, h1=2064, h2=4112, s1=4114, s2=5138, w1all=8224,
                        w2slab=[16416, 17056], rfslab=[16448, 17472], res_toks=tok_res)

        for tt in range(NT):
            self._B_tile(s, tt)

        self.norm_stream("ln", out_bf=lambda c: sb["BIGA"][:, c, :])
        self.stage_dump("x4post", sb["STREAM"])

    def _B_tile(self, s, tt):
        kb, g, sb, S = self.kb, self.g, self.sb, self.S
        BA, SM, pb, Wg = sb["BIGA"], sb["SMALL"], sb["pb"], self.Wg
        B, W = self.Bv, self.B_w
        tsl = slice(512 * tt, 512 * tt + 512)

        rhT = sb["SB2"].ap()[:, 4096:5120].rearrange("p (c o) -> p c o", o=512)
        for mc in range(2):
            bank = pb[4 + mc]

            def rmm(e, mc=mc, bank=bank):
                last = None
                for c in range(DC):
                    last = e.matmul(bank[:, :], Wg[:, W["r1"] + 256 * c + 128 * mc:W["r1"] + 256 * c + 128 * mc + 128],
                                    BA[:, c, tsl], start=(c == 0), stop=(c == DC - 1))
                return last
            kb.op("pe", rmm, waits=W["res_toks"] + [(S["dve"], S["dve"].val), (S["act"], S["act"].val)],
                  inc=S["pe"])
            kb.op("act", lambda e, mc=mc, bank=bank: e.activation(
                rhT[:, mc, :], bank[:, :], AF.Gelu, bias=sb["f_router_b1"][:, mc:mc + 1]),
                waits=[(S["pe"], S["pe"].val)], inc=S["act"])
        rh_done = S["act"].val
        self.mark()
        for st in range(4):
            def lmm(e, st=st):
                last = None
                for c in range(2):
                    last = e.matmul(pb[6][:, 0:8], rhT[:, c, 128 * st:128 * st + 128],
                                    Wg[:, W["r2"] + 8 * c:W["r2"] + 8 * c + 8],
                                    start=(c == 0), stop=(c == 1))
                return last
            kb.op("pe", lmm, waits=[(S["act"], rh_done), (S["dve"], S["dve"].val)], inc=S["pe"])
            kb.op("dve", lambda e, st=st: e.tensor_tensor(
                B["logits"][:, st, :], pb[6][:, 0:8], sb["f_router_b2"][:, :], ALU.add),
                waits=[(S["pe"], S["pe"].val)], inc=S["dve"])
            kb.op("dve", lambda e, st=st: e.max(out=B["m8"][:, st, :], in_=B["logits"][:, st, :]),
                  inc=S["dve"])
            kb.op("dve", lambda e, st=st: e.tensor_tensor(
                B["wcols"][:, 2 * st:2 * st + 1], B["m8"][:, st, 0:1], B["m8"][:, st, 1:2], ALU.subtract),
                inc=S["dve"])
            for r in range(2):
                kb.op("dve", lambda e, st=st, r=r: e.tensor_single_scalar(
                    B["masks"][r][:, st, :], B["logits"][:, st, :], B["m8"][:, st, r:r + 1], ALU.is_equal),
                    inc=S["dve"])
        self.mark()
        for r in range(2):
            def tmm(e, r=r):
                last = None
                for st in range(4):
                    last = e.matmul(pb[6][0:8, 128 * st:128 * st + 128], B["masks"][r][:, st, :],
                                    sb["ident_b"][:, :], start=True, stop=True)
                return last
            kb.op("pe", tmm, waits=[(S["dve"], S["dve"].val)], inc=S["pe"])
            kb.op("dve", lambda e, r=r: e.tensor_copy(B["maskT"][r], pb[6][0:8, :]),
                  waits=[(S["pe"], S["pe"].val)], inc=S["dve"])

        self.mark()
        for it in range(2):
            self._B_iter(s, tt, tsl, it)
            self.mark()
        self._B_refine(s, tt, tsl)
        self.mark()

    def _B_wsig(self):
        kb, g, sb, S = self.kb, self.g, self.sb, self.S
        B, pb = self.Bv, sb["pb"]

        def sig(e):
            last = None
            for st in range(4):
                last = e.activation(B["wcols"][:, 2 * st:2 * st + 1], B["wcols"][:, 2 * st:2 * st + 1],
                                    AF.Sigmoid)
            return last
        kb.op("act", sig, waits=[(S["dve"], S["dve"].val)], inc=S["act"])

        def w2c(e):
            last = None
            for st in range(4):
                last = e.tensor_scalar(B["wcols"][:, 2 * st + 1:2 * st + 2],
                                       B["wcols"][:, 2 * st:2 * st + 1], -1.0, 1.0, ALU.mult, ALU.add)
            return last
        kb.op("dve", w2c, waits=[(S["act"], S["act"].val)], inc=S["dve"])
        self.mark()
        SMb = sb["SMALL"].ap().bitcast(BF16)
        stage8 = SMb[:, 4352:4360]
        rows8 = SMb[0:8, 4480:4608]
        kb.op("dve", lambda e: e.tensor_single_scalar(
            stage8, sb["SMALL"][:, 64:72], sb["sw_col"][:, 1:2], ALU.mult),
            waits=[(S["pe"], S["pe"].val)], inc=S["dve"])
        kb.op("pe", lambda e: e.matmul(pb[7][0:8, 0:128], stage8, sb["ident_b"][:, :],
                                       start=True, stop=True),
              waits=[(S["dve"], S["dve"].val)], inc=S["pe"])
        kb.op("dve", lambda e: e.tensor_copy(rows8, pb[7][0:8, 0:128]),
              waits=[(S["pe"], S["pe"].val)], inc=S["dve"])
        stok = S["qs"].start(lambda e: e.dma_start(out=g["scr_w8"][:, :], in_=rows8),
                             waits=[(S["dve"], S["dve"].val)])
        toks = []
        for st in range(4):
            for k in range(2):
                i = 2 * st + k
                toks.append(S["qs"].start(lambda e, st=st, k=k, i=i: e.dma_start(
                    out=B["wrep"][k][:, 128 * st:128 * st + 128],
                    in_=g["scr_w8"][i:i + 1, :].broadcast_to([128, 128])),
                    waits=[stok]))
        self._wrep_toks = toks
        self.mark()

    def _B_iter(self, s, tt, tsl, it):
        kb, g, sb, S = self.kb, self.g, self.sb, self.S
        BA, pb, Wg = sb["BIGA"], sb["pb"], self.Wg
        B, W = self.Bv, self.B_w
        Q = S["qp"]
        nk = 8 if it == 0 else 16

        def intop(j):
            return BA[:, j, tsl] if it == 0 else B["INT"](j)

        def ctxop(c):
            return intop(c) if c < 8 else B["V"](c - 8)

        for mc in range(2):
            def hmm(e, mc=mc):
                last = None
                for c in range(DC):
                    last = e.matmul(pb[4 + mc][:, :], Wg[:, W["h1"] + 256 * c + 128 * mc:W["h1"] + 256 * c + 128 * mc + 128],
                                    intop(c), start=(c == 0), stop=(c == DC - 1))
                return last
            kb.op("pe", hmm, waits=[(S["dve"], S["dve"].val), (S["act"], S["act"].val)], inc=S["pe"])
            kb.op("act", lambda e, mc=mc: e.activation(
                B["T"][mc], pb[4 + mc][:, :], AF.Gelu, bias=sb["f_halt_b1"][:, mc:mc + 1]),
                waits=[(S["pe"], S["pe"].val), (S["dve"], S["dve"].val)], inc=S["act"])
        hh_done = S["act"].val
        self.mark()

        def hrmm(e):
            last = None
            for c in range(2):
                last = e.matmul(pb[6][0:1, :], Wg[:, W["h2"] + c:W["h2"] + c + 1], B["T"][c],
                                start=(c == 0), stop=(c == 1))
            return last
        kb.op("pe", hrmm, waits=[(S["act"], hh_done)], inc=S["pe"])
        kb.op("dve", lambda e: e.tensor_copy(B["hrow"], pb[6][0:1, :]),
              waits=[(S["pe"], S["pe"].val)], inc=S["dve"])

        def shmm(e):
            last = None
            for c in range(nk):
                last = e.matmul(pb[4][0:64, :], Wg[:, W["s1"] + 64 * c:W["s1"] + 64 * c + 64],
                                ctxop(c), start=(c == 0), stop=(c == nk - 1))
            return last
        kb.op("pe", shmm, waits=[(S["dve"], S["dve"].val)], inc=S["pe"])
        shh = sb["SB2"].ap()[0:64, 6144:6656]
        kb.op("act", lambda e: e.activation(shh, pb[4][0:64, :], AF.Gelu,
                                            bias=sb["f_shared_b1"][0:64, 0:1]),
              waits=[(S["pe"], S["pe"].val)], inc=S["act"])

        for m in range(4):
            def amm(e, m=m):
                last = None
                for c in range(nk):
                    last = e.matmul(pb[m][:, :], Wg[:, W["w1all"] + 512 * c + 128 * m:W["w1all"] + 512 * c + 128 * m + 128],
                                    ctxop(c), start=(c == 0), stop=(c == nk - 1))
                return last
            kb.op("pe", amm, waits=[(S["dve"], S["dve"].val), (S["act"], S["act"].val)], inc=S["pe"])
            kb.op("act", lambda e, m=m: e.activation(
                B["hall"](m), pb[m][:, :], AF.Gelu, bias=sb["f_b1all"][:, m:m + 1]),
                waits=[(S["pe"], S["pe"].val)], inc=S["act"])
        hall_done = S["act"].val
        self._B_g["lnbank"] = {0: [(S["act"], hall_done)], 1: [(S["act"], hall_done)]}
        self._B_g["sgab"] = {0: [(S["act"], hall_done)], 1: [(S["act"], hall_done)]}
        self.mark()

        for r in range(2):
            for m in range(4):
                bank = pb[4 + (m % 2)]
                kb.op("pe", lambda e, r=r, m=m, bank=bank: e.matmul(
                    bank[:, :], sb["e8_b"][:, 128 * m:128 * m + 128], B["maskT"][r],
                    start=True, stop=True),
                    waits=[(S["act"], hall_done), (S["dve"], S["dve"].val)], inc=S["pe"])
                kb.op("dve", lambda e, r=r, m=m, bank=bank: e.tensor_tensor(
                    B["hm"](r, m), B["hall"](m), bank[:, :], ALU.mult),
                    waits=[(S["pe"], S["pe"].val)], inc=S["dve"])
        hm_done = S["dve"].val
        self._B_g["sgs2"] = {0: [(S["dve"], hm_done)], 1: [(S["dve"], hm_done)]}
        self.mark()

        for p in range(3):
            shared = (p == 0)
            r = p - 1

            def cmm(e, m, bank, shared=shared, r=r):
                if shared:
                    return e.matmul(bank[:, :], Wg[0:64, W["s2"] + 128 * m:W["s2"] + 128 * m + 128],
                                    shh, start=True, stop=True)
                slab = W["w2slab"][m % 2]
                last = None
                for c in range(4):
                    last = e.matmul(bank[:, :], Wg[:, slab + 128 * c:slab + 128 * c + 128],
                                    B["hm"](r, c), start=(c == 0), stop=False)
                return e.matmul(bank[:, :], Wg[0:8, slab + 512:slab + 512 + 128],
                                B["maskT"][r], start=False, stop=True)

            def slab_load(m, extra=()):
                if shared:
                    return []
                slab = W["w2slab"][m % 2]
                t1 = Q.start(lambda e, m=m, slab=slab: e.dma_start(
                    out=Wg[:, slab:slab + 512].rearrange("p (c o) -> p c o", o=128),
                    in_=g["f_w2cat"][0:512].rearrange("(c p) o -> p c o", p=P)[:, :, 128 * m:128 * m + 128]),
                    waits=list(extra))
                t2 = Q.start(lambda e, m=m, slab=slab: e.dma_start(
                    out=Wg[0:8, slab + 512:slab + 512 + 128],
                    in_=g["f_w2cat"][512:520, 128 * m:128 * m + 128]))
                return [t1, t2]

            gbank = self._B_g.setdefault("lnbank", {0: [], 1: []})
            gtmp = self._B_g.setdefault("lntmp", {0: [], 1: []})
            for j in range(DC):
                m = 8 + j
                p2 = j % 2
                Ta, Tb = (B["T"][2], B["T"][3]) if p2 == 0 else (B["T"][0], B["T"][1])
                bank = pb[p2]
                toks = slab_load(m, extra=[(S["pe"], S["pe"].val)])
                kb.op("pe", lambda e, m=m, bank=bank: cmm(e, m, bank),
                      waits=toks + gbank[p2] + [(S["dve"], hm_done)], inc=S["pe"])
                mm_done = S["pe"].val
                bias = sb["f_shared_b2"][:, m:m + 1] if shared else 0.0
                kb.op("act", lambda e, bank=bank, bias=bias, Ta=Ta: e.activation(
                    Ta, bank[:, :], AF.Abs, bias=bias),
                    waits=[(S["pe"], mm_done)] + gtmp[p2], inc=S["act"])
                kb.op("act", lambda e, Ta=Ta: e.activation(Ta, Ta, AF.Exp, scale=-1.0),
                      inc=S["act"])
                kb.op("act", lambda e, Ta=Ta: e.activation(Ta, Ta, AF.Ln, bias=1.0),
                      inc=S["act"])
                kb.op("act", lambda e, bank=bank, bias=bias, Tb=Tb: e.activation(
                    Tb, bank[:, :], AF.Relu, bias=bias),
                    inc=S["act"])
                gbank[p2] = [(S["act"], S["act"].val)]
                kb.op("dve", lambda e, j=j, Ta=Ta, Tb=Tb: e.tensor_tensor(
                    B["sp"](j), Ta, Tb, ALU.add),
                    waits=[(S["act"], S["act"].val)], inc=S["dve"])
                gtmp[p2] = [(S["dve"], S["dve"].val)]
            self.mark()

            if shared:
                kb.op("act", lambda e: e.activation(B["hrow"], B["hrow"], AF.Sigmoid,
                                                    bias=sb["f_halt_b2"][0:1, 0:1]),
                      waits=[(S["dve"], S["dve"].val)], inc=S["act"])
                self.mark()
                if it == 0:
                    self._B_wsig()
                    kb.op("dve", lambda e: None, waits=self._wrep_toks, inc=S["dve"])
                self.mark()
                kb.op("pe", lambda e: e.matmul(pb[7][:, 0:512], sb["ones_f"][0:1, :],
                                               B["hrow"], start=True, stop=True),
                      waits=[(S["act"], S["act"].val), (S["dve"], S["dve"].val)], inc=S["pe"])
                kb.op("dve", lambda e: e.tensor_copy(B["haltrep"], pb[7][:, 0:512]),
                      waits=[(S["pe"], S["pe"].val)], inc=S["dve"])
                self._B_g["sgs2"][0] = [(S["dve"], S["dve"].val)]
                self._B_g["sgs2"][1] = [(S["dve"], S["dve"].val)]
                self.mark()

            gab = self._B_g.setdefault("sgab", {0: [], 1: []})
            ggb = self._B_g.setdefault("sggb", {0: [], 1: []})
            gs2 = self._B_g.setdefault("sgs2", {0: [], 1: []})
            for j in range(DC):
                p2 = j % 2
                ab = pb[2 + p2]
                gb = pb[p2]
                sa = pb[4 + p2]
                sg2 = pb[6 + p2]
                if it > 0:
                    toks_a = slab_load(j, extra=[(S["pe"], S["pe"].val)])
                    kb.op("pe", lambda e, j=j, ab=ab: cmm(e, j, ab),
                          waits=toks_a + gab[p2], inc=S["pe"])
                    a_done = S["pe"].val
                    bias_a = sb["f_shared_b2"][:, j:j + 1] if shared else 0.0
                    kb.op("act", lambda e, ab=ab, bias_a=bias_a, sa=sa: e.activation(
                        sa[:, :], ab[:, :], AF.Sigmoid, bias=bias_a),
                        waits=[(S["pe"], a_done)] + gs2[p2], inc=S["act"])
                    gab[p2] = [(S["act"], S["act"].val)]
                    kb.op("dve", lambda e, j=j, sa=sa: e.tensor_tensor(
                        B["T"][3], sa[:, :], B["V"](j), ALU.mult),
                        waits=[(S["act"], S["act"].val)], inc=S["dve"])
                t1_done = S["dve"].val
                toks_g = slab_load(16 + j, extra=[(S["pe"], S["pe"].val)])
                kb.op("pe", lambda e, j=j, gb=gb: cmm(e, 16 + j, gb),
                      waits=toks_g + gbank[p2], inc=S["pe"])
                g_done = S["pe"].val
                bias_g = sb["f_shared_b2"][:, 16 + j:16 + j + 1] if shared else 0.0
                kb.op("act", lambda e, gb=gb, bias_g=bias_g, sg2=sg2: e.activation(
                    sg2[:, :], gb[:, :], AF.Sigmoid, bias=bias_g),
                    waits=[(S["pe"], g_done)] + gs2[p2], inc=S["act"])
                gbank[p2] = [(S["act"], S["act"].val)]
                sg_done = S["act"].val
                kb.op("dve", lambda e, j=j: e.tensor_tensor(B["T"][1], B["sp"](j), intop(j), ALU.mult),
                      inc=S["dve"])
                kb.op("dve", lambda e, j=j: e.scalar_tensor_tensor(
                    B["T"][1], B["sp"](j), sb["f_mu"][:, j:j + 1], B["T"][1], ALU.mult, ALU.subtract),
                    inc=S["dve"])
                if it > 0:
                    kb.op("dve", lambda e: e.tensor_tensor(B["T"][1], B["T"][3], B["T"][1], ALU.add),
                          inc=S["dve"])
                kb.op("dve", lambda e, sg2=sg2: e.tensor_tensor(
                    B["T"][3], sg2[:, :], B["T"][1], ALU.mult),
                    waits=[(S["act"], sg_done)], inc=S["dve"])
                gs2[p2] = [(S["dve"], S["dve"].val)]
                self.mark()
                if shared:
                    kb.op("dve", lambda e, j=j: e.tensor_single_scalar(
                        B["VN"](j), B["T"][1], sb["sw_col"][:, 0:1], ALU.mult), inc=S["dve"])
                    kb.op("dve", lambda e, j=j: e.tensor_single_scalar(
                        B["U"](j), B["T"][3], sb["sw_col"][:, 0:1], ALU.mult), inc=S["dve"])
                else:
                    kb.op("dve", lambda e, j=j, r=r: e.tensor_tensor(
                        B["T"][1], B["T"][1], B["wrep"][r], ALU.mult), inc=S["dve"])
                    kb.op("dve", lambda e, j=j: e.tensor_tensor(
                        B["VN"](j), B["VN"](j), B["T"][1], ALU.add), inc=S["dve"])
                    kb.op("dve", lambda e, j=j, r=r: e.tensor_tensor(
                        B["T"][3], B["T"][3], B["wrep"][r], ALU.mult), inc=S["dve"])
                    kb.op("dve", lambda e, j=j: e.tensor_tensor(
                        B["U"](j), B["U"](j), B["T"][3], ALU.add), inc=S["dve"])
            gtmp[0] = [(S["dve"], S["dve"].val)]
            gtmp[1] = [(S["dve"], S["dve"].val)]

        gtmp = self._B_g["lntmp"]
        for j in range(DC):
            kb.op("dve", lambda e, j=j: e.tensor_tensor(B["T"][2], B["U"](j), B["haltrep"], ALU.mult),
                  inc=S["dve"])
            kb.op("dve", lambda e, j=j: e.scalar_tensor_tensor(
                B["INT"](j), B["T"][2], 0.1, intop(j), ALU.mult, ALU.add), inc=S["dve"])
            if it == 0:
                kb.op("dve", lambda e, j=j: e.tensor_tensor(B["V"](j), B["VN"](j), B["haltrep"], ALU.mult),
                      inc=S["dve"])
            else:
                kb.op("dve", lambda e, j=j: e.tensor_tensor(B["T"][2], B["VN"](j), B["V"](j), ALU.subtract),
                      inc=S["dve"])
                kb.op("dve", lambda e, j=j: e.tensor_tensor(B["T"][2], B["T"][2], B["haltrep"], ALU.mult),
                      inc=S["dve"])
                kb.op("dve", lambda e, j=j: e.tensor_tensor(B["V"](j), B["T"][2], B["V"](j), ALU.add),
                      inc=S["dve"])
        gtmp[0] = [(S["dve"], S["dve"].val)]
        gtmp[1] = [(S["dve"], S["dve"].val)]
        for key, base in [(f"int{it}", 0), (f"v{it}", 4096)]:
            if self.dev_stage == key and tt == 0 and not self.stage_hit:
                self.stage_hit = True
                for half in range(2):
                    S["qp"].start(lambda e, half=half, base=base: e.dma_start(
                        out=self.g["dbg"][half, :, :],
                        in_=sb["BIGB"].ap().rearrange("p c t -> p (c t)")[:, base + 2048 * half:base + 2048 * half + 2048]),
                        waits=[(S["dve"], S["dve"].val)])
                raise StopBuild()

    def _B_refine(self, s, tt, tsl):
        kb, g, sb, S = self.kb, self.g, self.sb, self.S
        ST, pb, Wg = sb["STREAM"], sb["pb"], self.Wg
        B, W = self.Bv, self.B_w
        Q = S["qp"]
        SB2f = sb["SB2"].ap()
        r1T = lambda m: SB2f[:, 512 * m:512 * m + 512]
        tmp = sb["SMALL"][:, 2048:2560]

        for m in range(16):
            slab = W["rfslab"][m % 2]
            tok = Q.start(lambda e, m=m, slab=slab: e.dma_start(
                out=Wg[:, slab:slab + 1024].rearrange("p (c o) -> p c o", o=128),
                in_=g["f_refine_w1"].rearrange("(c p) o -> p c o", p=P)[:, :, 128 * m:128 * m + 128]),
                waits=[(S["pe"], S["pe"].val)])
            bank = pb[m % 2]

            def r1mm(e, m=m, slab=slab, bank=bank):
                last = None
                for c in range(DC):
                    last = e.matmul(bank[:, :], Wg[:, slab + 128 * c:slab + 128 * c + 128],
                                    B["INT"](c), start=(c == 0), stop=(c == DC - 1))
                return last
            kb.op("pe", r1mm, waits=[tok, (S["act"], S["act"].val), (S["dve"], S["dve"].val)], inc=S["pe"])
            kb.op("act", lambda e, m=m, bank=bank: e.activation(
                r1T(m), bank[:, :], AF.Gelu, bias=sb["f_refine_b1"][:, m:m + 1]),
                waits=[(S["pe"], S["pe"].val)], inc=S["act"])
        r1_done = S["act"].val

        for m in range(DC):
            slab = W["rfslab"][0]
            tok = Q.start(lambda e, m=m, slab=slab: e.dma_start(
                out=Wg[:, slab:slab + 2048].rearrange("p (c o) -> p c o", o=128),
                in_=g["f_refine_w2"].rearrange("(c p) o -> p c o", p=P)[:, :, 128 * m:128 * m + 128]),
                waits=[(S["pe"], S["pe"].val)])
            bank = pb[4 + m % 2]

            def r2mm(e, m=m, slab=slab, bank=bank):
                last = None
                for c in range(16):
                    last = e.matmul(bank[:, :], Wg[:, slab + 128 * c:slab + 128 * c + 128],
                                    r1T(c), start=(c == 0), stop=(c == 15))
                return last
            kb.op("pe", r2mm, waits=[tok, (S["act"], r1_done), (S["dve"], S["dve"].val)], inc=S["pe"])
            kb.op("dve", lambda e, m=m, bank=bank: e.tensor_scalar(
                tmp, bank[:, :], sb["f_refine_b2"][:, m:m + 1], sb["f_iw"][:, m:m + 1],
                ALU.add, ALU.mult),
                waits=[(S["pe"], S["pe"].val)], inc=S["dve"])
            kb.op("dve", lambda e, m=m: e.scalar_tensor_tensor(
                ST[:, m, tsl], ST[:, m, tsl], 2.0, tmp, ALU.mult, ALU.add), inc=S["dve"])

    def phase_C(self, s):
        kb, g, sb, S = self.kb, self.g, self.sb, self.S
        ST, BA, BB, SB2, SM = sb["STREAM"], sb["BIGA"], sb["BIGB"], sb["SB2"], sb["SMALL"]
        pb = sb["pb"]
        Wg = self.Wg
        Q = S["qp"]
        SMb = SM.ap().bitcast(BF16)
        CS = sb["CSTAT"]
        CSb = CS.ap().bitcast(BF16)

        qkv = Wg[:, 0:12288].rearrange("p (w pr t c) -> p w pr t c", w=3, pr=2, t=16)
        qhatT = Wg[:, 12288:14336].rearrange("p (t c) -> p t c", c=128)
        khatT = Wg[:, 14336:16384].rearrange("p (t c) -> p t c", c=128)
        slabs = [16384, 17408]
        sq_tmp = SMb[:, 0:4096]
        murep = SMb[:, 4096:8192]

        lnrow_ofs = dict(a_qln_w=16384, a_qln_b=16640, a_kln_w=16896, a_kln_b=17152)
        for quad in range(4):
            row_toks = []
            for nm, ofs in lnrow_ofs.items():
                row_toks.append(Q.start(lambda e, nm=nm, ofs=ofs: e.dma_start(
                    out=Wg[:, ofs:ofs + 256],
                    in_=g[nm][:, 256 * quad:256 * quad + 256].broadcast_to([P, 256])),
                    waits=[(S["pe"], S["pe"].val)]))
            oc = 256 * quad
            tok_qk = []
            for w, wname in enumerate(["a_wq", "a_wk"]):
                tok_qk.append(Q.start(lambda e, w=w, wname=wname, oc=oc: e.dma_start(
                    out=Wg[:, 12288 + 2048 * w:12288 + 2048 * w + 2048]
                    .rearrange("p (c o) -> p c o", o=256),
                    in_=g[wname].rearrange("(c p) o -> p c o", p=P)[:, :, oc:oc + 256]),
                    waits=[(S["pe"], S["pe"].val)]))
            pguard = {0: [(S["act"], S["act"].val), (S["dve"], S["dve"].val)],
                      1: [(S["act"], S["act"].val), (S["dve"], S["dve"].val)]}
            for t in range(16):
                bank = pb[t % 2]

                def pmm(e, t=t, bank=bank):
                    last = None
                    for c in range(DC):
                        for w in range(2):
                            last = e.matmul(bank[:, 256 * w:256 * w + 256],
                                            BA[:, c, 128 * t:128 * t + 128],
                                            Wg[:, 12288 + 2048 * w + 256 * c:
                                               12288 + 2048 * w + 256 * c + 256],
                                            start=(c == 0 and w == 0),
                                            stop=(c == DC - 1 and w == 1))
                    return last
                kb.op("pe", pmm, waits=tok_qk + pguard[t % 2], inc=S["pe"])
                kb.op("act", lambda e, t=t, bank=bank: e.copy(
                    qkv[:, 0:2, :, t, :], bank[:, 0:512]),
                    waits=[(S["pe"], S["pe"].val)], inc=S["act"])
                pguard[t % 2] = [(S["act"], S["act"].val)]
            kb.op("pe", lambda e: None, waits=[(S["pe"], S["pe"].val)], inc=S["pe"])
            tok_v = Q.start(lambda e, oc=oc: e.dma_start(
                out=Wg[:, 12288:14336].rearrange("p (c o) -> p c o", o=256),
                in_=g["a_wv"].rearrange("(c p) o -> p c o", p=P)[:, :, oc:oc + 256]),
                waits=[(S["pe"], S["pe"].val)])
            vguard = {0: [(S["act"], S["act"].val), (S["dve"], S["dve"].val)],
                      1: [(S["act"], S["act"].val), (S["dve"], S["dve"].val)]}
            for t in range(16):
                vbank = pb[2 + t % 2]

                def vmm(e, t=t, vbank=vbank):
                    last = None
                    for c in range(DC):
                        last = e.matmul(vbank[:, 0:256], BA[:, c, 128 * t:128 * t + 128],
                                        Wg[:, 12288 + 256 * c:12288 + 256 * c + 256],
                                        start=(c == 0), stop=(c == DC - 1))
                    return last
                kb.op("pe", vmm, waits=[tok_v] + vguard[t % 2], inc=S["pe"])
                kb.op("act", lambda e, t=t, vbank=vbank: e.copy(
                    qkv[:, 2, :, t, :], vbank[:, 0:256]),
                    waits=[(S["pe"], S["pe"].val)], inc=S["act"])
                vguard[t % 2] = [(S["act"], S["act"].val)]
            proj_done = S["act"].val

            for w, (wr, br) in enumerate([("a_qln_w", "a_qln_b"), ("a_kln_w", "a_kln_b")]):
                flat = qkv[:, w, :, :, :].rearrange("p pr t c -> p (pr t c)")
                v64 = flat.rearrange("p (a b) -> p a b", b=64)
                kb.op("act", lambda e, flat=flat: e.activation(sq_tmp, flat, AF.Square),
                      waits=[(S["act"], proj_done), (S["dve"], S["dve"].val)], inc=S["act"])
                kb.op("dve", lambda e, v64=v64: e.reduce_sum(CS[:, 256:320], v64, AX.X),
                      waits=[(S["act"], S["act"].val)], inc=S["dve"])
                kb.op("dve", lambda e: e.reduce_sum(
                    CS[:, 320:384], sq_tmp.rearrange("p (a b) -> p a b", b=64), AX.X), inc=S["dve"])
                kb.op("dve", lambda e: e.tensor_scalar(CS[:, 256:320], CS[:, 256:320],
                                                       1.0 / 64, None, ALU.mult), inc=S["dve"])
                kb.op("dve", lambda e: e.tensor_tensor(CS[:, 384:448], CS[:, 256:320],
                                                       CS[:, 256:320], ALU.mult), inc=S["dve"])
                kb.op("dve", lambda e: e.scalar_tensor_tensor(
                    CS[:, 320:384], CS[:, 320:384], 1.0 / 64, CS[:, 384:448],
                    ALU.mult, ALU.subtract), inc=S["dve"])
                kb.op("dve", lambda e: e.tensor_scalar(CS[:, 320:384], CS[:, 320:384],
                                                       EPS, None, ALU.add), inc=S["dve"])
                kb.op("act", lambda e: e.activation(CS[:, 320:384], CS[:, 320:384], AF.Ln),
                      waits=[(S["dve"], S["dve"].val)], inc=S["act"])
                kb.op("act", lambda e: e.activation(CS[:, 320:384], CS[:, 320:384], AF.Exp, scale=-0.5),
                      inc=S["act"])
                kb.op("dve", lambda e: e.tensor_copy(
                    murep.rearrange("p (a b) -> p a b", b=64),
                    CS[:, 256:320].rearrange("p (a b) -> p a b", b=1).broadcast_to([P, 64, 64])),
                    waits=[(S["act"], S["act"].val)], inc=S["dve"])
                kb.op("dve", lambda e, flat=flat: e.tensor_tensor(flat, flat, murep, ALU.subtract),
                      inc=S["dve"])
                kb.op("dve", lambda e: e.tensor_copy(
                    murep.rearrange("p (a b) -> p a b", b=64),
                    CS[:, 320:384].rearrange("p (a b) -> p a b", b=1).broadcast_to([P, 64, 64])),
                    inc=S["dve"])
                kb.op("dve", lambda e, flat=flat: e.tensor_tensor(flat, flat, murep, ALU.mult),
                      inc=S["dve"])
                v4d = qkv[:, w, :, :, :]
                wrow = Wg[:, lnrow_ofs[wr]:lnrow_ofs[wr] + 256].rearrange(
                    "p (pr c) -> p pr c", pr=2).unsqueeze(2).broadcast_to([P, 2, 16, 128])
                brow = Wg[:, lnrow_ofs[br]:lnrow_ofs[br] + 256].rearrange(
                    "p (pr c) -> p pr c", pr=2).unsqueeze(2).broadcast_to([P, 2, 16, 128])
                kb.op("dve", lambda e, v4d=v4d, wrow=wrow: e.tensor_tensor(v4d, v4d, wrow, ALU.mult),
                      waits=row_toks, inc=S["dve"])
                kb.op("dve", lambda e, v4d=v4d, brow=brow: e.tensor_tensor(v4d, v4d, brow, ALU.add),
                      inc=S["dve"])
            ln_done = S["dve"].val

            for pr2 in range(2):
                pair = 2 * quad + pr2
                tq = S["qs"].start(lambda e, pr2=pr2: e.dma_start(
                    out=qhatT[:, :, :], in_=qkv[:, 0, pr2, :, :].rearrange("p t c -> p (t c)"),
                    transpose=True), waits=[(S["dve"], ln_done), (S["pe"], S["pe"].val)])
                tk = S["qs"].start(lambda e, pr2=pr2: e.dma_start(
                    out=khatT[:, :, :], in_=qkv[:, 1, pr2, :, :].rearrange("p t c -> p (t c)"),
                    transpose=True), waits=[(S["dve"], ln_done), (S["pe"], S["pe"].val)])
                qhf = qhatT.rearrange("p t c -> p (t c)")
                khf = khatT.rearrange("p t c -> p (t c)")
                prev = None
                cbg = {k: [(S["dve"], S["dve"].val), (S["act"], S["act"].val)] for k in range(6)}
                self._C_cbg = cbg
                for qi in range(16):
                    Wi = 128 * (qi + 1)
                    nb = (Wi + 511) // 512
                    par = qi % 2
                    tp_toks = []
                    for h2 in range(2):
                        hofs = 64 * h2
                        bidx = list(range(4)) if h2 == 0 else [4, 5, 0, 1]
                        banks = [pb[b] for b in bidx]
                        for b in range(nb):
                            w0 = 512 * b
                            wspan = min(512, Wi - w0)
                            wts = list(cbg[bidx[b]])
                            if qi == 0 and b == 0:
                                wts += [tq, tk]
                            kb.op("pe", lambda e, qi=qi, hofs=hofs, w0=w0, wspan=wspan, bank=banks[b]:
                                  e.matmul(bank[:, 0:wspan],
                                           qhf[hofs:hofs + 64, 128 * qi:128 * qi + 128],
                                           khf[hofs:hofs + 64, w0:w0 + wspan],
                                           start=True, stop=True, tile_position=(hofs, 0)),
                                  waits=wts, inc=S["pe"])
                        mm_done = S["pe"].val
                        dspan = Wi - 512 * (nb - 1)
                        kb.op("dve", lambda e, bank=banks[nb - 1], dspan=dspan: e.tensor_tensor(
                            bank[:, dspan - 128:dspan], bank[:, dspan - 128:dspan],
                            sb["trilneg_b"][:, :], ALU.add),
                            waits=[(S["pe"], mm_done)], inc=S["dve"])
                        for b in range(nb):
                            wspan = min(512, Wi - 512 * b)
                            kb.op("dve", lambda e, h2=h2, b=b, bank=banks[b], wspan=wspan: e.reduce_max(
                                CS[:, 16 * h2 + b:16 * h2 + b + 1], bank[:, 0:wspan], AX.X),
                                waits=[(S["pe"], mm_done)], inc=S["dve"])

                        kb.op("dve", lambda e, h2=h2: e.tensor_copy(
                            CS[:, 16 * h2 + 9:16 * h2 + 10], CS[:, 16 * h2:16 * h2 + 1]), inc=S["dve"])
                        for b in range(1, nb):
                            kb.op("dve", lambda e, h2=h2, b=b: e.tensor_tensor(
                                CS[:, 16 * h2 + 9:16 * h2 + 10], CS[:, 16 * h2 + 9:16 * h2 + 10],
                                CS[:, 16 * h2 + b:16 * h2 + b + 1], ALU.max), inc=S["dve"])
                        kb.op("dve", lambda e, h2=h2: e.tensor_scalar(
                            CS[:, 16 * h2 + 9:16 * h2 + 10], CS[:, 16 * h2 + 9:16 * h2 + 10],
                            -0.125, None, ALU.mult), inc=S["dve"])
                        m_done = S["dve"].val
                        pbuf = SMb[:, 4096 * par + 2048 * h2:4096 * par + 2048 * h2 + 2048]
                        sum_cols = []
                        for b in range(nb):
                            w0 = 512 * b
                            wspan = min(512, Wi - w0)
                            sum_cols.append(16 * h2 + 4 + b)
                            kb.op("act", lambda e, bank=banks[b], pbuf=pbuf, w0=w0, wspan=wspan, h2=h2, b=b:
                                  e.activation(pbuf[:, w0:w0 + wspan], bank[:, 0:wspan], AF.Exp,
                                               scale=0.125, bias=CS[:, 16 * h2 + 9:16 * h2 + 10],
                                               accum_out=CS[:, 16 * h2 + 4 + b:16 * h2 + 5 + b]),
                                  waits=[(S["dve"], m_done)], inc=S["act"])
                        for b in range(nb):
                            cbg[bidx[b]] = [(S["dve"], m_done), (S["act"], S["act"].val)]

                        rdsl = slice(16 * h2 + 10 + par, 16 * h2 + 11 + par)
                        kb.op("dve", lambda e, rdsl=rdsl, c0=sum_cols[0]: e.tensor_copy(
                            CS[:, rdsl], CS[:, c0:c0 + 1]),
                            waits=[(S["act"], S["act"].val)], inc=S["dve"])
                        for ccol in sum_cols[1:]:
                            kb.op("dve", lambda e, rdsl=rdsl, ccol=ccol: e.tensor_tensor(
                                CS[:, rdsl], CS[:, rdsl], CS[:, ccol:ccol + 1], ALU.add), inc=S["dve"])
                        kb.op("dve", lambda e, rdsl=rdsl: e.reciprocal(CS[:, rdsl], CS[:, rdsl]),
                              inc=S["dve"])
                        kb.op("dve", lambda e, rdsl=rdsl, h2=h2, par=par: e.tensor_copy(
                            CSb[:, 64 + 4 * h2 + 2 * par:64 + 4 * h2 + 2 * par + 1], CS[:, rdsl]),
                            inc=S["dve"])
                        ptre = sb["SB2"].ap()[:, 4096 * h2 + 2048 * par:4096 * h2 + 2048 * par + 2048]
                        tp_toks.append(S["qs"].start(lambda e, pbuf=pbuf, ptre=ptre, Wi=Wi: e.dma_start(
                            out=ptre[:, 0:Wi].rearrange("p (t c) -> p t c", c=128),
                            in_=pbuf[:, 0:Wi], transpose=True),
                            waits=[(S["dve"], S["dve"].val)]))
                    if prev is not None:
                        self._C_av(pair, prev[0], pr2, prev[1])
                    prev = (qi, tp_toks)
                self._C_av(pair, prev[0], pr2, prev[1])

        wu_tok = Q.start(lambda e: e.dma_start(
            out=Wg[:, 0:8192].rearrange("p (c o) -> p c o", o=D),
            in_=g["a_wu"].rearrange("(c p) o -> p c o", p=P)),
            waits=[(S["pe"], S["pe"].val), (S["act"], S["act"].val)])
        for n in range(NT):
            for m in range(DC):
                bank = pb[(n * DC + m) % 2]

                def wumm(e, n=n, m=m, bank=bank):
                    last = None
                    for k in range(DC):
                        last = e.matmul(bank[:, :], Wg[:, 1024 * k + 128 * m:1024 * k + 128 * m + 128],
                                        BB[:, k, 512 * n:512 * n + 512], start=(k == 0), stop=(k == DC - 1))
                    return last
                kb.op("pe", wumm, waits=[wu_tok, (S["dve"], S["dve"].val)], inc=S["pe"])
                kb.op("dve", lambda e, n=n, m=m, bank=bank: e.tensor_tensor(
                    ST[:, m, 512 * n:512 * n + 512], ST[:, m, 512 * n:512 * n + 512], bank[:, :], ALU.add),
                    waits=[(S["pe"], S["pe"].val)], inc=S["dve"])
        self.norm_stream("ln", out_bf=lambda c: BA[:, c, :])

    def _C_av(self, pair, qi, pr2, tp_toks):
        kb, sb, S = self.kb, self.sb, self.S
        pb, Wg, CS = sb["pb"], self.Wg, sb["CSTAT"]
        CSb = CS.ap().bitcast(BF16)
        qkv = Wg[:, 0:12288].rearrange("p (w pr t c) -> p w pr t c", w=3, pr=2, t=16)
        Wi = 128 * (qi + 1)
        nch = Wi // 128
        par = qi % 2
        for h2 in range(2):
            ptre = sb["SB2"].ap()[:, 4096 * h2 + 2048 * par:4096 * h2 + 2048 * par + 2048]
            obank = pb[6 + h2]

            def avmm(e, h2=h2, nch=nch, ptre=ptre, obank=obank):
                last = None
                for c in range(nch):
                    last = e.matmul(obank[0:64, 0:128],
                                    qkv[:, 2, pr2, c, 64 * h2:64 * h2 + 64],
                                    ptre[:, 128 * c:128 * c + 128],
                                    start=(c == 0), stop=(c == nch - 1))
                return last
            kb.op("pe", avmm, waits=list(tp_toks) + [(S["dve"], S["dve"].val)], inc=S["pe"])
            av_done = S["pe"].val
            kb.op("pe", lambda e, h2=h2, par=par: e.matmul(
                pb[2][0:1, 0:128], CSb[:, 64 + 4 * h2 + 2 * par:64 + 4 * h2 + 2 * par + 1],
                sb["ident_b"][:, :], start=True, stop=True),
                waits=[(S["dve"], S["dve"].val)], inc=S["pe"])
            kb.op("dve", lambda e: e.tensor_copy(CSb[0:1, 96:224], pb[2][0:1, 0:128]),
                  waits=[(S["pe"], S["pe"].val)], inc=S["dve"])
            kb.op("pe", lambda e: e.matmul(pb[2][0:64, 0:128], sb["ones_b"][0:1, 0:64],
                                           CSb[0:1, 96:224], start=True, stop=True),
                  waits=[(S["dve"], S["dve"].val)], inc=S["pe"])
            kb.op("dve", lambda e: e.tensor_copy(CSb[0:64, 256:384], pb[2][0:64, 0:128]),
                  waits=[(S["pe"], S["pe"].val)], inc=S["dve"])
            kb.op("dve", lambda e, h2=h2, qi=qi, obank=obank: e.tensor_tensor(
                sb["BIGB"][:, pair, 128 * qi:128 * qi + 128][64 * h2:64 * h2 + 64, :],
                obank[0:64, 0:128], CSb[0:64, 256:384], ALU.mult),
                waits=[(S["pe"], av_done)], inc=S["dve"])
            self._C_cbg[2] = [(S["dve"], S["dve"].val), (S["act"], S["act"].val)]

    def phase_D(self, s):
        kb, g, sb, S = self.kb, self.g, self.sb, self.S
        self.nmark = 1000
        ST, BA, BB, SB2, SM = sb["STREAM"], sb["BIGA"], sb["BIGB"], sb["SB2"], sb["SMALL"]
        pb = sb["pb"]
        Wg = self.Wg
        Wf = sb["WBUF"].ap().bitcast(F32)
        Q = S["qp"]
        SMb = SM.ap().bitcast(BF16)
        SB2b = SB2.ap().bitcast(BF16) if False else SB2.ap()
        CS = sb["CSTAT"]
        STf = ST.ap().rearrange("p c t -> p (c t)")
        STb = STf.bitcast(BF16)
        x5T = BB.ap().rearrange("p c t -> p (c t)").rearrange("p (t f) -> p t f", f=D)
        hbuf = STf[:, 0:8192].rearrange("p (a b) -> p a b", b=4096)
        qh = STb[:, 16384:24576].rearrange("p (a b) -> p a b", b=4096)
        qhT = STb[:, 24576:32768]

        if s == 0:
            wtok = None
            for wi, (wname, nrow, ncols) in enumerate([("b_wgate", 1024, 4096),
                                                       ("b_wup", 1024, 4096),
                                                       ("b_wdown", 4096, 1024)]):
                nslab = nrow // P
                for c in range(nslab):
                    buf = 4096 * (c % 2)
                    tok = S["qs"].start(lambda e, c=c, buf=buf, wname=wname, ncols=ncols: e.dma_start(
                        out=Wf[:, buf:buf + ncols], in_=g[wname][128 * c:128 * c + 128, :]),
                        waits=[(S["dve"], S["dve"].val)])
                    kb.op("dve", lambda e, c=c, buf=buf, ncols=ncols: e.tensor_reduce(
                        CS[:, 448 + c % 2:449 + c % 2], Wf[:, buf:buf + ncols], AX.X, ALU.add,
                        apply_absolute_value=True),
                        waits=[tok], inc=S["dve"])
                    if c == 0:
                        kb.op("dve", lambda e: e.tensor_copy(CS[:, 450:451], CS[:, 448:449]),
                              inc=S["dve"])
                    else:
                        kb.op("dve", lambda e, c=c: e.tensor_tensor(
                            CS[:, 450:451], CS[:, 450:451], CS[:, 448 + c % 2:449 + c % 2], ALU.add),
                            inc=S["dve"])
                self.mark()
                kb.op("pe", lambda e: e.matmul(pb[2][0:1, 0:1], sb["ones_f"][:, 0:1], CS[:, 450:451],
                                               start=True, stop=True),
                      waits=[(S["dve"], S["dve"].val)], inc=S["pe"])
                kb.op("dve", lambda e, nrow=nrow, ncols=ncols: e.tensor_scalar(
                    CS[0:1, 451:452], pb[2][0:1, 0:1], 1.0 / (nrow * ncols), 1e-5, ALU.mult, ALU.max),
                    waits=[(S["pe"], S["pe"].val)], inc=S["dve"])
                kb.op("dve", lambda e: e.reciprocal(CS[0:1, 451:452], CS[0:1, 451:452]), inc=S["dve"])
                kb.op("pe", lambda e: e.matmul(pb[2][:, 0:1], sb["ones_f"][0:1, :], CS[0:1, 451:452],
                                               start=True, stop=True),
                      waits=[(S["dve"], S["dve"].val)], inc=S["pe"])
                kb.op("dve", lambda e: e.tensor_copy(CS[:, 452:453], pb[2][:, 0:1]),
                      waits=[(S["pe"], S["pe"].val)], inc=S["dve"])
                kb.op("dve", lambda e, wi=wi: e.reciprocal(CS[:, 192 + wi:193 + wi], CS[:, 452:453]),
                      inc=S["dve"])
                wsc_done = S["dve"].val
                self.mark()
                st_toks = {}
                for c in range(nslab):
                    buf = 4096 * (c % 2)
                    tok = S["qs"].start(lambda e, c=c, buf=buf, wname=wname, ncols=ncols: e.dma_start(
                        out=Wf[:, buf:buf + ncols], in_=g[wname][128 * c:128 * c + 128, :]),
                        waits=[(S["dve"], S["dve"].val)])
                    kb.op("dve", lambda e, buf=buf, ncols=ncols: e.tensor_scalar(
                        Wf[:, buf:buf + ncols], Wf[:, buf:buf + ncols], CS[:, 452:453], MAGIC,
                        ALU.mult, ALU.add),
                        waits=[tok, (S["dve"], wsc_done)], inc=S["dve"])
                    stg = 4096 * (c % 2)
                    w2 = [st_toks[c - 2]] if (c - 2) in st_toks else []
                    kb.op("dve", lambda e, buf=buf, stg=stg, ncols=ncols: e.tensor_scalar(
                        SMb[:, stg:stg + ncols], Wf[:, buf:buf + ncols], MAGIC, None, ALU.subtract),
                        waits=w2, inc=S["dve"])
                    kb.op("dve", lambda e, stg=stg, ncols=ncols: e.tensor_scalar(
                        SMb[:, stg:stg + ncols], SMb[:, stg:stg + ncols], 1.0, -1.0, ALU.min, ALU.max),
                        inc=S["dve"])
                    wtok = S["qs"].start(lambda e, wi=wi, c=c, stg=stg, ncols=ncols: e.dma_start(
                        out=g["scr_wq"][wi, 128 * c * ncols:(128 * c + 128) * ncols]
                        .rearrange("(a b) -> a b", a=128), in_=SMb[:, stg:stg + ncols]),
                        waits=[(S["dve"], S["dve"].val)])
                    st_toks[c] = wtok
            kb.op("pe", lambda e: None, waits=[st_toks[nslab - 1]] +
                  ([st_toks[nslab - 2]] if nslab >= 2 else []), inc=S["pe"])
            self.mark()
        wq_ready = S["pe"].val

        bg = {0: [(S["act"], S["act"].val), (S["dve"], S["dve"].val)],
              1: [(S["act"], S["act"].val), (S["dve"], S["dve"].val)]}
        for t in range(16):
            for c in range(DC):
                bank = pb[c % 2]
                kb.op("pe", lambda e, t=t, c=c, bank=bank: e.matmul(
                    bank[:, 0:128], ST[:, c, 128 * t:128 * t + 128], sb["ident_f"][:, :],
                    start=True, stop=True),
                    waits=bg[c % 2], inc=S["pe"])
                mmd = S["pe"].val
                kb.op("act", lambda e, t=t, c=c, bank=bank: e.copy(
                    x5T[:, t, 128 * c:128 * c + 128], bank[:, 0:128]),
                    waits=[(S["pe"], mmd)], inc=S["act"])
                kb.op("dve", lambda e, t=t, c=c, bank=bank: e.reduce_max(
                    CS[:, 448 + c:449 + c], bank[:, 0:128], AX.X, apply_absolute_value=True),
                    waits=[(S["act"], S["act"].val)], inc=S["dve"])
                bg[c % 2] = [(S["act"], S["act"].val), (S["dve"], S["dve"].val)]
            def amx(e, t=t):
                last = e.tensor_copy(CS[:, t:t + 1], CS[:, 448:449])
                return last
            kb.op("dve", amx, inc=S["dve"])
            for c in range(1, DC):
                kb.op("dve", lambda e, t=t, c=c: e.tensor_tensor(
                    CS[:, t:t + 1], CS[:, t:t + 1], CS[:, 448 + c:449 + c], ALU.max), inc=S["dve"])
            kb.op("dve", lambda e, t=t: e.tensor_scalar(CS[:, t:t + 1], CS[:, t:t + 1],
                                                        1e-5, None, ALU.max), inc=S["dve"])
            kb.op("dve", lambda e, t=t: e.tensor_scalar(CS[:, 16 + t:17 + t], CS[:, t:t + 1],
                                                        1.0 / 127, None, ALU.mult), inc=S["dve"])
            kb.op("dve", lambda e, t=t: e.reciprocal(CS[:, t:t + 1], CS[:, t:t + 1]), inc=S["dve"])
            kb.op("dve", lambda e, t=t: e.tensor_scalar(CS[:, t:t + 1], CS[:, t:t + 1],
                                                        127.0, None, ALU.mult), inc=S["dve"])
        for t in range(16):
            kb.op("dve", lambda e, t=t: e.tensor_tensor(CS[:, 64 + t:65 + t], CS[:, 16 + t:17 + t],
                                                        CS[:, 192:193], ALU.mult), inc=S["dve"])
            kb.op("dve", lambda e, t=t: e.tensor_tensor(CS[:, 80 + t:81 + t], CS[:, 16 + t:17 + t],
                                                        CS[:, 193:194], ALU.mult), inc=S["dve"])
        amax_done = S["dve"].val
        self.mark()
        bg2 = {0: [(S["dve"], S["dve"].val)], 1: [(S["dve"], S["dve"].val)]}
        for t in range(16):
            half_buf = SB2b if t < 8 else SMb
            tl = t % 8
            for c in range(DC):
                bank = pb[2 + c % 2]
                kb.op("pe", lambda e, t=t, c=c, bank=bank: e.matmul(
                    bank[:, 0:128], ST[:, c, 128 * t:128 * t + 128], sb["ident_f"][:, :],
                    start=True, stop=True),
                    waits=bg2[c % 2], inc=S["pe"])
                mmd = S["pe"].val
                kb.op("dve", lambda e, t=t, c=c, bank=bank: e.tensor_scalar(
                    CS[:, 256:384], bank[:, 0:128],
                    CS[:, t:t + 1], MAGIC, ALU.mult, ALU.add),
                    waits=[(S["pe"], mmd), (S["dve"], amax_done)], inc=S["dve"])
                kb.op("dve", lambda e, t=t, c=c, half_buf=half_buf, tl=tl: e.tensor_scalar(
                    half_buf[:, 1024 * c + 128 * tl:1024 * c + 128 * tl + 128],
                    CS[:, 256:384],
                    MAGIC, None, ALU.subtract),
                    waits=[(S["pe"], wq_ready)], inc=S["dve"])
                bg2[c % 2] = [(S["dve"], S["dve"].val)]
        qx_done = S["dve"].val
        self.mark()
        if self.dev_stage == "dqx" and not self.stage_hit:
            self.stage_hit = True
            for half in range(4):
                S["qp"].start(lambda e, half=half: e.dma_start(
                    out=self.g["dbg"][half, :, :],
                    in_=SB2b[:, 2048 * half:2048 * half + 2048]),
                    waits=[(S["dve"], qx_done)])
            raise StopBuild()
        qxT = BA.ap().rearrange("p c t -> p (c t)")
        tks = []
        for half in range(2):
            srcb = SB2b[:, 0:8192] if half == 0 else SMb[:, 0:8192]
            out_ap = BA.ap().rearrange("p c t -> p (c t)")[:, 8192 * half:8192 * half + 8192] \
                .rearrange("p (t c) -> p t c", c=128)
            tks.append(S["qs"].start(lambda e, srcb=srcb, out_ap=out_ap: e.dma_start(
                out=out_ap, in_=srcb, transpose=True),
                waits=[(S["dve"], qx_done), (S["pe"], S["pe"].val), (S["act"], S["act"].val)]))

        if self.dev_stage == "dqxT" and not self.stage_hit:
            self.stage_hit = True
            for c in range(DC):
                S["qp"].start(lambda e, c=c: e.dma_start(
                    out=self.g["dbg"][c, :, :], in_=BA[:, c, :]),
                    waits=tks + [(S["dve"], S["dve"].val)])
            raise StopBuild()
        self.mark()
        self._y_toks = {}
        for tt in range(8):
            tsl = slice(256 * tt, 256 * tt + 256)
            subs = [2 * tt, 2 * tt + 1]
            for oc in range(8):
                gslab = 0 + 4096 * (oc % 2)
                uslab = 8192 + 4096 * (oc % 2)
                gt = Q.start(lambda e, oc=oc, gslab=gslab: e.dma_start(
                    out=Wg[:, gslab:gslab + 4096].rearrange("p (c o) -> p c o", o=512),
                    in_=g["scr_wq"][0].rearrange("(c p o) -> p c o", p=P, o=4096)[:, :, 512 * oc:512 * oc + 512]),
                    waits=[(S["pe"], wq_ready), (S["pe"], S["pe"].val)])
                ut = Q.start(lambda e, oc=oc, uslab=uslab: e.dma_start(
                    out=Wg[:, uslab:uslab + 4096].rearrange("p (c o) -> p c o", o=512),
                    in_=g["scr_wq"][1].rearrange("(c p o) -> p c o", p=P, o=4096)[:, :, 512 * oc:512 * oc + 512]),
                    waits=[])
                for si, sub in enumerate(subs):
                    gb, ub = pb[si], pb[2 + si]

                    def gmm(e, sub=sub, gslab=gslab, gb=gb):
                        last = None
                        base = 8192 * (sub // 8) + 128 * (sub % 8)
                        for c in range(DC):
                            last = e.matmul(gb[:, :], qxT[:, base + 1024 * c:base + 1024 * c + 128],
                                            Wg[:, gslab + 512 * c:gslab + 512 * c + 512],
                                            start=(c == 0), stop=(c == DC - 1))
                        return last
                    kb.op("pe", gmm, waits=[gt] + tks + [(S["act"], S["act"].val)], inc=S["pe"])
                    kb.op("act", lambda e, sub=sub, gb=gb, si=si: e.activation(
                        SMb[:, 5120 + 512 * si:5120 + 512 * si + 512],
                        gb[:, :], AF.Silu, scale=CS[:, 64 + sub:65 + sub]),
                        waits=[(S["pe"], S["pe"].val), (S["dve"], S["dve"].val)], inc=S["act"])
                    silu_done = S["act"].val

                    def umm(e, sub=sub, uslab=uslab, ub=ub):
                        last = None
                        base = 8192 * (sub // 8) + 128 * (sub % 8)
                        for c in range(DC):
                            last = e.matmul(ub[:, :], qxT[:, base + 1024 * c:base + 1024 * c + 128],
                                            Wg[:, uslab + 512 * c:uslab + 512 * c + 512],
                                            start=(c == 0), stop=(c == DC - 1))
                        return last
                    kb.op("pe", umm, waits=[ut, (S["dve"], S["dve"].val)], inc=S["pe"])
                    kb.op("dve", lambda e, si=si, oc=oc, ub=ub: e.tensor_tensor(
                        hbuf[:, si, 512 * oc:512 * oc + 512],
                        SMb[:, 5120 + 512 * si:5120 + 512 * si + 512],
                        ub[:, :], ALU.mult),
                        waits=[(S["pe"], S["pe"].val), (S["act"], silu_done)], inc=S["dve"])
            h_done = S["dve"].val
            self.mark()
            for si, sub in enumerate(subs):
                kb.op("dve", lambda e, si=si: e.reduce_max(CS[:, 32 + si:33 + si],
                                                           hbuf[:, si, :], AX.X,
                                                           apply_absolute_value=True),
                      waits=[(S["dve"], h_done)], inc=S["dve"])
                kb.op("dve", lambda e, si=si: e.tensor_scalar(CS[:, 32 + si:33 + si],
                                                              CS[:, 32 + si:33 + si], 1e-5, None,
                                                              ALU.max), inc=S["dve"])
                kb.op("dve", lambda e, si=si: e.tensor_scalar(CS[:, 34 + si:35 + si],
                                                              CS[:, 32 + si:33 + si], 1.0 / 127,
                                                              None, ALU.mult), inc=S["dve"])
                kb.op("dve", lambda e, si=si: e.reciprocal(CS[:, 32 + si:33 + si],
                                                           CS[:, 32 + si:33 + si]), inc=S["dve"])
                kb.op("dve", lambda e, si=si: e.tensor_scalar(CS[:, 32 + si:33 + si],
                                                              CS[:, 32 + si:33 + si], 127.0, None,
                                                              ALU.mult), inc=S["dve"])
                kb.op("dve", lambda e, si=si, sub=sub: e.tensor_tensor(
                    CS[:, 96 + sub:97 + sub], CS[:, 34 + si:35 + si], CS[:, 80 + sub:81 + sub],
                    ALU.mult), inc=S["dve"])
                kb.op("dve", lambda e, si=si, sub=sub: e.tensor_tensor(
                    CS[:, 96 + sub:97 + sub], CS[:, 96 + sub:97 + sub], CS[:, 194:195],
                    ALU.mult), inc=S["dve"])
                kb.op("dve", lambda e, si=si: e.tensor_scalar(
                    hbuf[:, si, :], hbuf[:, si, :], CS[:, 32 + si:33 + si], MAGIC, ALU.mult, ALU.add),
                    inc=S["dve"])
                kb.op("dve", lambda e, si=si: e.tensor_scalar(
                    qh[:, si, :], hbuf[:, si, :], MAGIC, None, ALU.subtract), inc=S["dve"])
            self.mark()
            htk = []
            for si, sub in enumerate(subs):
                htk.append(S["qs"].start(lambda e, si=si: e.dma_start(
                    out=qhT[:, 4096 * si:4096 * si + 4096].rearrange("p (t c) -> p t c", c=128),
                    in_=qh[:, si, :], transpose=True),
                    waits=[(S["dve"], S["dve"].val)]))
            for m2 in range(2):
                dtoks = []
                for half in range(2):
                    dtoks.append(Q.start(lambda e, m2=m2, half=half: e.dma_start(
                        out=Wg[:, 8192 * half:8192 * half + 8192].rearrange("p (c o) -> p c o", o=512),
                        in_=g["scr_wq"][2].rearrange("(c p o) -> p c o", p=P, o=D)
                        [:, 16 * half:16 * half + 16, 512 * m2:512 * m2 + 512]),
                        waits=[(S["pe"], S["pe"].val)]))
                for si, sub in enumerate(subs):
                    bank = pb[4 + si]

                    def dmm(e, si=si, bank=bank):
                        last = None
                        for c in range(32):
                            base = 8192 * (c // 16)
                            last = e.matmul(bank[:, :], qhT[:, 4096 * si + 128 * c:4096 * si + 128 * c + 128],
                                            Wg[:, base + 512 * (c % 16):base + 512 * (c % 16) + 512],
                                            start=(c == 0), stop=(c == 31))
                        return last
                    kb.op("pe", dmm, waits=dtoks + htk + [(S["dve"], S["dve"].val)], inc=S["pe"])
                    kb.op("dve", lambda e, si=si, sub=sub, m2=m2, bank=bank: e.tensor_scalar(
                        SM[:, 1024 * si + 512 * m2:1024 * si + 512 * m2 + 512],
                        bank[:, :], CS[:, 96 + sub:97 + sub], None, ALU.mult),
                        waits=[(S["pe"], S["pe"].val)], inc=S["dve"])
                    kb.op("dve", lambda e, si=si, sub=sub, m2=m2: e.tensor_tensor(
                        SM[:, 1024 * si + 512 * m2:1024 * si + 512 * m2 + 512],
                        SM[:, 1024 * si + 512 * m2:1024 * si + 512 * m2 + 512],
                        x5T[:, sub, 512 * m2:512 * m2 + 512], ALU.add), inc=S["dve"])
            for si, sub in enumerate(subs):
                row = SM[:, 1024 * si:1024 * si + 1024]
                kb.op("act", lambda e, row=row, si=si: e.activation(
                    SMb[:, 4096 + 2048 * si:4096 + 2048 * si + 1024], row, AF.Square,
                    accum_out=CS[:, 40 + si:41 + si]),
                    waits=[(S["dve"], S["dve"].val)], inc=S["act"])
                kb.op("dve", lambda e, row=row, si=si: e.reduce_sum(CS[:, 42 + si:43 + si], row, AX.X),
                      waits=[(S["act"], S["act"].val)], inc=S["dve"])
                kb.op("dve", lambda e, si=si: e.tensor_scalar(CS[:, 42 + si:43 + si],
                                                              CS[:, 42 + si:43 + si], 1.0 / D, None,
                                                              ALU.mult), inc=S["dve"])
                kb.op("dve", lambda e, si=si: e.tensor_tensor(CS[:, 44 + si:45 + si],
                                                              CS[:, 42 + si:43 + si],
                                                              CS[:, 42 + si:43 + si], ALU.mult),
                      inc=S["dve"])
                kb.op("dve", lambda e, si=si: e.scalar_tensor_tensor(
                    CS[:, 40 + si:41 + si], CS[:, 40 + si:41 + si], 1.0 / D, CS[:, 44 + si:45 + si],
                    ALU.mult, ALU.subtract), inc=S["dve"])
                kb.op("dve", lambda e, si=si: e.tensor_scalar(CS[:, 40 + si:41 + si],
                                                              CS[:, 40 + si:41 + si], EPS, None,
                                                              ALU.add), inc=S["dve"])
                kb.op("act", lambda e, si=si: e.activation(CS[:, 40 + si:41 + si],
                                                           CS[:, 40 + si:41 + si], AF.Ln),
                      waits=[(S["dve"], S["dve"].val)], inc=S["act"])
                kb.op("act", lambda e, si=si: e.activation(CS[:, 40 + si:41 + si],
                                                           CS[:, 40 + si:41 + si], AF.Exp, scale=-0.5),
                      inc=S["act"])
                kb.op("dve", lambda e, si=si: e.tensor_tensor(CS[:, 46 + si:47 + si],
                                                              CS[:, 42 + si:43 + si],
                                                              CS[:, 40 + si:41 + si], ALU.mult),
                      waits=[(S["act"], S["act"].val)], inc=S["dve"])
                kb.op("dve", lambda e, si=si: e.tensor_scalar(CS[:, 46 + si:47 + si],
                                                              CS[:, 46 + si:47 + si], -1.0, None,
                                                              ALU.mult), inc=S["dve"])
                slot = si + 2 * (tt % 2)
                outb = SB2.ap().bitcast(F32)[:, 1024 * slot:1024 * slot + 1024]
                wprev = [self._y_toks[slot]] if slot in self._y_toks else []
                kb.op("act", lambda e, row=row, outb=outb, si=si: e.activation(
                    outb, row, AF.Identity, scale=CS[:, 40 + si:41 + si],
                    bias=CS[:, 46 + si:47 + si]),
                    waits=[(S["dve"], S["dve"].val)] + wprev, inc=S["act"])
                self._y_toks[slot] = S["qs"].start(lambda e, sub=sub, outb=outb: e.dma_start(
                    out=g["y"][s, 128 * sub:128 * sub + 128, :], in_=outb),
                    waits=[(S["act"], S["act"].val)])
            self.mark()

def prep_inputs(inputs, core):
    i = inputs
    s0 = 2 * core
    x = np.ascontiguousarray(i["L0_attn_hidden_states"][s0:s0 + 2])
    x_f = np.ascontiguousarray(x.reshape(NSEQ, SEQ, DC, P).transpose(0, 2, 3, 1))
    d = {"x_f": x_f}
    d["l0_wq"] = i["L0_attn_Wq"]; d["l0_wk"] = i["L0_attn_Wk"]; d["l0_wv"] = i["L0_attn_Wv"]
    d["l0_wo"] = i["L0_attn_Wo"]
    d["l0_attn_norm_w"] = np.ascontiguousarray(i["L0_attn_attn_norm_w"].reshape(DC, P).T)
    d["l0_mlp_norm_w"] = np.ascontiguousarray(i["L0_attn_mlp_norm_w"].reshape(DC, P).T)
    d["l0_out_norm_w"] = np.ascontiguousarray(i["L0_attn_attn_out_norm_w"].reshape(P, 1))
    d["l0_wgate"] = i["L0_attn_Wgate"]; d["l0_wup"] = i["L0_attn_Wup"]; d["l0_wdown"] = i["L0_attn_Wdown"]
    d["f_router_w1"] = i["L0_ffn_router_w1"]
    d["f_router_b1"] = np.ascontiguousarray(i["L0_ffn_router_b1"].reshape(2, P).T)
    d["f_router_w2"] = i["L0_ffn_router_w2"]
    d["f_router_b2"] = np.ascontiguousarray(i["L0_ffn_router_b2"].reshape(1, 8))
    d["f_halt_w1"] = i["L0_ffn_halt_w1"]
    d["f_halt_b1"] = np.ascontiguousarray(i["L0_ffn_halt_b1"].reshape(2, P).T)
    d["f_halt_w2"] = np.ascontiguousarray(i["L0_ffn_halt_w2"].reshape(2, P).T)
    d["f_halt_b2"] = np.ascontiguousarray(i["L0_ffn_halt_b2"].reshape(1, 1))
    d["f_w1all"] = np.ascontiguousarray(i["L0_ffn_expert_w1"].transpose(1, 0, 2).reshape(2048, 512))
    d["f_b1all"] = np.ascontiguousarray(i["L0_ffn_expert_b1"].reshape(512).reshape(4, P).T)
    d["f_w2cat"] = np.ascontiguousarray(
        np.concatenate([i["L0_ffn_expert_w2"].reshape(512, 3072), i["L0_ffn_expert_b2"]], axis=0))
    d["f_shared_w1"] = i["L0_ffn_shared_w1"]
    b1 = np.zeros((P, 1), np.float32); b1[0:64, 0] = i["L0_ffn_shared_b1"]
    d["f_shared_b1"] = b1
    d["f_shared_w2"] = i["L0_ffn_shared_w2"]
    d["f_shared_b2"] = np.ascontiguousarray(i["L0_ffn_shared_b2"].reshape(24, P).T)
    d["f_shared_weight"] = np.ascontiguousarray(np.float32(i["L0_ffn_shared_weight"]).reshape(1, 1))
    d["f_refine_w1"] = i["L0_ffn_refine_w1"]
    d["f_refine_b1"] = np.ascontiguousarray(i["L0_ffn_refine_b1"].reshape(16, P).T)
    d["f_refine_w2"] = i["L0_ffn_refine_w2"]
    d["f_refine_b2"] = np.ascontiguousarray(i["L0_ffn_refine_b2"].reshape(DC, P).T)
    d["f_mu"] = np.ascontiguousarray(i["L0_ffn_mu"].reshape(DC, P).T)
    d["f_iw"] = np.ascontiguousarray(i["L0_ffn_integration_weight"].reshape(DC, P).T)
    for nm, key in [("a_wq", "L1_attn_Wq"), ("a_wk", "L1_attn_Wk"),
                    ("a_wv", "L1_attn_Wv"), ("a_wu", "L1_attn_Wu")]:
        d[nm] = np.ascontiguousarray(i[key].T)
    for nm, key in [("a_qln_w", "L1_attn_qln_w"), ("a_qln_b", "L1_attn_qln_b"),
                    ("a_kln_w", "L1_attn_kln_w"), ("a_kln_b", "L1_attn_kln_b")]:
        d[nm] = np.ascontiguousarray(np.tile(i[key], 16).reshape(1, D))
    d["b_wgate"] = np.ascontiguousarray(i["L1_ffn_w_gate"].T)
    d["b_wup"] = np.ascontiguousarray(i["L1_ffn_w_up"].T)
    d["b_wdown"] = np.ascontiguousarray(i["L1_ffn_w_down"].T)
    d["c_ident"] = np.eye(P, dtype=np.float32)
    d["c_triu"] = np.triu(np.ones((P, P), np.float32))
    d["c_trilq"] = np.tril(np.ones((P, P), np.float32))
    d["c_trilneg"] = np.where(d["c_trilq"] > 0, np.float32(0), np.float32(-1e38))
    e8 = np.zeros((8, 512), np.float32)
    for e in range(8):
        e8[e, 64 * e:64 * e + 64] = 1.0
    d["c_e8"] = e8
    return {k: np.ascontiguousarray(v, dtype=np.float32) for k, v in d.items()}


def kernel(**inputs):
    inputs = {k: np.asarray(v) for k, v in inputs.items()}
    nc, g, kb = build()
    in_maps = [prep_inputs(inputs, core) for core in range(8)]
    res = run_bass_kernel_spmd(nc, in_maps, core_ids=list(range(8)))
    out = np.empty((16, SEQ, D), np.float32)
    for core in range(8):
        out[2 * core:2 * core + 2] = res.results[core]["y"]
    return out


# version 06 (final)
# speedup vs baseline: 1.1622x; 1.0118x over previous
import numpy as np
import ml_dtypes
import concourse.bass as bass
import concourse.mybir as mybir
from concourse.bass_utils import run_bass_kernel_spmd

F32 = mybir.dt.float32
BF16 = mybir.dt.bfloat16
U8 = mybir.dt.uint8
AF = mybir.ActivationFunctionType
ALU = mybir.AluOpType
AX = mybir.AxisListType
BF = ml_dtypes.bfloat16

P = 128
SEQ = 2048
NSEQ = 2
D = 1024
DC = 8
NT = 4
MAGIC = 12582912.0
EPS = 1e-5


class Sem:
    __slots__ = ("h", "name", "val")

    def __init__(self, h, name):
        self.h = h; self.name = name; self.val = 0

    def inc(self, k=1):
        self.val += k
        return self.val


class KB:
    def __init__(self, nc):
        self.nc = nc
        self.q = {"pe": [], "act": [], "dve": [], "pool": [], "sp": []}
        self.nops = 0
        self.selfsem = None
        self.cur = "init"

    def mksem(self, name):
        return Sem(self.nc.semaphore(name).__enter__(), name)

    def op(self, eng, build, waits=(), inc=None, inck=1):
        waits = list(waits)
        if eng in ("dve", "act") and self.selfsem is not None:
            own = self.selfsem[eng]
            waits.append((own, own.val))
        waits = [(s, v) for (s, v) in waits if v > 0]
        for s, v in waits:
            assert v <= s.val, f"wait on future {s.name}: {v} > {s.val}"
        if eng in ("dve", "act") and self.selfsem is not None:
            if inc is None:
                inc = self.selfsem[eng]
                inck = 1
            else:
                assert inc is self.selfsem[eng], f"{eng} op must inc its own sem"
        if inc is not None:
            inc.inc(inck)
        self.nops += 1

        def fn(e, waits=tuple(waits), inc=inc, inck=inck, build=build):
            for s, v in waits:
                e.wait_ge(s.h, v)
            ins = build(e)
            if inc is not None:
                if ins is None:
                    ins = e.nop()
                ins.then_inc(inc.h, inck)
        self.q[eng].append((fn, self.cur))
        return inc.val if inc is not None else None


class StopBuild(Exception):
    pass


class DmaQ:

    def __init__(self, kb, eng, name, R=4):
        self.kb = kb
        self.eng = eng
        self.sems = [kb.mksem(f"{name}{i}") for i in range(R)]
        self.i = 0

    def start(self, build, waits=()):
        s = self.sems[self.i % len(self.sems)]
        self.i += 1
        w = list(waits) + [(s, s.val)]
        self.kb.op(self.eng, build, waits=w, inc=s, inck=16)
        return (s, s.val)

    def drain_waits(self):
        return [(s, s.val) for s in self.sems]


def build(dev_phases="ABCD", dev_dump=None, dev_stage=None, dev_stop=None):
    nc = bass.Bass("TRN2", target_bir_lowering=False, debug=False)
    g = {}

    def din(name, shape, dt=F32):
        g[name] = nc.dram_tensor(name, shape, dt, kind="ExternalInput").ap()

    din("x_f", [NSEQ, DC, P, SEQ])
    din("l0_wq", [D, D]); din("l0_wk", [D, D]); din("l0_wv", [D, D]); din("l0_wo", [D, D])
    din("l0_attn_norm_w", [P, DC]); din("l0_mlp_norm_w", [P, DC]); din("l0_out_norm_w", [P, 1])
    din("l0_wgate", [D, 4096]); din("l0_wup", [D, 4096]); din("l0_wdown", [4096, D])
    din("f_router_w1", [D, 256]); din("f_router_b1", [P, 2])
    din("f_router_w2", [256, 8]); din("f_router_b2", [1, 8])
    din("f_halt_w1", [D, 256]); din("f_halt_b1", [P, 2])
    din("f_halt_w2", [P, 2]); din("f_halt_b2", [1, 1])
    din("f_w1all", [2048, 512]); din("f_b1all", [P, 4])
    din("f_w2cat", [520, 3072])
    din("f_shared_w1", [2048, 64]); din("f_shared_b1", [P, 1])
    din("f_shared_w2", [64, 3072]); din("f_shared_b2", [P, 24])
    din("f_shared_weight", [1, 1])
    din("f_refine_w1", [D, 2048]); din("f_refine_b1", [P, 16])
    din("f_refine_w2", [2048, D]); din("f_refine_b2", [P, DC])
    din("f_mu", [P, DC]); din("f_iw", [P, DC])
    din("a_wq", [D, D]); din("a_wk", [D, D]); din("a_wv", [D, D]); din("a_wu", [D, D])
    din("a_qln_w", [1, D]); din("a_qln_b", [1, D]); din("a_kln_w", [1, D]); din("a_kln_b", [1, D])
    din("b_wgate", [D, 4096]); din("b_wup", [D, 4096]); din("b_wdown", [4096, D])
    din("c_ident", [P, P])
    din("c_triu", [P, P])
    din("c_trilq", [P, P])
    din("c_trilneg", [P, P])
    din("c_e8", [8, 512])
    g["y"] = nc.dram_tensor("y", [NSEQ, SEQ, D], F32, kind="ExternalOutput").ap()
    g["dbg"] = nc.dram_tensor("dbg", [DC, P, SEQ], F32, kind="ExternalOutput").ap()
    g["scr_wq"] = nc.dram_tensor("scr_wq", [3, 4096 * D], BF16, kind="Internal").ap()
    g["scr_w8"] = nc.dram_tensor("scr_w8", [8, 128], BF16, kind="Internal").ap()

    kb = KB(nc)
    ctxs = []

    def sbuf(name, shape, dt):
        t = nc.sbuf_tensor(name, shape, dt)
        ctxs.append(t)
        return t.__enter__()

    def psum(name, shape, dt=F32):
        t = nc.psum_tensor(name, shape, dt)
        ctxs.append(t)
        return t.__enter__()

    sb = {}
    sb["STREAM"] = sbuf("STREAM", [P, DC, SEQ], F32)
    sb["BIGA"] = sbuf("BIGA", [P, DC, SEQ], BF16)
    sb["BIGB"] = sbuf("BIGB", [P, DC, SEQ], BF16)
    sb["WBUF"] = sbuf("WBUF", [P, 18688], BF16)
    sb["SB2"] = sbuf("SB2", [P, 8192], BF16)
    sb["SMALL"] = sbuf("SMALL", [P, 4096], F32)
    sb["ident_b"] = sbuf("ident_b", [P, P], BF16)
    sb["triu_b"] = sbuf("triu_b", [P, P], BF16)
    sb["trilq_b"] = sbuf("trilq_b", [P, P], BF16)
    sb["trilneg_b"] = sbuf("trilneg_b", [P, P], BF16)
    sb["ones_b"] = sbuf("ones_b", [P, P], BF16)
    sb["ones_f"] = sbuf("ones_f", [P, P], F32)
    sb["e8_b"] = sbuf("e8_b", [8, 512], BF16)
    sb["ident_f"] = sbuf("ident_f", [P, P], F32)
    sb["CSTAT"] = sbuf("CSTAT", [P, 512], F32)
    for nm, w in [("l0_attn_norm_w", DC), ("l0_mlp_norm_w", DC), ("l0_out_norm_w", 1),
                  ("f_mu", DC), ("f_iw", DC), ("f_router_b1", 2), ("f_halt_b1", 2),
                  ("f_b1all", 4), ("f_shared_b1", 1), ("f_shared_b2", 24),
                  ("f_refine_b1", 16), ("f_refine_b2", DC), ("f_router_b2", 8),
                  ("f_halt_b2", 1), ("f_shared_weight", 1)]:
        sb[nm] = sbuf("s_" + nm, [P, w], F32)
    pb = [psum(f"pb{i}", [P, 512], F32) for i in range(8)]
    sb["pb"] = pb

    S = {n: kb.mksem(n) for n in ["pe", "act", "dve", "pool", "sp"]}
    kb.selfsem = {"dve": S["dve"], "act": S["act"]}
    S["qp"] = DmaQ(kb, "pool", "dma_p")
    S["qs"] = DmaQ(kb, "sp", "dma_s")

    S["qp"].start(lambda e: e.dma_start(out=sb["ident_b"][:], in_=g["c_ident"][:, :]))
    S["qp"].start(lambda e: e.dma_start(out=sb["triu_b"][:], in_=g["c_triu"][:, :]))
    S["qp"].start(lambda e: e.dma_start(out=sb["trilq_b"][:], in_=g["c_trilq"][:, :]))
    S["qp"].start(lambda e: e.dma_start(out=sb["trilneg_b"][:], in_=g["c_trilneg"][:, :]))
    S["qp"].start(lambda e: e.dma_start(out=sb["e8_b"][:], in_=g["c_e8"][:, :]))
    S["qs"].start(lambda e: e.dma_start(out=sb["ident_f"][:], in_=g["c_ident"][:, :]))
    for nm in ["l0_attn_norm_w", "l0_mlp_norm_w", "l0_out_norm_w", "f_mu", "f_iw",
               "f_router_b1", "f_halt_b1", "f_b1all", "f_shared_b1", "f_shared_b2",
               "f_refine_b1", "f_refine_b2"]:
        S["qs"].start(lambda e, nm=nm: e.dma_start(out=sb[nm][:], in_=g[nm][:, :]))
    for nm in ["f_halt_b2", "f_shared_weight"]:
        S["qs"].start(lambda e, nm=nm: e.dma_start(out=sb[nm][:], in_=g[nm].broadcast_to([P, 1])))
    S["qs"].start(lambda e: e.dma_start(out=sb["f_router_b2"][:], in_=g["f_router_b2"].broadcast_to([P, 8])))
    kb.op("dve", lambda e: e.memset(sb["ones_b"][:], 1.0))
    kb.op("dve", lambda e: e.memset(sb["ones_f"][:], 1.0),
          waits=S["qp"].drain_waits() + S["qs"].drain_waits(), inc=S["dve"])
    sb["sw_col"] = sbuf("sw_col", [P, 4], F32)
    kb.op("act", lambda e: e.activation(sb["sw_col"][:, 0:1], sb["f_shared_weight"][:, 0:1], AF.Sigmoid),
          waits=[(S["dve"], S["dve"].val)], inc=S["act"])
    kb.op("dve", lambda e: e.tensor_scalar(sb["sw_col"][:, 1:2], sb["sw_col"][:, 0:1], -1.0, 1.0,
                                           ALU.mult, ALU.add),
          waits=[(S["act"], S["act"].val)], inc=S["dve"])
    kb.op("pe", lambda e: None, waits=[(S["dve"], S["dve"].val)], inc=S["pe"])
    kb.op("act", lambda e: None, waits=[(S["dve"], S["dve"].val)], inc=S["act"])
    kb.op("pool", lambda e: None, waits=[(S["dve"], S["dve"].val)], inc=S["pool"])
    kb.op("sp", lambda e: None, waits=[(S["dve"], S["dve"].val)], inc=S["sp"])

    E = Emitters(kb, g, sb, S)
    E.dev_stage = dev_stage
    E.dev_stop = dev_stop
    done = False
    try:
        for s in range(NSEQ):
            for ph in "ABCD":
                if ph in dev_phases:
                    kb.cur = f"{ph}{s}"
                    getattr(E, "phase_" + ph)(s)
                    if dev_dump == (ph, s):
                        E.dump_stream()
                        done = True
                        break
            if done:
                break
    except StopBuild:
        pass

    kb.op("sp", lambda e: None, waits=S["qs"].drain_waits() + S["qp"].drain_waits())

    def emit(t, key):
        cur, ctx = None, None
        for fn, lab in kb.q[key]:
            if lab != cur:
                if ctx is not None:
                    ctx.__exit__(None, None, None)
                ctx = nc.named_scope(lab)
                ctx.__enter__()
                cur = lab
            fn(t)
        if ctx is not None:
            ctx.__exit__(None, None, None)

    with nc.Block() as block:
        @block.tensor
        def _(t):
            emit(t, "pe")

        @block.scalar
        def _(t):
            emit(t, "act")

        @block.vector
        def _(t):
            emit(t, "dve")

        @block.gpsimd
        def _(t):
            emit(t, "pool")

        @block.sync
        def _(t):
            emit(t, "sp")

    for t in reversed(ctxs):
        t.__exit__(None, None, None)
    return nc, g, kb


class Emitters:
    def __init__(self, kb, g, sb, S):
        self.kb = kb; self.g = g; self.sb = sb; self.S = S
        self.Wg = sb["WBUF"].ap()
        self.dev_stage = None
        self.stage_hit = False

    def mark(self):
        self.nmark = getattr(self, "nmark", 0) + 1
        if getattr(self, "dev_stop", None) is not None and self.nmark >= self.dev_stop:
            raise StopBuild()

    def stage_dump(self, name, src_ap=None):
        if self.dev_stage != name or self.stage_hit:
            return
        self.stage_hit = True
        self._dump_body(src_ap)
        raise StopBuild()

    def _dump_body(self, src_ap):
        kb, g, sb, S = self.kb, self.g, self.sb, self.S
        kb, g, sb, S = self.kb, self.g, self.sb, self.S
        src_ap = src_ap if src_ap is not None else sb["STREAM"]
        for c in range(DC):
            S["qp"].start(lambda e, c=c: e.dma_start(out=g["dbg"][c, :, :], in_=src_ap[:, c, :]),
                          waits=[(S["dve"], S["dve"].val), (S["act"], S["act"].val), (S["pe"], S["pe"].val)])

    def dump_stream(self):
        kb, g, sb, S = self.kb, self.g, self.sb, self.S
        for c in range(DC):
            S["qs"].start(lambda e, c=c: e.dma_start(out=g["dbg"][c, :, :], in_=sb["STREAM"][:, c, :]),
                          waits=[(S["dve"], S["dve"].val), (S["act"], S["act"].val), (S["pe"], S["pe"].val)])

    def norm_stream(self, kind, w_col=None, residual=None, out_bf=None):
        kb, g, sb, S = self.kb, self.g, self.sb, self.S
        ST, pb, rows = sb["STREAM"], sb["pb"], sb["SMALL"]
        sq = lambda c: sb["SB2"][:, (c % 2) * 2048:(c % 2) * 2048 + 2048]
        tb = lambda c: sb["SB2"][:, 4096 + (c % 2) * 2048:4096 + (c % 2) * 2048 + 2048]

        if residual is not None:
            for c in range(DC):
                wts, ap, mode = residual(c)
                op = ALU.add
                if mode == "x2add":
                    kb.op("dve", lambda e, c=c, ap=ap: e.scalar_tensor_tensor(
                        ST[:, c, :], ST[:, c, :], 2.0, ap, ALU.mult, ALU.add), waits=wts, inc=S["dve"])
                else:
                    kb.op("dve", lambda e, c=c, ap=ap: e.tensor_tensor(
                        ST[:, c, :], ST[:, c, :], ap, ALU.add), waits=wts, inc=S["dve"])
        add_done = S["dve"].val

        pe_hist = []
        for c in range(DC):
            w = [(S["dve"], add_done)]
            if len(pe_hist) >= 2:
                w.append((S["pe"], pe_hist[-2]))
            kb.op("act", lambda e, c=c: e.activation(sq(c), ST[:, c, :], AF.Square), waits=w, inc=S["act"])
            if kind == "ln":
                kb.op("act", lambda e, c=c: e.copy(tb(c), ST[:, c, :]), inc=S["act"])
            a_now = S["act"].val

            def mm(e, c=c):
                last = None
                for n in range(NT):
                    last = e.matmul(pb[4 + n][0:1, :], sb["ones_b"][:, 0:1], sq(c)[:, 512 * n:512 * n + 512],
                                    start=(c == 0), stop=(c == DC - 1))
                    if kind == "ln":
                        last = e.matmul(pb[n][0:1, :], sb["ones_b"][:, 0:1], tb(c)[:, 512 * n:512 * n + 512],
                                        start=(c == 0), stop=(c == DC - 1))
                return last
            pe_hist.append(kb.op("pe", mm, waits=[(S["act"], a_now)], inc=S["pe"]))
        stats_done = pe_hist[-1]

        if kind == "rms":
            def rowmath(e):
                last = None
                for n in range(NT):
                    sl = slice(512 * n, 512 * n + 512)
                    last = e.tensor_scalar(rows[0:1, sl], pb[4 + n][0:1, :], 1.0 / D, EPS, ALU.mult, ALU.add)
                return last
            kb.op("dve", rowmath, waits=[(S["pe"], stats_done)], inc=S["dve"])
        else:
            def rm1(e):
                last = None
                for n in range(NT):
                    last = e.tensor_scalar(rows[0:1, 2048 + 512 * n:2048 + 512 * n + 512],
                                           pb[n][0:1, :], 1.0 / D, None, ALU.mult)
                return last
            kb.op("dve", rm1, waits=[(S["pe"], stats_done)], inc=S["dve"])
            def rm2(e):
                last = None
                for n in range(NT):
                    mr = rows[0:1, 2048 + 512 * n:2048 + 512 * n + 512]
                    last = e.tensor_tensor(rows[0:1, 512 * n:512 * n + 512], mr, mr, ALU.mult)
                return last
            kb.op("dve", rm2, inc=S["dve"])
            def rm3(e):
                last = None
                for n in range(NT):
                    rr = rows[0:1, 512 * n:512 * n + 512]
                    last = e.scalar_tensor_tensor(rr, pb[4 + n][0:1, :], 1.0 / D, rr, ALU.mult, ALU.subtract)
                return last
            kb.op("dve", rm3, inc=S["dve"])
            def rm4(e):
                last = None
                for n in range(NT):
                    rr = rows[0:1, 512 * n:512 * n + 512]
                    last = e.tensor_scalar(rr, rr, EPS, None, ALU.add)
                return last
            kb.op("dve", rm4, inc=S["dve"])

        if getattr(self, "dev_stage", None) == "hssq" and not self.stage_hit:
            self.stage_hit = True
            S["qs"].start(lambda e: e.dma_start(out=g["dbg"][0, 0:1, :], in_=rows[0:1, 0:2048]),
                          waits=[(S["dve"], S["dve"].val)])
            raise StopBuild()
        kb.op("act", lambda e: e.activation(rows[0:1, 0:2048], rows[0:1, 0:2048], AF.Ln),
              waits=[(S["dve"], S["dve"].val)], inc=S["act"])
        kb.op("act", lambda e: e.activation(rows[0:1, 0:2048], rows[0:1, 0:2048], AF.Exp, scale=-0.5),
              inc=S["act"])
        rows_done = S["dve"].val
        if kind == "ln":
            kb.op("dve", lambda e: e.tensor_tensor(rows[0:1, 2048:4096], rows[0:1, 2048:4096],
                                                   rows[0:1, 0:2048], ALU.mult),
                  waits=[(S["act"], S["act"].val)], inc=S["dve"])
            rows_done = S["dve"].val
        else:
            kb.op("dve", lambda e: None, waits=[(S["act"], S["act"].val)], inc=S["dve"])
            rows_done = S["dve"].val

        def reps(e):
            last = None
            for n in range(NT):
                sl = slice(512 * n, 512 * n + 512)
                last = e.matmul(pb[4 + n][:, :], sb["ones_f"][0:1, :], rows[0:1, sl], start=True, stop=True)
                if kind == "ln":
                    last = e.matmul(pb[n][:, :], sb["ones_f"][0:1, :], rows[0:1, 2048 + 512 * n:2048 + 512 * n + 512], start=True, stop=True)
            return last
        kb.op("pe", reps, waits=[(S["dve"], rows_done)], inc=S["pe"])
        reps_done = S["pe"].val

        for c in range(DC):
            if kind == "rms":
                def apply(e, c=c):
                    last = None
                    for n in range(NT):
                        sl = slice(512 * n, 512 * n + 512)
                        last = e.scalar_tensor_tensor(out_bf(c)[:, sl], ST[:, c, sl], w_col[:, c:c + 1],
                                                      pb[4 + n][:, :], ALU.mult, ALU.mult)
                    return last
                kb.op("dve", apply, waits=[(S["pe"], reps_done)], inc=S["dve"])
            else:
                def ap1(e, c=c):
                    last = None
                    for n in range(NT):
                        sl = slice(512 * n, 512 * n + 512)
                        last = e.tensor_tensor(ST[:, c, sl], ST[:, c, sl], pb[4 + n][:, :], ALU.mult)
                    return last
                kb.op("dve", ap1, waits=[(S["pe"], reps_done)], inc=S["dve"])
                def ap2(e, c=c):
                    last = None
                    for n in range(NT):
                        sl = slice(512 * n, 512 * n + 512)
                        last = e.tensor_tensor(ST[:, c, sl], ST[:, c, sl], pb[n][:, :], ALU.subtract)
                    return last
                kb.op("dve", ap2, inc=S["dve"])
            if out_bf is not None and kind == "ln":
                kb.op("act", lambda e, c=c: e.copy(out_bf(c), ST[:, c, :]),
                      waits=[(S["dve"], S["dve"].val)], inc=S["act"])
        return S["dve"].val, S["act"].val

    def phase_A(self, s):
        kb, g, sb, S = self.kb, self.g, self.sb, self.S
        ST, BA, BB, SB2, SM = sb["STREAM"], sb["BIGA"], sb["BIGB"], sb["SB2"], sb["SMALL"]
        pb = sb["pb"]
        Wg = self.Wg

        ytk = list(getattr(self, "_y_toks", {}).values())
        if ytk:
            kb.op("act", lambda e: None, waits=ytk, inc=S["act"])
            kb.op("dve", lambda e: None, waits=ytk, inc=S["dve"])
            self._y_toks = {}

        x_tok = S["qs"].start(lambda e: e.dma_start(out=ST[:, :, :], in_=g["x_f"][s].transpose([1, 0, 2])),
                              waits=[(S["dve"], S["dve"].val), (S["act"], S["act"].val), (S["pe"], S["pe"].val)])
        kb.op("dve", lambda e: None, waits=[x_tok], inc=S["dve"])
        kb.op("act", lambda e: None, waits=[x_tok], inc=S["act"])

        pe_now = S["pe"].val
        qk_toks = []
        for i, nm in enumerate(["l0_wq", "l0_wk"]):
            qk_toks.append(S["qp"].start(lambda e, i=i, nm=nm: e.dma_start(
                out=Wg[:, 8192 * i:8192 * i + 8192].rearrange("p (c o) -> p c o", o=D),
                in_=g[nm].rearrange("(c p) o -> p c o", p=P)),
                waits=[(S["pe"], pe_now)]))

        h_dve, _ = self.norm_stream("rms", w_col=sb["l0_attn_norm_w"], out_bf=lambda c: BA[:, c, :])
        self.stage_dump("h", sb["BIGA"])
        if self.dev_stage == "hrows" and not self.stage_hit:
            self.stage_hit = True
            S["qs"].start(lambda e: e.dma_start(
                out=g["dbg"][0, 0:1, :], in_=sb["SMALL"][0:1, 0:2048]),
                waits=[(S["dve"], S["dve"].val), (S["act"], S["act"].val), (S["pe"], S["pe"].val)])
            raise StopBuild()

        TIb = SM.ap().bitcast(BF16)
        attnT, S_sb, osq = TIb[:, 4096:4224], TIb[:, 4224:4352], TIb[:, 4352:4480]
        rs_col = SM[0:1, 2304:2432]
        rep_f = SM[:, 2560:2688]
        S_f32 = SM[:, 2944:3072]
        for h in range(8):
            vofs = 16384 + (h % 2) * 1024
            v_tokk = S["qp"].start(lambda e, vofs=vofs, h=h: e.dma_start(
                out=Wg[:, vofs:vofs + 1024].rearrange("p (c o) -> p c o", o=128),
                in_=g["l0_wv"].rearrange("(c p) o -> p c o", p=P)[:, :, 128 * h:128 * h + 128]),
                waits=[(S["pe"], S["pe"].val)])
            for j in range(3):
                for n in range(NT):
                    bank = pb[(j * NT + n) % 2]
                    if j < 2:
                        lhs = (lambda c, j=j, h=h: Wg[:, 8192 * j + 1024 * c + 128 * h:8192 * j + 1024 * c + 128 * h + 128])
                        wwait = qk_toks[j]
                    else:
                        lhs = (lambda c, vofs=vofs: Wg[:, vofs + 128 * c:vofs + 128 * c + 128])
                        wwait = v_tokk

                    def proj(e, lhs=lhs, n=n, bank=bank):
                        last = None
                        for c in range(DC):
                            last = e.matmul(bank[:, :], lhs(c), BA[:, c, 512 * n:512 * n + 512],
                                            start=(c == 0), stop=(c == DC - 1))
                        return last
                    kb.op("pe", proj, waits=[wwait, (S["dve"], S["dve"].val),
                                             (S["act"], S["act"].val)], inc=S["pe"])
                    kb.op("act", lambda e, j=j, n=n, bank=bank: e.copy(
                        SB2[:, 2048 * j + 512 * n:2048 * j + 512 * n + 512], bank[:, :]),
                        waits=[(S["pe"], S["pe"].val)], inc=S["act"])
            qkv_done = S["act"].val
            t1 = S["qs"].start(lambda e: e.dma_start(
                out=SB2[:, 6144:8192].rearrange("p (t c) -> p t c", c=128),
                in_=SB2[:, 4096:6144], transpose=True),
                waits=[(S["act"], qkv_done)])
            t2 = S["qs"].start(lambda e: e.dma_start(
                out=SB2[:, 4096:6144].rearrange("p (t c) -> p t c", c=128),
                in_=SB2[:, 2048:4096], transpose=True),
                waits=[t1])
            kb.op("pe", lambda e: None, waits=[t1, t2], inc=S["pe"])

            qT = SB2[:, 0:2048]
            kT = SB2[:, 2048:4096]
            k_tok = SB2[:, 4096:6144].rearrange("p (t c) -> p t c", c=128)
            v_tok = SB2[:, 6144:8192].rearrange("p (t c) -> p t c", c=128)
            aT_sb = [TIb[:, 4096:4224], TIb[:, 4224:4352]]
            S_sb2 = [TIb[:, 4352:4480], TIb[:, 4480:4608]]
            a_bank = [pb[3], pb[6]]
            o_bank = [pb[4], pb[7]]
            mguard = {0: [], 1: []}
            eguard = {0: [], 1: []}
            sadd_done = S["dve"].val
            for c in range(16):
                cp = c % 2
                cs = slice(128 * c, 128 * c + 128)

                def peA(e, c=c, cs=cs, cp=cp):
                    e.matmul(a_bank[cp][:, 0:128], kT[:, cs], qT[:, cs], start=True, stop=True)
                    return e.matmul(pb[2][:, 0:128], k_tok[:, c, :], v_tok[:, c, :],
                                    start=True, stop=True)
                kb.op("pe", peA, waits=mguard[cp] + [(S["dve"], sadd_done)], inc=S["pe"])
                peA_done = S["pe"].val
                kb.op("dve", lambda e, cp=cp: e.tensor_tensor(
                    aT_sb[cp], a_bank[cp][:, 0:128], sb["triu_b"][:], ALU.mult),
                    waits=[(S["pe"], peA_done)], inc=S["dve"])
                mask_done = S["dve"].val
                mguard[cp] = [(S["dve"], mask_done)]

                def omm(e, c=c, cs=cs, cp=cp):
                    ins = e.matmul(o_bank[cp][:, 0:128], v_tok[:, c, :], aT_sb[cp],
                                   start=True, stop=(c == 0))
                    if c > 0:
                        ins = e.matmul(o_bank[cp][:, 0:128], S_sb2[1 - cp], qT[:, cs],
                                       start=False, stop=True)
                    return ins
                kb.op("pe", omm, waits=[(S["dve"], mask_done)] + eguard[cp], inc=S["pe"])
                o_done = S["pe"].val
                if c == 0:
                    kb.op("dve", lambda e: e.tensor_copy(S_f32, pb[2][:, 0:128]),
                          waits=[(S["pe"], peA_done)], inc=S["dve"])
                else:
                    kb.op("dve", lambda e: e.tensor_tensor(S_f32, S_f32, pb[2][:, 0:128], ALU.add),
                          waits=[(S["pe"], peA_done)], inc=S["dve"])
                sadd_done = S["dve"].val
                kb.op("dve", lambda e, cp=cp: e.tensor_copy(S_sb2[cp], S_f32), inc=S["dve"])
                kb.op("act", lambda e, h=h, cs=cs, cp=cp: e.copy(BB[:, h, cs], o_bank[cp][:, 0:128]),
                      waits=[(S["pe"], o_done)], inc=S["act"])
                eguard[cp] = [(S["act"], S["act"].val)]
            kb.op("pe", lambda e: None, waits=[(S["act"], S["act"].val)], inc=S["pe"])
            row_f = SM[0:1, 1024:1536]
            for nb4 in range(4):
                bsl = slice(512 * nb4, 512 * nb4 + 512)
                sqb = TIb[:, 512 * (nb4 % 2):512 * (nb4 % 2) + 512]
                kb.op("act", lambda e, h=h, bsl=bsl, sqb=sqb: e.activation(
                    sqb, BB[:, h, bsl], AF.Square),
                    waits=[(S["dve"], S["dve"].val), (S["pe"], S["pe"].val)], inc=S["act"])
                kb.op("pe", lambda e, sqb=sqb: e.matmul(pb[5][0:1, 0:512], sb["ones_b"][:, 0:1],
                                                        sqb, start=True, stop=True),
                      waits=[(S["act"], S["act"].val)], inc=S["pe"])
                kb.op("dve", lambda e: e.tensor_scalar(row_f, pb[5][0:1, 0:512], 1.0 / P, EPS,
                                                       ALU.mult, ALU.add),
                      waits=[(S["pe"], S["pe"].val)], inc=S["dve"])
                kb.op("act", lambda e: e.activation(row_f, row_f, AF.Ln),
                      waits=[(S["dve"], S["dve"].val)], inc=S["act"])
                kb.op("act", lambda e: e.activation(row_f, row_f, AF.Exp, scale=-0.5),
                      inc=S["act"])
                kb.op("pe", lambda e: e.matmul(pb[5][:, 0:512], sb["ones_f"][0:1, :], row_f,
                                               start=True, stop=True),
                      waits=[(S["act"], S["act"].val)], inc=S["pe"])
                kb.op("dve", lambda e, h=h, bsl=bsl: e.scalar_tensor_tensor(
                    BB[:, h, bsl], pb[5][:, 0:512], sb["l0_out_norm_w"][:, 0:1], BB[:, h, bsl],
                    ALU.mult, ALU.mult),
                    waits=[(S["pe"], S["pe"].val)], inc=S["dve"])
            kb.op("pe", lambda e: None, waits=[(S["dve"], S["dve"].val)], inc=S["pe"])

        wo_tok = S["qp"].start(lambda e: e.dma_start(
            out=Wg[:, 0:8192].rearrange("p (c o) -> p c o", o=D),
            in_=g["l0_wo"].rearrange("(c p) o -> p c o", p=P)),
            waits=[(S["pe"], S["pe"].val)])
        for n in range(NT):
            for m in range(DC):
                bank = pb[(n * DC + m) % 2]

                def womm(e, n=n, m=m, bank=bank):
                    last = None
                    for k in range(DC):
                        last = e.matmul(bank[:, :], Wg[:, 1024 * k + 128 * m:1024 * k + 128 * m + 128],
                                        BB[:, k, 512 * n:512 * n + 512], start=(k == 0), stop=(k == DC - 1))
                    return last
                kb.op("pe", womm, waits=[wo_tok, (S["dve"], S["dve"].val)], inc=S["pe"])
                kb.op("dve", lambda e, n=n, m=m, bank=bank: e.tensor_tensor(
                    ST[:, m, 512 * n:512 * n + 512], ST[:, m, 512 * n:512 * n + 512], bank[:, :], ALU.add),
                    waits=[(S["pe"], S["pe"].val)], inc=S["dve"])

        self.stage_dump("x1")
        self.stage_dump("oT", sb["BIGB"])
        h2_done, _ = self.norm_stream("rms", w_col=sb["l0_mlp_norm_w"], out_bf=lambda c: BA[:, c, :])

        guT = BB.ap().rearrange("p c t -> p (c t)").rearrange("p (k n) -> p k n", n=512)
        silu0 = SM.ap().bitcast(BF16)[:, 0:512]
        silu1 = SM.ap().bitcast(BF16)[:, 512:1024]
        for n in range(NT):
            for oc in range(16):
                gofs = (oc % 2) * 4096
                uofs = 8192 + (oc % 2) * 4096
                pe_now = S["pe"].val
                g_tok = S["qp"].start(lambda e, oc=oc, gofs=gofs: e.dma_start(
                    out=Wg[:, gofs:gofs + 2048].rearrange("p (c o) -> p c o", o=256),
                    in_=g["l0_wgate"].rearrange("(c p) o -> p c o", p=P)[:, :, 256 * oc:256 * oc + 256]),
                    waits=[(S["pe"], pe_now)])
                u_tok = S["qp"].start(lambda e, oc=oc, uofs=uofs: e.dma_start(
                    out=Wg[:, uofs:uofs + 2048].rearrange("p (c o) -> p c o", o=256),
                    in_=g["l0_wup"].rearrange("(c p) o -> p c o", p=P)[:, :, 256 * oc:256 * oc + 256]))
                for m2 in range(2):
                    mg = oc * 2 + m2
                    gb, ub = pb[m2], pb[2 + m2]
                    silu = silu0 if m2 == 0 else silu1

                    def gmm(e, n=n, m2=m2, gofs=gofs, gb=gb):
                        last = None
                        for c in range(DC):
                            last = e.matmul(gb[:, :], Wg[:, gofs + 256 * c + 128 * m2:gofs + 256 * c + 128 * m2 + 128],
                                            BA[:, c, 512 * n:512 * n + 512], start=(c == 0), stop=(c == DC - 1))
                        return last
                    kb.op("pe", gmm, waits=[g_tok, u_tok, (S["dve"], h2_done),
                                            (S["dve"], S["dve"].val)], inc=S["pe"])
                    kb.op("act", lambda e, gb=gb, silu=silu: e.activation(silu, gb[:, :], AF.Silu),
                          waits=[(S["pe"], S["pe"].val), (S["dve"], S["dve"].val)], inc=S["act"])
                    silu_done = S["act"].val

                    def umm(e, n=n, m2=m2, uofs=uofs, ub=ub):
                        last = None
                        for c in range(DC):
                            last = e.matmul(ub[:, :], Wg[:, uofs + 256 * c + 128 * m2:uofs + 256 * c + 128 * m2 + 128],
                                            BA[:, c, 512 * n:512 * n + 512], start=(c == 0), stop=(c == DC - 1))
                        return last
                    kb.op("pe", umm, inc=S["pe"])
                    kb.op("dve", lambda e, mg=mg, ub=ub, silu=silu: e.tensor_tensor(
                        guT[:, mg, :], silu, ub[:, :], ALU.mult),
                        waits=[(S["pe"], S["pe"].val), (S["act"], silu_done)], inc=S["dve"])
            gu_done = S["dve"].val
            for m in range(DC):
                dofs = (m % 2) * 4096
                d_tok = S["qp"].start(lambda e, m=m, dofs=dofs: e.dma_start(
                    out=Wg[:, dofs:dofs + 4096].rearrange("p (k c) -> p k c", c=128),
                    in_=g["l0_wdown"].rearrange("(k p) o -> p k o", p=P)[:, :, 128 * m:128 * m + 128]),
                    waits=[(S["pe"], S["pe"].val)])
                bank = pb[4 + m % 2]

                def dmm(e, dofs=dofs, bank=bank):
                    last = None
                    for k in range(32):
                        last = e.matmul(bank[:, :], Wg[:, dofs + 128 * k:dofs + 128 * k + 128],
                                        guT[:, k, :], start=(k == 0), stop=(k == 31))
                    return last
                kb.op("pe", dmm, waits=[d_tok, (S["dve"], S["dve"].val)], inc=S["pe"])
                kb.op("dve", lambda e, n=n, m=m, bank=bank: e.tensor_tensor(
                    ST[:, m, 512 * n:512 * n + 512], ST[:, m, 512 * n:512 * n + 512], bank[:, :], ALU.add),
                    waits=[(S["pe"], S["pe"].val)], inc=S["dve"])

        self.stage_dump("x2")
        ld_state = {}

        entry_dve = S["dve"].val

        def residual(c):
            slot = 2048 * (c % 2)
            guard = [(S["dve"], ld_state[c - 2])] if (c - 2) in ld_state else [(S["dve"], entry_dve)]
            tok = S["qs"].start(lambda e, c=c, slot=slot: e.dma_start(out=SM[:, slot:slot + 2048],
                                                                      in_=g["x_f"][s, c]),
                                waits=guard)
            ld_state[c] = S["dve"].val + 1
            return [tok], SM[:, slot:slot + 2048], "add"
        self.norm_stream("ln", residual=residual, out_bf=lambda c: BA[:, c, :])

    def phase_B(self, s):
        kb, g, sb, S = self.kb, self.g, self.sb, self.S
        ST, BA, BB, SB2, SM = sb["STREAM"], sb["BIGA"], sb["BIGB"], sb["SB2"], sb["SMALL"]
        pb = sb["pb"]
        Wg = self.Wg
        Q = S["qp"]
        SMb = SM.ap().bitcast(BF16)
        BBf = BB.ap().rearrange("p c t -> p (c t)")
        SB2f = SB2.ap()

        self._B_g = {}
        self.Bv = dict(
            INT=lambda j: BBf[:, 512 * j:512 * j + 512],
            V=lambda j: BBf[:, 4096 + 512 * j:4096 + 512 * j + 512],
            VN=lambda j: BBf[:, 8192 + 512 * j:8192 + 512 * j + 512],
            U=lambda j: BBf[:, 12288 + 512 * j:12288 + 512 * j + 512],
            hm=lambda r, m: SB2f[:, 2048 * r + 512 * m:2048 * r + 512 * m + 512],
            hall=lambda m: SB2f[:, 4096 + 512 * m:4096 + 512 * m + 512],
            sp=lambda j: SB2f[:, 4096 + 512 * j:4096 + 512 * j + 512],
            shh=lambda c: SB2f[0:64, 6144 + 256 * c:6144 + 256 * c + 256],
            logits=SM[:, 0:32].rearrange("p (a b) -> p a b", b=8),
            m8=SM[:, 32:64].rearrange("p (a b) -> p a b", b=8),
            wcols=SM[:, 64:72],
            hrow=SM[0:1, 512:1024],
            maskT=[SMb[0:8, 2560:3072], SMb[0:8, 3072:3584]],
            masks=[SMb[:, 4096:4160].rearrange("p (a b) -> p a b", b=8),
                   SMb[:, 4160:4224].rearrange("p (a b) -> p a b", b=8)],
            stage=SMb[:, 4352:4480],
            stageT=SMb[:, 4480:4608],
            wrep=[SMb[:, 4608:5120], SMb[:, 5120:5632]],
            haltrep=SMb[:, 5632:6144],
            T=[SMb[:, 6144 + 512 * i:6144 + 512 * i + 512] for i in range(4)],
        )
        B = self.Bv

        pe_now = S["pe"].val
        tok_res = []
        tok_res.append(Q.start(lambda e: e.dma_start(
            out=Wg[:, 0:2048].rearrange("p (c o) -> p c o", o=256),
            in_=g["f_router_w1"].rearrange("(c p) o -> p c o", p=P)), waits=[(S["pe"], pe_now)]))
        tok_res.append(Q.start(lambda e: e.dma_start(
            out=Wg[:, 2048:2064].rearrange("p (c o) -> p c o", o=8),
            in_=g["f_router_w2"].rearrange("(c p) o -> p c o", p=P))))
        tok_res.append(Q.start(lambda e: e.dma_start(
            out=Wg[:, 2064:4112].rearrange("p (c o) -> p c o", o=256),
            in_=g["f_halt_w1"].rearrange("(c p) o -> p c o", p=P))))
        tok_res.append(Q.start(lambda e: e.dma_start(out=Wg[:, 4112:4114], in_=g["f_halt_w2"][:, :])))
        tok_res.append(Q.start(lambda e: e.dma_start(
            out=Wg[:, 4114:5138].rearrange("p (c o) -> p c o", o=64),
            in_=g["f_shared_w1"].rearrange("(c p) o -> p c o", p=P))))
        tok_res.append(Q.start(lambda e: e.dma_start(
            out=Wg[0:64, 5138:8210], in_=g["f_shared_w2"][:, :])))
        tok_res.append(Q.start(lambda e: e.dma_start(
            out=Wg[:, 8224:16416].rearrange("p (c o) -> p c o", o=512),
            in_=g["f_w1all"].rearrange("(c p) o -> p c o", p=P))))
        self.B_w = dict(r1=0, r2=2048, h1=2064, h2=4112, s1=4114, s2=5138, w1all=8224,
                        w2slab=[16416, 17056], rfslab=[16448, 17472], res_toks=tok_res)

        for tt in range(NT):
            self._B_tile(s, tt)

        self.norm_stream("ln", out_bf=lambda c: sb["BIGA"][:, c, :])
        self.stage_dump("x4post", sb["STREAM"])

    def _B_tile(self, s, tt):
        kb, g, sb, S = self.kb, self.g, self.sb, self.S
        BA, SM, pb, Wg = sb["BIGA"], sb["SMALL"], sb["pb"], self.Wg
        B, W = self.Bv, self.B_w
        tsl = slice(512 * tt, 512 * tt + 512)

        rhT = sb["SB2"].ap()[:, 4096:5120].rearrange("p (c o) -> p c o", o=512)
        for mc in range(2):
            bank = pb[4 + mc]

            def rmm(e, mc=mc, bank=bank):
                last = None
                for c in range(DC):
                    last = e.matmul(bank[:, :], Wg[:, W["r1"] + 256 * c + 128 * mc:W["r1"] + 256 * c + 128 * mc + 128],
                                    BA[:, c, tsl], start=(c == 0), stop=(c == DC - 1))
                return last
            kb.op("pe", rmm, waits=W["res_toks"] + [(S["dve"], S["dve"].val), (S["act"], S["act"].val)],
                  inc=S["pe"])
            kb.op("act", lambda e, mc=mc, bank=bank: e.activation(
                rhT[:, mc, :], bank[:, :], AF.Gelu, bias=sb["f_router_b1"][:, mc:mc + 1]),
                waits=[(S["pe"], S["pe"].val)], inc=S["act"])
        rh_done = S["act"].val
        self.mark()
        for st in range(4):
            def lmm(e, st=st):
                last = None
                for c in range(2):
                    last = e.matmul(pb[6][:, 0:8], rhT[:, c, 128 * st:128 * st + 128],
                                    Wg[:, W["r2"] + 8 * c:W["r2"] + 8 * c + 8],
                                    start=(c == 0), stop=(c == 1))
                return last
            kb.op("pe", lmm, waits=[(S["act"], rh_done), (S["dve"], S["dve"].val)], inc=S["pe"])
            kb.op("dve", lambda e, st=st: e.tensor_tensor(
                B["logits"][:, st, :], pb[6][:, 0:8], sb["f_router_b2"][:, :], ALU.add),
                waits=[(S["pe"], S["pe"].val)], inc=S["dve"])
            kb.op("dve", lambda e, st=st: e.max(out=B["m8"][:, st, :], in_=B["logits"][:, st, :]),
                  inc=S["dve"])
            kb.op("dve", lambda e, st=st: e.tensor_tensor(
                B["wcols"][:, 2 * st:2 * st + 1], B["m8"][:, st, 0:1], B["m8"][:, st, 1:2], ALU.subtract),
                inc=S["dve"])
            for r in range(2):
                kb.op("dve", lambda e, st=st, r=r: e.tensor_single_scalar(
                    B["masks"][r][:, st, :], B["logits"][:, st, :], B["m8"][:, st, r:r + 1], ALU.is_equal),
                    inc=S["dve"])
        self.mark()
        for r in range(2):
            def tmm(e, r=r):
                last = None
                for st in range(4):
                    last = e.matmul(pb[6][0:8, 128 * st:128 * st + 128], B["masks"][r][:, st, :],
                                    sb["ident_b"][:, :], start=True, stop=True)
                return last
            kb.op("pe", tmm, waits=[(S["dve"], S["dve"].val)], inc=S["pe"])
            kb.op("dve", lambda e, r=r: e.tensor_copy(B["maskT"][r], pb[6][0:8, :]),
                  waits=[(S["pe"], S["pe"].val)], inc=S["dve"])

        self.mark()
        for it in range(2):
            self._B_iter(s, tt, tsl, it)
            self.mark()
        self._B_refine(s, tt, tsl)
        self.mark()

    def _B_wsig(self):
        kb, g, sb, S = self.kb, self.g, self.sb, self.S
        B, pb = self.Bv, sb["pb"]

        def sig(e):
            last = None
            for st in range(4):
                last = e.activation(B["wcols"][:, 2 * st:2 * st + 1], B["wcols"][:, 2 * st:2 * st + 1],
                                    AF.Sigmoid)
            return last
        kb.op("act", sig, waits=[(S["dve"], S["dve"].val)], inc=S["act"])

        def w2c(e):
            last = None
            for st in range(4):
                last = e.tensor_scalar(B["wcols"][:, 2 * st + 1:2 * st + 2],
                                       B["wcols"][:, 2 * st:2 * st + 1], -1.0, 1.0, ALU.mult, ALU.add)
            return last
        kb.op("dve", w2c, waits=[(S["act"], S["act"].val)], inc=S["dve"])
        self.mark()
        SMb = sb["SMALL"].ap().bitcast(BF16)
        stage8 = SMb[:, 4352:4360]
        rows8 = SMb[0:8, 4480:4608]
        kb.op("dve", lambda e: e.tensor_single_scalar(
            stage8, sb["SMALL"][:, 64:72], sb["sw_col"][:, 1:2], ALU.mult),
            waits=[(S["pe"], S["pe"].val)], inc=S["dve"])
        kb.op("pe", lambda e: e.matmul(pb[7][0:8, 0:128], stage8, sb["ident_b"][:, :],
                                       start=True, stop=True),
              waits=[(S["dve"], S["dve"].val)], inc=S["pe"])
        kb.op("dve", lambda e: e.tensor_copy(rows8, pb[7][0:8, 0:128]),
              waits=[(S["pe"], S["pe"].val)], inc=S["dve"])
        stok = S["qs"].start(lambda e: e.dma_start(out=g["scr_w8"][:, :], in_=rows8),
                             waits=[(S["dve"], S["dve"].val)])
        toks = []
        for st in range(4):
            for k in range(2):
                i = 2 * st + k
                toks.append(S["qs"].start(lambda e, st=st, k=k, i=i: e.dma_start(
                    out=B["wrep"][k][:, 128 * st:128 * st + 128],
                    in_=g["scr_w8"][i:i + 1, :].broadcast_to([128, 128])),
                    waits=[stok]))
        self._wrep_toks = toks
        self.mark()

    def _B_iter(self, s, tt, tsl, it):
        kb, g, sb, S = self.kb, self.g, self.sb, self.S
        BA, pb, Wg = sb["BIGA"], sb["pb"], self.Wg
        B, W = self.Bv, self.B_w
        Q = S["qp"]
        nk = 8 if it == 0 else 16

        def intop(j):
            return BA[:, j, tsl] if it == 0 else B["INT"](j)

        def ctxop(c):
            return intop(c) if c < 8 else B["V"](c - 8)

        for mc in range(2):
            def hmm(e, mc=mc):
                last = None
                for c in range(DC):
                    last = e.matmul(pb[4 + mc][:, :], Wg[:, W["h1"] + 256 * c + 128 * mc:W["h1"] + 256 * c + 128 * mc + 128],
                                    intop(c), start=(c == 0), stop=(c == DC - 1))
                return last
            kb.op("pe", hmm, waits=[(S["dve"], S["dve"].val), (S["act"], S["act"].val)], inc=S["pe"])
            kb.op("act", lambda e, mc=mc: e.activation(
                B["T"][mc], pb[4 + mc][:, :], AF.Gelu, bias=sb["f_halt_b1"][:, mc:mc + 1]),
                waits=[(S["pe"], S["pe"].val), (S["dve"], S["dve"].val)], inc=S["act"])
        hh_done = S["act"].val
        self.mark()

        def hrmm(e):
            last = None
            for c in range(2):
                last = e.matmul(pb[6][0:1, :], Wg[:, W["h2"] + c:W["h2"] + c + 1], B["T"][c],
                                start=(c == 0), stop=(c == 1))
            return last
        kb.op("pe", hrmm, waits=[(S["act"], hh_done)], inc=S["pe"])
        kb.op("dve", lambda e: e.tensor_copy(B["hrow"], pb[6][0:1, :]),
              waits=[(S["pe"], S["pe"].val)], inc=S["dve"])

        def shmm(e):
            last = None
            for c in range(nk):
                last = e.matmul(pb[4][0:64, :], Wg[:, W["s1"] + 64 * c:W["s1"] + 64 * c + 64],
                                ctxop(c), start=(c == 0), stop=(c == nk - 1))
            return last
        kb.op("pe", shmm, waits=[(S["dve"], S["dve"].val)], inc=S["pe"])
        shh = sb["SB2"].ap()[0:64, 6144:6656]
        kb.op("act", lambda e: e.activation(shh, pb[4][0:64, :], AF.Gelu,
                                            bias=sb["f_shared_b1"][0:64, 0:1]),
              waits=[(S["pe"], S["pe"].val)], inc=S["act"])

        for m in range(4):
            def amm(e, m=m):
                last = None
                for c in range(nk):
                    last = e.matmul(pb[m][:, :], Wg[:, W["w1all"] + 512 * c + 128 * m:W["w1all"] + 512 * c + 128 * m + 128],
                                    ctxop(c), start=(c == 0), stop=(c == nk - 1))
                return last
            kb.op("pe", amm, waits=[(S["dve"], S["dve"].val), (S["act"], S["act"].val)], inc=S["pe"])
            kb.op("act", lambda e, m=m: e.activation(
                B["hall"](m), pb[m][:, :], AF.Gelu, bias=sb["f_b1all"][:, m:m + 1]),
                waits=[(S["pe"], S["pe"].val)], inc=S["act"])
        hall_done = S["act"].val
        self._B_g["lnbank"] = {0: [(S["act"], hall_done)], 1: [(S["act"], hall_done)]}
        self._B_g["sgab"] = {0: [(S["act"], hall_done)], 1: [(S["act"], hall_done)]}
        self.mark()

        for r in range(2):
            for m in range(4):
                bank = pb[4 + (m % 2)]
                kb.op("pe", lambda e, r=r, m=m, bank=bank: e.matmul(
                    bank[:, :], sb["e8_b"][:, 128 * m:128 * m + 128], B["maskT"][r],
                    start=True, stop=True),
                    waits=[(S["act"], hall_done), (S["dve"], S["dve"].val)], inc=S["pe"])
                kb.op("dve", lambda e, r=r, m=m, bank=bank: e.tensor_tensor(
                    B["hm"](r, m), B["hall"](m), bank[:, :], ALU.mult),
                    waits=[(S["pe"], S["pe"].val)], inc=S["dve"])
        hm_done = S["dve"].val
        self._B_g["sgs2"] = {0: [(S["dve"], hm_done)], 1: [(S["dve"], hm_done)]}
        self.mark()

        for p in range(3):
            shared = (p == 0)
            r = p - 1

            def cmm(e, m, bank, shared=shared, r=r):
                if shared:
                    return e.matmul(bank[:, :], Wg[0:64, W["s2"] + 128 * m:W["s2"] + 128 * m + 128],
                                    shh, start=True, stop=True)
                slab = W["w2slab"][m % 2]
                last = None
                for c in range(4):
                    last = e.matmul(bank[:, :], Wg[:, slab + 128 * c:slab + 128 * c + 128],
                                    B["hm"](r, c), start=(c == 0), stop=False)
                return e.matmul(bank[:, :], Wg[0:8, slab + 512:slab + 512 + 128],
                                B["maskT"][r], start=False, stop=True)

            def slab_load(m, extra=()):
                if shared:
                    return []
                slab = W["w2slab"][m % 2]
                t1 = Q.start(lambda e, m=m, slab=slab: e.dma_start(
                    out=Wg[:, slab:slab + 512].rearrange("p (c o) -> p c o", o=128),
                    in_=g["f_w2cat"][0:512].rearrange("(c p) o -> p c o", p=P)[:, :, 128 * m:128 * m + 128]),
                    waits=list(extra))
                t2 = Q.start(lambda e, m=m, slab=slab: e.dma_start(
                    out=Wg[0:8, slab + 512:slab + 512 + 128],
                    in_=g["f_w2cat"][512:520, 128 * m:128 * m + 128]))
                return [t1, t2]

            gbank = self._B_g.setdefault("lnbank", {0: [], 1: []})
            gtmp = self._B_g.setdefault("lntmp", {0: [], 1: []})
            for j in range(DC):
                m = 8 + j
                p2 = j % 2
                Ta, Tb = (B["T"][2], B["T"][3]) if p2 == 0 else (B["T"][0], B["T"][1])
                bank = pb[p2]
                toks = slab_load(m, extra=[(S["pe"], S["pe"].val)])
                kb.op("pe", lambda e, m=m, bank=bank: cmm(e, m, bank),
                      waits=toks + gbank[p2] + [(S["dve"], hm_done)], inc=S["pe"])
                mm_done = S["pe"].val
                bias = sb["f_shared_b2"][:, m:m + 1] if shared else 0.0
                kb.op("act", lambda e, bank=bank, bias=bias, Ta=Ta: e.activation(
                    Ta, bank[:, :], AF.Abs, bias=bias),
                    waits=[(S["pe"], mm_done)] + gtmp[p2], inc=S["act"])
                kb.op("act", lambda e, Ta=Ta: e.activation(Ta, Ta, AF.Exp, scale=-1.0),
                      inc=S["act"])
                kb.op("act", lambda e, Ta=Ta: e.activation(Ta, Ta, AF.Ln, bias=1.0),
                      inc=S["act"])
                kb.op("act", lambda e, bank=bank, bias=bias, Tb=Tb: e.activation(
                    Tb, bank[:, :], AF.Relu, bias=bias),
                    inc=S["act"])
                gbank[p2] = [(S["act"], S["act"].val)]
                kb.op("dve", lambda e, j=j, Ta=Ta, Tb=Tb: e.tensor_tensor(
                    B["sp"](j), Ta, Tb, ALU.add),
                    waits=[(S["act"], S["act"].val)], inc=S["dve"])
                gtmp[p2] = [(S["dve"], S["dve"].val)]
            self.mark()

            if shared:
                kb.op("act", lambda e: e.activation(B["hrow"], B["hrow"], AF.Sigmoid,
                                                    bias=sb["f_halt_b2"][0:1, 0:1]),
                      waits=[(S["dve"], S["dve"].val)], inc=S["act"])
                self.mark()
                if it == 0:
                    self._B_wsig()
                    kb.op("dve", lambda e: None, waits=self._wrep_toks, inc=S["dve"])
                self.mark()
                kb.op("pe", lambda e: e.matmul(pb[7][:, 0:512], sb["ones_f"][0:1, :],
                                               B["hrow"], start=True, stop=True),
                      waits=[(S["act"], S["act"].val), (S["dve"], S["dve"].val)], inc=S["pe"])
                kb.op("dve", lambda e: e.tensor_copy(B["haltrep"], pb[7][:, 0:512]),
                      waits=[(S["pe"], S["pe"].val)], inc=S["dve"])
                self._B_g["sgs2"][0] = [(S["dve"], S["dve"].val)]
                self._B_g["sgs2"][1] = [(S["dve"], S["dve"].val)]
                self.mark()

            gab = self._B_g.setdefault("sgab", {0: [], 1: []})
            ggb = self._B_g.setdefault("sggb", {0: [], 1: []})
            gs2 = self._B_g.setdefault("sgs2", {0: [], 1: []})
            for j in range(DC):
                p2 = j % 2
                ab = pb[2 + p2]
                gb = pb[p2]
                sa = pb[4 + p2]
                sg2 = pb[6 + p2]
                if it > 0:
                    toks_a = slab_load(j, extra=[(S["pe"], S["pe"].val)])
                    kb.op("pe", lambda e, j=j, ab=ab: cmm(e, j, ab),
                          waits=toks_a + gab[p2], inc=S["pe"])
                    a_done = S["pe"].val
                    bias_a = sb["f_shared_b2"][:, j:j + 1] if shared else 0.0
                    kb.op("act", lambda e, ab=ab, bias_a=bias_a, sa=sa: e.activation(
                        sa[:, :], ab[:, :], AF.Sigmoid, bias=bias_a),
                        waits=[(S["pe"], a_done)] + gs2[p2], inc=S["act"])
                    gab[p2] = [(S["act"], S["act"].val)]
                    kb.op("dve", lambda e, j=j, sa=sa: e.tensor_tensor(
                        B["T"][3], sa[:, :], B["V"](j), ALU.mult),
                        waits=[(S["act"], S["act"].val)], inc=S["dve"])
                t1_done = S["dve"].val
                toks_g = slab_load(16 + j, extra=[(S["pe"], S["pe"].val)])
                kb.op("pe", lambda e, j=j, gb=gb: cmm(e, 16 + j, gb),
                      waits=toks_g + gbank[p2], inc=S["pe"])
                g_done = S["pe"].val
                bias_g = sb["f_shared_b2"][:, 16 + j:16 + j + 1] if shared else 0.0
                kb.op("act", lambda e, gb=gb, bias_g=bias_g, sg2=sg2: e.activation(
                    sg2[:, :], gb[:, :], AF.Sigmoid, bias=bias_g),
                    waits=[(S["pe"], g_done)] + gs2[p2], inc=S["act"])
                gbank[p2] = [(S["act"], S["act"].val)]
                sg_done = S["act"].val
                kb.op("dve", lambda e, j=j: e.tensor_tensor(B["T"][1], B["sp"](j), intop(j), ALU.mult),
                      inc=S["dve"])
                kb.op("dve", lambda e, j=j: e.scalar_tensor_tensor(
                    B["T"][1], B["sp"](j), sb["f_mu"][:, j:j + 1], B["T"][1], ALU.mult, ALU.subtract),
                    inc=S["dve"])
                if it > 0:
                    kb.op("dve", lambda e: e.tensor_tensor(B["T"][1], B["T"][3], B["T"][1], ALU.add),
                          inc=S["dve"])
                kb.op("dve", lambda e, sg2=sg2: e.tensor_tensor(
                    B["T"][3], sg2[:, :], B["T"][1], ALU.mult),
                    waits=[(S["act"], sg_done)], inc=S["dve"])
                gs2[p2] = [(S["dve"], S["dve"].val)]
                self.mark()
                if shared:
                    kb.op("dve", lambda e, j=j: e.tensor_single_scalar(
                        B["VN"](j), B["T"][1], sb["sw_col"][:, 0:1], ALU.mult), inc=S["dve"])
                    kb.op("dve", lambda e, j=j: e.tensor_single_scalar(
                        B["U"](j), B["T"][3], sb["sw_col"][:, 0:1], ALU.mult), inc=S["dve"])
                else:
                    kb.op("dve", lambda e, j=j, r=r: e.tensor_tensor(
                        B["T"][1], B["T"][1], B["wrep"][r], ALU.mult), inc=S["dve"])
                    kb.op("dve", lambda e, j=j: e.tensor_tensor(
                        B["VN"](j), B["VN"](j), B["T"][1], ALU.add), inc=S["dve"])
                    kb.op("dve", lambda e, j=j, r=r: e.tensor_tensor(
                        B["T"][3], B["T"][3], B["wrep"][r], ALU.mult), inc=S["dve"])
                    kb.op("dve", lambda e, j=j: e.tensor_tensor(
                        B["U"](j), B["U"](j), B["T"][3], ALU.add), inc=S["dve"])
            gtmp[0] = [(S["dve"], S["dve"].val)]
            gtmp[1] = [(S["dve"], S["dve"].val)]

        gtmp = self._B_g["lntmp"]
        for j in range(DC):
            kb.op("dve", lambda e, j=j: e.tensor_tensor(B["T"][2], B["U"](j), B["haltrep"], ALU.mult),
                  inc=S["dve"])
            kb.op("dve", lambda e, j=j: e.scalar_tensor_tensor(
                B["INT"](j), B["T"][2], 0.1, intop(j), ALU.mult, ALU.add), inc=S["dve"])
            if it == 0:
                kb.op("dve", lambda e, j=j: e.tensor_tensor(B["V"](j), B["VN"](j), B["haltrep"], ALU.mult),
                      inc=S["dve"])
            else:
                kb.op("dve", lambda e, j=j: e.tensor_tensor(B["T"][2], B["VN"](j), B["V"](j), ALU.subtract),
                      inc=S["dve"])
                kb.op("dve", lambda e, j=j: e.tensor_tensor(B["T"][2], B["T"][2], B["haltrep"], ALU.mult),
                      inc=S["dve"])
                kb.op("dve", lambda e, j=j: e.tensor_tensor(B["V"](j), B["T"][2], B["V"](j), ALU.add),
                      inc=S["dve"])
        gtmp[0] = [(S["dve"], S["dve"].val)]
        gtmp[1] = [(S["dve"], S["dve"].val)]
        for key, base in [(f"int{it}", 0), (f"v{it}", 4096)]:
            if self.dev_stage == key and tt == 0 and not self.stage_hit:
                self.stage_hit = True
                for half in range(2):
                    S["qp"].start(lambda e, half=half, base=base: e.dma_start(
                        out=self.g["dbg"][half, :, :],
                        in_=sb["BIGB"].ap().rearrange("p c t -> p (c t)")[:, base + 2048 * half:base + 2048 * half + 2048]),
                        waits=[(S["dve"], S["dve"].val)])
                raise StopBuild()

    def _B_refine(self, s, tt, tsl):
        kb, g, sb, S = self.kb, self.g, self.sb, self.S
        ST, pb, Wg = sb["STREAM"], sb["pb"], self.Wg
        B, W = self.Bv, self.B_w
        Q = S["qp"]
        SB2f = sb["SB2"].ap()
        r1T = lambda m: SB2f[:, 512 * m:512 * m + 512]
        tmp = sb["SMALL"][:, 2048:2560]

        for m in range(16):
            slab = W["rfslab"][m % 2]
            tok = Q.start(lambda e, m=m, slab=slab: e.dma_start(
                out=Wg[:, slab:slab + 1024].rearrange("p (c o) -> p c o", o=128),
                in_=g["f_refine_w1"].rearrange("(c p) o -> p c o", p=P)[:, :, 128 * m:128 * m + 128]),
                waits=[(S["pe"], S["pe"].val)])
            bank = pb[m % 2]

            def r1mm(e, m=m, slab=slab, bank=bank):
                last = None
                for c in range(DC):
                    last = e.matmul(bank[:, :], Wg[:, slab + 128 * c:slab + 128 * c + 128],
                                    B["INT"](c), start=(c == 0), stop=(c == DC - 1))
                return last
            kb.op("pe", r1mm, waits=[tok, (S["act"], S["act"].val), (S["dve"], S["dve"].val)], inc=S["pe"])
            kb.op("act", lambda e, m=m, bank=bank: e.activation(
                r1T(m), bank[:, :], AF.Gelu, bias=sb["f_refine_b1"][:, m:m + 1]),
                waits=[(S["pe"], S["pe"].val)], inc=S["act"])
        r1_done = S["act"].val

        for m in range(DC):
            slab = W["rfslab"][0]
            tok = Q.start(lambda e, m=m, slab=slab: e.dma_start(
                out=Wg[:, slab:slab + 2048].rearrange("p (c o) -> p c o", o=128),
                in_=g["f_refine_w2"].rearrange("(c p) o -> p c o", p=P)[:, :, 128 * m:128 * m + 128]),
                waits=[(S["pe"], S["pe"].val)])
            bank = pb[4 + m % 2]

            def r2mm(e, m=m, slab=slab, bank=bank):
                last = None
                for c in range(16):
                    last = e.matmul(bank[:, :], Wg[:, slab + 128 * c:slab + 128 * c + 128],
                                    r1T(c), start=(c == 0), stop=(c == 15))
                return last
            kb.op("pe", r2mm, waits=[tok, (S["act"], r1_done), (S["dve"], S["dve"].val)], inc=S["pe"])
            kb.op("dve", lambda e, m=m, bank=bank: e.tensor_scalar(
                tmp, bank[:, :], sb["f_refine_b2"][:, m:m + 1], sb["f_iw"][:, m:m + 1],
                ALU.add, ALU.mult),
                waits=[(S["pe"], S["pe"].val)], inc=S["dve"])
            kb.op("dve", lambda e, m=m: e.scalar_tensor_tensor(
                ST[:, m, tsl], ST[:, m, tsl], 2.0, tmp, ALU.mult, ALU.add), inc=S["dve"])

    def phase_C(self, s):
        kb, g, sb, S = self.kb, self.g, self.sb, self.S
        ST, BA, BB, SB2, SM = sb["STREAM"], sb["BIGA"], sb["BIGB"], sb["SB2"], sb["SMALL"]
        pb = sb["pb"]
        Wg = self.Wg
        Q = S["qp"]
        SMb = SM.ap().bitcast(BF16)
        CS = sb["CSTAT"]
        CSb = CS.ap().bitcast(BF16)

        qkv = Wg[:, 0:12288].rearrange("p (w pr t c) -> p w pr t c", w=3, pr=2, t=16)
        qhatT = Wg[:, 12288:14336].rearrange("p (t c) -> p t c", c=128)
        khatT = Wg[:, 14336:16384].rearrange("p (t c) -> p t c", c=128)
        slabs = [16384, 17408]
        sq_tmp = SMb[:, 0:4096]
        murep = SMb[:, 4096:8192]

        lnrow_ofs = dict(a_qln_w=16384, a_qln_b=16640, a_kln_w=16896, a_kln_b=17152)
        for quad in range(4):
            row_toks = []
            for nm, ofs in lnrow_ofs.items():
                row_toks.append(Q.start(lambda e, nm=nm, ofs=ofs: e.dma_start(
                    out=Wg[:, ofs:ofs + 256],
                    in_=g[nm][:, 256 * quad:256 * quad + 256].broadcast_to([P, 256])),
                    waits=[(S["pe"], S["pe"].val)]))
            oc = 256 * quad
            tok_qk = []
            for w, wname in enumerate(["a_wq", "a_wk"]):
                tok_qk.append(Q.start(lambda e, w=w, wname=wname, oc=oc: e.dma_start(
                    out=Wg[:, 12288 + 2048 * w:12288 + 2048 * w + 2048]
                    .rearrange("p (c o) -> p c o", o=256),
                    in_=g[wname].rearrange("(c p) o -> p c o", p=P)[:, :, oc:oc + 256]),
                    waits=[(S["pe"], S["pe"].val)]))
            pguard = {0: [(S["act"], S["act"].val), (S["dve"], S["dve"].val)],
                      1: [(S["act"], S["act"].val), (S["dve"], S["dve"].val)]}
            for t in range(16):
                bank = pb[t % 2]

                def pmm(e, t=t, bank=bank):
                    last = None
                    for c in range(DC):
                        for w in range(2):
                            last = e.matmul(bank[:, 256 * w:256 * w + 256],
                                            BA[:, c, 128 * t:128 * t + 128],
                                            Wg[:, 12288 + 2048 * w + 256 * c:
                                               12288 + 2048 * w + 256 * c + 256],
                                            start=(c == 0 and w == 0),
                                            stop=(c == DC - 1 and w == 1))
                    return last
                kb.op("pe", pmm, waits=tok_qk + pguard[t % 2], inc=S["pe"])
                kb.op("act", lambda e, t=t, bank=bank: e.copy(
                    qkv[:, 0:2, :, t, :], bank[:, 0:512]),
                    waits=[(S["pe"], S["pe"].val)], inc=S["act"])
                pguard[t % 2] = [(S["act"], S["act"].val)]
            kb.op("pe", lambda e: None, waits=[(S["pe"], S["pe"].val)], inc=S["pe"])
            tok_v = Q.start(lambda e, oc=oc: e.dma_start(
                out=Wg[:, 12288:14336].rearrange("p (c o) -> p c o", o=256),
                in_=g["a_wv"].rearrange("(c p) o -> p c o", p=P)[:, :, oc:oc + 256]),
                waits=[(S["pe"], S["pe"].val)])
            vguard = {0: [(S["act"], S["act"].val), (S["dve"], S["dve"].val)],
                      1: [(S["act"], S["act"].val), (S["dve"], S["dve"].val)]}
            for t in range(16):
                vbank = pb[2 + t % 2]

                def vmm(e, t=t, vbank=vbank):
                    last = None
                    for c in range(DC):
                        last = e.matmul(vbank[:, 0:256], BA[:, c, 128 * t:128 * t + 128],
                                        Wg[:, 12288 + 256 * c:12288 + 256 * c + 256],
                                        start=(c == 0), stop=(c == DC - 1))
                    return last
                kb.op("pe", vmm, waits=[tok_v] + vguard[t % 2], inc=S["pe"])
                kb.op("act", lambda e, t=t, vbank=vbank: e.copy(
                    qkv[:, 2, :, t, :], vbank[:, 0:256]),
                    waits=[(S["pe"], S["pe"].val)], inc=S["act"])
                vguard[t % 2] = [(S["act"], S["act"].val)]
            proj_done = S["act"].val

            for w, (wr, br) in enumerate([("a_qln_w", "a_qln_b"), ("a_kln_w", "a_kln_b")]):
                flat = qkv[:, w, :, :, :].rearrange("p pr t c -> p (pr t c)")
                v64 = flat.rearrange("p (a b) -> p a b", b=64)
                kb.op("act", lambda e, flat=flat: e.activation(sq_tmp, flat, AF.Square),
                      waits=[(S["act"], proj_done), (S["dve"], S["dve"].val)], inc=S["act"])
                kb.op("dve", lambda e, v64=v64: e.reduce_sum(CS[:, 256:320], v64, AX.X),
                      waits=[(S["act"], S["act"].val)], inc=S["dve"])
                kb.op("dve", lambda e: e.reduce_sum(
                    CS[:, 320:384], sq_tmp.rearrange("p (a b) -> p a b", b=64), AX.X), inc=S["dve"])
                kb.op("dve", lambda e: e.tensor_scalar(CS[:, 256:320], CS[:, 256:320],
                                                       1.0 / 64, None, ALU.mult), inc=S["dve"])
                kb.op("dve", lambda e: e.tensor_tensor(CS[:, 384:448], CS[:, 256:320],
                                                       CS[:, 256:320], ALU.mult), inc=S["dve"])
                kb.op("dve", lambda e: e.scalar_tensor_tensor(
                    CS[:, 320:384], CS[:, 320:384], 1.0 / 64, CS[:, 384:448],
                    ALU.mult, ALU.subtract), inc=S["dve"])
                kb.op("dve", lambda e: e.tensor_scalar(CS[:, 320:384], CS[:, 320:384],
                                                       EPS, None, ALU.add), inc=S["dve"])
                kb.op("act", lambda e: e.activation(CS[:, 320:384], CS[:, 320:384], AF.Ln),
                      waits=[(S["dve"], S["dve"].val)], inc=S["act"])
                kb.op("act", lambda e: e.activation(CS[:, 320:384], CS[:, 320:384], AF.Exp, scale=-0.5),
                      inc=S["act"])
                kb.op("dve", lambda e: e.tensor_copy(
                    murep.rearrange("p (a b) -> p a b", b=64),
                    CS[:, 256:320].rearrange("p (a b) -> p a b", b=1).broadcast_to([P, 64, 64])),
                    waits=[(S["act"], S["act"].val)], inc=S["dve"])
                kb.op("dve", lambda e, flat=flat: e.tensor_tensor(flat, flat, murep, ALU.subtract),
                      inc=S["dve"])
                kb.op("dve", lambda e: e.tensor_copy(
                    murep.rearrange("p (a b) -> p a b", b=64),
                    CS[:, 320:384].rearrange("p (a b) -> p a b", b=1).broadcast_to([P, 64, 64])),
                    inc=S["dve"])
                kb.op("dve", lambda e, flat=flat: e.tensor_tensor(flat, flat, murep, ALU.mult),
                      inc=S["dve"])
                v4d = qkv[:, w, :, :, :]
                wrow = Wg[:, lnrow_ofs[wr]:lnrow_ofs[wr] + 256].rearrange(
                    "p (pr c) -> p pr c", pr=2).unsqueeze(2).broadcast_to([P, 2, 16, 128])
                brow = Wg[:, lnrow_ofs[br]:lnrow_ofs[br] + 256].rearrange(
                    "p (pr c) -> p pr c", pr=2).unsqueeze(2).broadcast_to([P, 2, 16, 128])
                kb.op("dve", lambda e, v4d=v4d, wrow=wrow: e.tensor_tensor(v4d, v4d, wrow, ALU.mult),
                      waits=row_toks, inc=S["dve"])
                kb.op("dve", lambda e, v4d=v4d, brow=brow: e.tensor_tensor(v4d, v4d, brow, ALU.add),
                      inc=S["dve"])
            ln_done = S["dve"].val

            for pr2 in range(2):
                pair = 2 * quad + pr2
                tq = S["qs"].start(lambda e, pr2=pr2: e.dma_start(
                    out=qhatT[:, :, :], in_=qkv[:, 0, pr2, :, :].rearrange("p t c -> p (t c)"),
                    transpose=True), waits=[(S["dve"], ln_done), (S["pe"], S["pe"].val)])
                tk = S["qs"].start(lambda e, pr2=pr2: e.dma_start(
                    out=khatT[:, :, :], in_=qkv[:, 1, pr2, :, :].rearrange("p t c -> p (t c)"),
                    transpose=True), waits=[(S["dve"], ln_done), (S["pe"], S["pe"].val)])
                qhf = qhatT.rearrange("p t c -> p (t c)")
                khf = khatT.rearrange("p t c -> p (t c)")
                prev = None
                cbg = {k: [(S["dve"], S["dve"].val), (S["act"], S["act"].val)] for k in range(6)}
                self._C_cbg = cbg
                for qi in range(16):
                    Wi = 128 * (qi + 1)
                    nb = (Wi + 511) // 512
                    par = qi % 2
                    tp_toks = []
                    for h2 in range(2):
                        hofs = 64 * h2
                        bidx = list(range(4)) if h2 == 0 else [4, 5, 0, 1]
                        banks = [pb[b] for b in bidx]
                        for b in range(nb):
                            w0 = 512 * b
                            wspan = min(512, Wi - w0)
                            wts = list(cbg[bidx[b]])
                            if qi == 0 and b == 0:
                                wts += [tq, tk]
                            kb.op("pe", lambda e, qi=qi, hofs=hofs, w0=w0, wspan=wspan, bank=banks[b]:
                                  e.matmul(bank[:, 0:wspan],
                                           qhf[hofs:hofs + 64, 128 * qi:128 * qi + 128],
                                           khf[hofs:hofs + 64, w0:w0 + wspan],
                                           start=True, stop=True, tile_position=(hofs, 0)),
                                  waits=wts, inc=S["pe"])
                        mm_done = S["pe"].val
                        dspan = Wi - 512 * (nb - 1)
                        kb.op("dve", lambda e, bank=banks[nb - 1], dspan=dspan: e.tensor_tensor(
                            bank[:, dspan - 128:dspan], bank[:, dspan - 128:dspan],
                            sb["trilneg_b"][:, :], ALU.add),
                            waits=[(S["pe"], mm_done)], inc=S["dve"])
                        for b in range(nb):
                            wspan = min(512, Wi - 512 * b)
                            kb.op("dve", lambda e, h2=h2, b=b, bank=banks[b], wspan=wspan: e.reduce_max(
                                CS[:, 16 * h2 + b:16 * h2 + b + 1], bank[:, 0:wspan], AX.X),
                                waits=[(S["pe"], mm_done)], inc=S["dve"])

                        kb.op("dve", lambda e, h2=h2: e.tensor_copy(
                            CS[:, 16 * h2 + 9:16 * h2 + 10], CS[:, 16 * h2:16 * h2 + 1]), inc=S["dve"])
                        for b in range(1, nb):
                            kb.op("dve", lambda e, h2=h2, b=b: e.tensor_tensor(
                                CS[:, 16 * h2 + 9:16 * h2 + 10], CS[:, 16 * h2 + 9:16 * h2 + 10],
                                CS[:, 16 * h2 + b:16 * h2 + b + 1], ALU.max), inc=S["dve"])
                        kb.op("dve", lambda e, h2=h2: e.tensor_scalar(
                            CS[:, 16 * h2 + 9:16 * h2 + 10], CS[:, 16 * h2 + 9:16 * h2 + 10],
                            -0.125, None, ALU.mult), inc=S["dve"])
                        m_done = S["dve"].val
                        pbuf = SMb[:, 4096 * par + 2048 * h2:4096 * par + 2048 * h2 + 2048]
                        sum_cols = []
                        for b in range(nb):
                            w0 = 512 * b
                            wspan = min(512, Wi - w0)
                            sum_cols.append(16 * h2 + 4 + b)
                            kb.op("act", lambda e, bank=banks[b], pbuf=pbuf, w0=w0, wspan=wspan, h2=h2, b=b:
                                  e.activation(pbuf[:, w0:w0 + wspan], bank[:, 0:wspan], AF.Exp,
                                               scale=0.125, bias=CS[:, 16 * h2 + 9:16 * h2 + 10],
                                               accum_out=CS[:, 16 * h2 + 4 + b:16 * h2 + 5 + b]),
                                  waits=[(S["dve"], m_done)], inc=S["act"])
                        for b in range(nb):
                            cbg[bidx[b]] = [(S["dve"], m_done), (S["act"], S["act"].val)]

                        rdsl = slice(16 * h2 + 10 + par, 16 * h2 + 11 + par)
                        kb.op("dve", lambda e, rdsl=rdsl, c0=sum_cols[0]: e.tensor_copy(
                            CS[:, rdsl], CS[:, c0:c0 + 1]),
                            waits=[(S["act"], S["act"].val)], inc=S["dve"])
                        for ccol in sum_cols[1:]:
                            kb.op("dve", lambda e, rdsl=rdsl, ccol=ccol: e.tensor_tensor(
                                CS[:, rdsl], CS[:, rdsl], CS[:, ccol:ccol + 1], ALU.add), inc=S["dve"])
                        kb.op("dve", lambda e, rdsl=rdsl: e.reciprocal(CS[:, rdsl], CS[:, rdsl]),
                              inc=S["dve"])
                        kb.op("dve", lambda e, rdsl=rdsl, h2=h2, par=par: e.tensor_copy(
                            CSb[:, 64 + 4 * h2 + 2 * par:64 + 4 * h2 + 2 * par + 1], CS[:, rdsl]),
                            inc=S["dve"])
                        ptre = sb["SB2"].ap()[:, 4096 * h2 + 2048 * par:4096 * h2 + 2048 * par + 2048]
                        tp_toks.append(S["qs"].start(lambda e, pbuf=pbuf, ptre=ptre, Wi=Wi: e.dma_start(
                            out=ptre[:, 0:Wi].rearrange("p (t c) -> p t c", c=128),
                            in_=pbuf[:, 0:Wi], transpose=True),
                            waits=[(S["dve"], S["dve"].val)]))
                    if prev is not None:
                        self._C_av(pair, prev[0], pr2, prev[1])
                    prev = (qi, tp_toks)
                self._C_av(pair, prev[0], pr2, prev[1])

        wu_tok = Q.start(lambda e: e.dma_start(
            out=Wg[:, 0:8192].rearrange("p (c o) -> p c o", o=D),
            in_=g["a_wu"].rearrange("(c p) o -> p c o", p=P)),
            waits=[(S["pe"], S["pe"].val), (S["act"], S["act"].val)])
        for n in range(NT):
            for m in range(DC):
                bank = pb[(n * DC + m) % 2]

                def wumm(e, n=n, m=m, bank=bank):
                    last = None
                    for k in range(DC):
                        last = e.matmul(bank[:, :], Wg[:, 1024 * k + 128 * m:1024 * k + 128 * m + 128],
                                        BB[:, k, 512 * n:512 * n + 512], start=(k == 0), stop=(k == DC - 1))
                    return last
                kb.op("pe", wumm, waits=[wu_tok, (S["dve"], S["dve"].val)], inc=S["pe"])
                kb.op("dve", lambda e, n=n, m=m, bank=bank: e.tensor_tensor(
                    ST[:, m, 512 * n:512 * n + 512], ST[:, m, 512 * n:512 * n + 512], bank[:, :], ALU.add),
                    waits=[(S["pe"], S["pe"].val)], inc=S["dve"])
        self.norm_stream("ln", out_bf=lambda c: BA[:, c, :])

    def _C_av(self, pair, qi, pr2, tp_toks):
        kb, sb, S = self.kb, self.sb, self.S
        pb, Wg, CS = sb["pb"], self.Wg, sb["CSTAT"]
        CSb = CS.ap().bitcast(BF16)
        qkv = Wg[:, 0:12288].rearrange("p (w pr t c) -> p w pr t c", w=3, pr=2, t=16)
        Wi = 128 * (qi + 1)
        nch = Wi // 128
        par = qi % 2
        for h2 in range(2):
            ptre = sb["SB2"].ap()[:, 4096 * h2 + 2048 * par:4096 * h2 + 2048 * par + 2048]
            obank = pb[6 + h2]

            def avmm(e, h2=h2, nch=nch, ptre=ptre, obank=obank):
                last = None
                for c in range(nch):
                    last = e.matmul(obank[0:64, 0:128],
                                    qkv[:, 2, pr2, c, 64 * h2:64 * h2 + 64],
                                    ptre[:, 128 * c:128 * c + 128],
                                    start=(c == 0), stop=(c == nch - 1))
                return last
            kb.op("pe", avmm, waits=list(tp_toks) + [(S["dve"], S["dve"].val)], inc=S["pe"])
            av_done = S["pe"].val
            kb.op("pe", lambda e, h2=h2, par=par: e.matmul(
                pb[2][0:1, 0:128], CSb[:, 64 + 4 * h2 + 2 * par:64 + 4 * h2 + 2 * par + 1],
                sb["ident_b"][:, :], start=True, stop=True),
                waits=[(S["dve"], S["dve"].val)], inc=S["pe"])
            kb.op("dve", lambda e: e.tensor_copy(CSb[0:1, 96:224], pb[2][0:1, 0:128]),
                  waits=[(S["pe"], S["pe"].val)], inc=S["dve"])
            kb.op("pe", lambda e: e.matmul(pb[2][0:64, 0:128], sb["ones_b"][0:1, 0:64],
                                           CSb[0:1, 96:224], start=True, stop=True),
                  waits=[(S["dve"], S["dve"].val)], inc=S["pe"])
            kb.op("dve", lambda e: e.tensor_copy(CSb[0:64, 256:384], pb[2][0:64, 0:128]),
                  waits=[(S["pe"], S["pe"].val)], inc=S["dve"])
            kb.op("dve", lambda e, h2=h2, qi=qi, obank=obank: e.tensor_tensor(
                sb["BIGB"][:, pair, 128 * qi:128 * qi + 128][64 * h2:64 * h2 + 64, :],
                obank[0:64, 0:128], CSb[0:64, 256:384], ALU.mult),
                waits=[(S["pe"], av_done)], inc=S["dve"])
            self._C_cbg[2] = [(S["dve"], S["dve"].val), (S["act"], S["act"].val)]

    def phase_D(self, s):
        kb, g, sb, S = self.kb, self.g, self.sb, self.S
        self.nmark = 1000
        ST, BA, BB, SB2, SM = sb["STREAM"], sb["BIGA"], sb["BIGB"], sb["SB2"], sb["SMALL"]
        pb = sb["pb"]
        Wg = self.Wg
        Wf = sb["WBUF"].ap().bitcast(F32)
        Q = S["qp"]
        SMb = SM.ap().bitcast(BF16)
        SB2b = SB2.ap().bitcast(BF16) if False else SB2.ap()
        CS = sb["CSTAT"]
        STf = ST.ap().rearrange("p c t -> p (c t)")
        STb = STf.bitcast(BF16)
        x5T = BB.ap().rearrange("p c t -> p (c t)").rearrange("p (t f) -> p t f", f=D)
        hbuf = STf[:, 0:8192].rearrange("p (a b) -> p a b", b=4096)
        qh = STb[:, 16384:24576].rearrange("p (a b) -> p a b", b=4096)
        qhT = STb[:, 24576:32768]

        if s == 0:
            wtok = None
            for wi, (wname, nrow, ncols) in enumerate([("b_wgate", 1024, 4096),
                                                       ("b_wup", 1024, 4096),
                                                       ("b_wdown", 4096, 1024)]):
                nslab = nrow // P
                for c in range(nslab):
                    buf = 4096 * (c % 2)
                    tok = S["qs"].start(lambda e, c=c, buf=buf, wname=wname, ncols=ncols: e.dma_start(
                        out=Wf[:, buf:buf + ncols], in_=g[wname][128 * c:128 * c + 128, :]),
                        waits=[(S["dve"], S["dve"].val)])
                    kb.op("dve", lambda e, c=c, buf=buf, ncols=ncols: e.tensor_reduce(
                        CS[:, 448 + c % 2:449 + c % 2], Wf[:, buf:buf + ncols], AX.X, ALU.add,
                        apply_absolute_value=True),
                        waits=[tok], inc=S["dve"])
                    if c == 0:
                        kb.op("dve", lambda e: e.tensor_copy(CS[:, 450:451], CS[:, 448:449]),
                              inc=S["dve"])
                    else:
                        kb.op("dve", lambda e, c=c: e.tensor_tensor(
                            CS[:, 450:451], CS[:, 450:451], CS[:, 448 + c % 2:449 + c % 2], ALU.add),
                            inc=S["dve"])
                self.mark()
                kb.op("pe", lambda e: e.matmul(pb[2][0:1, 0:1], sb["ones_f"][:, 0:1], CS[:, 450:451],
                                               start=True, stop=True),
                      waits=[(S["dve"], S["dve"].val)], inc=S["pe"])
                kb.op("dve", lambda e, nrow=nrow, ncols=ncols: e.tensor_scalar(
                    CS[0:1, 451:452], pb[2][0:1, 0:1], 1.0 / (nrow * ncols), 1e-5, ALU.mult, ALU.max),
                    waits=[(S["pe"], S["pe"].val)], inc=S["dve"])
                kb.op("dve", lambda e: e.reciprocal(CS[0:1, 451:452], CS[0:1, 451:452]), inc=S["dve"])
                kb.op("pe", lambda e: e.matmul(pb[2][:, 0:1], sb["ones_f"][0:1, :], CS[0:1, 451:452],
                                               start=True, stop=True),
                      waits=[(S["dve"], S["dve"].val)], inc=S["pe"])
                kb.op("dve", lambda e: e.tensor_copy(CS[:, 452:453], pb[2][:, 0:1]),
                      waits=[(S["pe"], S["pe"].val)], inc=S["dve"])
                kb.op("dve", lambda e, wi=wi: e.reciprocal(CS[:, 192 + wi:193 + wi], CS[:, 452:453]),
                      inc=S["dve"])
                wsc_done = S["dve"].val
                self.mark()
                st_toks = {}
                for c in range(nslab):
                    buf = 4096 * (c % 2)
                    tok = S["qs"].start(lambda e, c=c, buf=buf, wname=wname, ncols=ncols: e.dma_start(
                        out=Wf[:, buf:buf + ncols], in_=g[wname][128 * c:128 * c + 128, :]),
                        waits=[(S["dve"], S["dve"].val)])
                    kb.op("dve", lambda e, buf=buf, ncols=ncols: e.tensor_scalar(
                        Wf[:, buf:buf + ncols], Wf[:, buf:buf + ncols], CS[:, 452:453], MAGIC,
                        ALU.mult, ALU.add),
                        waits=[tok, (S["dve"], wsc_done)], inc=S["dve"])
                    stg = 4096 * (c % 2)
                    w2 = [st_toks[c - 2]] if (c - 2) in st_toks else []
                    kb.op("dve", lambda e, buf=buf, stg=stg, ncols=ncols: e.tensor_scalar(
                        SMb[:, stg:stg + ncols], Wf[:, buf:buf + ncols], MAGIC, None, ALU.subtract),
                        waits=w2, inc=S["dve"])
                    kb.op("dve", lambda e, stg=stg, ncols=ncols: e.tensor_scalar(
                        SMb[:, stg:stg + ncols], SMb[:, stg:stg + ncols], 1.0, -1.0, ALU.min, ALU.max),
                        inc=S["dve"])
                    wtok = S["qs"].start(lambda e, wi=wi, c=c, stg=stg, ncols=ncols: e.dma_start(
                        out=g["scr_wq"][wi, 128 * c * ncols:(128 * c + 128) * ncols]
                        .rearrange("(a b) -> a b", a=128), in_=SMb[:, stg:stg + ncols]),
                        waits=[(S["dve"], S["dve"].val)])
                    st_toks[c] = wtok
            kb.op("pe", lambda e: None, waits=[st_toks[nslab - 1]] +
                  ([st_toks[nslab - 2]] if nslab >= 2 else []), inc=S["pe"])
            self.mark()
        wq_ready = S["pe"].val

        bg = {0: [(S["act"], S["act"].val), (S["dve"], S["dve"].val)],
              1: [(S["act"], S["act"].val), (S["dve"], S["dve"].val)]}
        for t in range(16):
            for c in range(DC):
                bank = pb[c % 2]
                kb.op("pe", lambda e, t=t, c=c, bank=bank: e.matmul(
                    bank[:, 0:128], ST[:, c, 128 * t:128 * t + 128], sb["ident_f"][:, :],
                    start=True, stop=True),
                    waits=bg[c % 2], inc=S["pe"])
                mmd = S["pe"].val
                kb.op("act", lambda e, t=t, c=c, bank=bank: e.copy(
                    x5T[:, t, 128 * c:128 * c + 128], bank[:, 0:128]),
                    waits=[(S["pe"], mmd)], inc=S["act"])
                kb.op("dve", lambda e, t=t, c=c, bank=bank: e.reduce_max(
                    CS[:, 448 + c:449 + c], bank[:, 0:128], AX.X, apply_absolute_value=True),
                    waits=[(S["act"], S["act"].val)], inc=S["dve"])
                bg[c % 2] = [(S["act"], S["act"].val), (S["dve"], S["dve"].val)]
            def amx(e, t=t):
                last = e.tensor_copy(CS[:, t:t + 1], CS[:, 448:449])
                return last
            kb.op("dve", amx, inc=S["dve"])
            for c in range(1, DC):
                kb.op("dve", lambda e, t=t, c=c: e.tensor_tensor(
                    CS[:, t:t + 1], CS[:, t:t + 1], CS[:, 448 + c:449 + c], ALU.max), inc=S["dve"])
            kb.op("dve", lambda e, t=t: e.tensor_scalar(CS[:, t:t + 1], CS[:, t:t + 1],
                                                        1e-5, None, ALU.max), inc=S["dve"])
            kb.op("dve", lambda e, t=t: e.tensor_scalar(CS[:, 16 + t:17 + t], CS[:, t:t + 1],
                                                        1.0 / 127, None, ALU.mult), inc=S["dve"])
            kb.op("dve", lambda e, t=t: e.reciprocal(CS[:, t:t + 1], CS[:, t:t + 1]), inc=S["dve"])
            kb.op("dve", lambda e, t=t: e.tensor_scalar(CS[:, t:t + 1], CS[:, t:t + 1],
                                                        127.0, None, ALU.mult), inc=S["dve"])
        for t in range(16):
            kb.op("dve", lambda e, t=t: e.tensor_tensor(CS[:, 64 + t:65 + t], CS[:, 16 + t:17 + t],
                                                        CS[:, 192:193], ALU.mult), inc=S["dve"])
            kb.op("dve", lambda e, t=t: e.tensor_tensor(CS[:, 80 + t:81 + t], CS[:, 16 + t:17 + t],
                                                        CS[:, 193:194], ALU.mult), inc=S["dve"])
        amax_done = S["dve"].val
        self.mark()
        bg2 = {0: [(S["dve"], S["dve"].val)], 1: [(S["dve"], S["dve"].val)]}
        for t in range(16):
            half_buf = SB2b if t < 8 else SMb
            tl = t % 8
            for c in range(DC):
                bank = pb[2 + c % 2]
                kb.op("pe", lambda e, t=t, c=c, bank=bank: e.matmul(
                    bank[:, 0:128], ST[:, c, 128 * t:128 * t + 128], sb["ident_f"][:, :],
                    start=True, stop=True),
                    waits=bg2[c % 2], inc=S["pe"])
                mmd = S["pe"].val
                kb.op("dve", lambda e, t=t, c=c, bank=bank: e.tensor_scalar(
                    CS[:, 256:384], bank[:, 0:128],
                    CS[:, t:t + 1], MAGIC, ALU.mult, ALU.add),
                    waits=[(S["pe"], mmd), (S["dve"], amax_done)], inc=S["dve"])
                kb.op("dve", lambda e, t=t, c=c, half_buf=half_buf, tl=tl: e.tensor_scalar(
                    half_buf[:, 1024 * c + 128 * tl:1024 * c + 128 * tl + 128],
                    CS[:, 256:384],
                    MAGIC, None, ALU.subtract),
                    waits=[(S["pe"], wq_ready)], inc=S["dve"])
                bg2[c % 2] = [(S["dve"], S["dve"].val)]
        qx_done = S["dve"].val
        self.mark()
        if self.dev_stage == "dqx" and not self.stage_hit:
            self.stage_hit = True
            for half in range(4):
                S["qp"].start(lambda e, half=half: e.dma_start(
                    out=self.g["dbg"][half, :, :],
                    in_=SB2b[:, 2048 * half:2048 * half + 2048]),
                    waits=[(S["dve"], qx_done)])
            raise StopBuild()
        qxT = BA.ap().rearrange("p c t -> p (c t)")
        tks = []
        for half in range(2):
            srcb = SB2b[:, 0:8192] if half == 0 else SMb[:, 0:8192]
            out_ap = BA.ap().rearrange("p c t -> p (c t)")[:, 8192 * half:8192 * half + 8192] \
                .rearrange("p (t c) -> p t c", c=128)
            tks.append(S["qs"].start(lambda e, srcb=srcb, out_ap=out_ap: e.dma_start(
                out=out_ap, in_=srcb, transpose=True),
                waits=[(S["dve"], qx_done), (S["pe"], S["pe"].val), (S["act"], S["act"].val)]))

        if self.dev_stage == "dqxT" and not self.stage_hit:
            self.stage_hit = True
            for c in range(DC):
                S["qp"].start(lambda e, c=c: e.dma_start(
                    out=self.g["dbg"][c, :, :], in_=BA[:, c, :]),
                    waits=tks + [(S["dve"], S["dve"].val)])
            raise StopBuild()
        self.mark()
        self._y_toks = {}
        dbg_g = {k: [] for k in range(4)}
        for tt in range(8):
            tsl = slice(256 * tt, 256 * tt + 256)
            subs = [2 * tt, 2 * tt + 1]
            for oc in range(8):
                gslab = 0 + 4096 * (oc % 2)
                uslab = 8192 + 4096 * (oc % 2)
                gt = Q.start(lambda e, oc=oc, gslab=gslab: e.dma_start(
                    out=Wg[:, gslab:gslab + 4096].rearrange("p (c o) -> p c o", o=512),
                    in_=g["scr_wq"][0].rearrange("(c p o) -> p c o", p=P, o=4096)[:, :, 512 * oc:512 * oc + 512]),
                    waits=[(S["pe"], wq_ready), (S["pe"], S["pe"].val)])
                ut = Q.start(lambda e, oc=oc, uslab=uslab: e.dma_start(
                    out=Wg[:, uslab:uslab + 4096].rearrange("p (c o) -> p c o", o=512),
                    in_=g["scr_wq"][1].rearrange("(c p o) -> p c o", p=P, o=4096)[:, :, 512 * oc:512 * oc + 512]),
                    waits=[])
                for si, sub in enumerate(subs):
                    gb, ub = pb[si], pb[2 + si]

                    def gmm(e, sub=sub, gslab=gslab, gb=gb):
                        last = None
                        base = 8192 * (sub // 8) + 128 * (sub % 8)
                        for c in range(DC):
                            last = e.matmul(gb[:, :], qxT[:, base + 1024 * c:base + 1024 * c + 128],
                                            Wg[:, gslab + 512 * c:gslab + 512 * c + 512],
                                            start=(c == 0), stop=(c == DC - 1))
                        return last
                    kb.op("pe", gmm, waits=[gt] + tks + dbg_g[si], inc=S["pe"])
                    kb.op("act", lambda e, sub=sub, gb=gb, si=si: e.activation(
                        SMb[:, 5120 + 512 * si:5120 + 512 * si + 512],
                        gb[:, :], AF.Silu, scale=CS[:, 64 + sub:65 + sub]),
                        waits=[(S["pe"], S["pe"].val), (S["dve"], S["dve"].val)], inc=S["act"])
                    silu_done = S["act"].val
                    dbg_g[si] = [(S["act"], silu_done)]

                    def umm(e, sub=sub, uslab=uslab, ub=ub):
                        last = None
                        base = 8192 * (sub // 8) + 128 * (sub % 8)
                        for c in range(DC):
                            last = e.matmul(ub[:, :], qxT[:, base + 1024 * c:base + 1024 * c + 128],
                                            Wg[:, uslab + 512 * c:uslab + 512 * c + 512],
                                            start=(c == 0), stop=(c == DC - 1))
                        return last
                    kb.op("pe", umm, waits=[ut] + dbg_g[2 + si], inc=S["pe"])
                    kb.op("dve", lambda e, si=si, oc=oc, ub=ub: e.tensor_tensor(
                        hbuf[:, si, 512 * oc:512 * oc + 512],
                        SMb[:, 5120 + 512 * si:5120 + 512 * si + 512],
                        ub[:, :], ALU.mult),
                        waits=[(S["pe"], S["pe"].val), (S["act"], silu_done)], inc=S["dve"])
                    dbg_g[2 + si] = [(S["dve"], S["dve"].val)]
            h_done = S["dve"].val
            self.mark()
            for si, sub in enumerate(subs):
                kb.op("dve", lambda e, si=si: e.reduce_max(CS[:, 32 + si:33 + si],
                                                           hbuf[:, si, :], AX.X,
                                                           apply_absolute_value=True),
                      waits=[(S["dve"], h_done)], inc=S["dve"])
                kb.op("dve", lambda e, si=si: e.tensor_scalar(CS[:, 32 + si:33 + si],
                                                              CS[:, 32 + si:33 + si], 1e-5, None,
                                                              ALU.max), inc=S["dve"])
                kb.op("dve", lambda e, si=si: e.tensor_scalar(CS[:, 34 + si:35 + si],
                                                              CS[:, 32 + si:33 + si], 1.0 / 127,
                                                              None, ALU.mult), inc=S["dve"])
                kb.op("dve", lambda e, si=si: e.reciprocal(CS[:, 32 + si:33 + si],
                                                           CS[:, 32 + si:33 + si]), inc=S["dve"])
                kb.op("dve", lambda e, si=si: e.tensor_scalar(CS[:, 32 + si:33 + si],
                                                              CS[:, 32 + si:33 + si], 127.0, None,
                                                              ALU.mult), inc=S["dve"])
                kb.op("dve", lambda e, si=si, sub=sub: e.tensor_tensor(
                    CS[:, 96 + sub:97 + sub], CS[:, 34 + si:35 + si], CS[:, 80 + sub:81 + sub],
                    ALU.mult), inc=S["dve"])
                kb.op("dve", lambda e, si=si, sub=sub: e.tensor_tensor(
                    CS[:, 96 + sub:97 + sub], CS[:, 96 + sub:97 + sub], CS[:, 194:195],
                    ALU.mult), inc=S["dve"])
                kb.op("dve", lambda e, si=si: e.tensor_scalar(
                    hbuf[:, si, :], hbuf[:, si, :], CS[:, 32 + si:33 + si], MAGIC, ALU.mult, ALU.add),
                    inc=S["dve"])
                kb.op("dve", lambda e, si=si: e.tensor_scalar(
                    qh[:, si, :], hbuf[:, si, :], MAGIC, None, ALU.subtract), inc=S["dve"])
            self.mark()
            htk = []
            for si, sub in enumerate(subs):
                htk.append(S["qs"].start(lambda e, si=si: e.dma_start(
                    out=qhT[:, 4096 * si:4096 * si + 4096].rearrange("p (t c) -> p t c", c=128),
                    in_=qh[:, si, :], transpose=True),
                    waits=[(S["dve"], S["dve"].val)]))
            for m2 in range(2):
                dtoks = []
                for half in range(2):
                    dtoks.append(Q.start(lambda e, m2=m2, half=half: e.dma_start(
                        out=Wg[:, 8192 * half:8192 * half + 8192].rearrange("p (c o) -> p c o", o=512),
                        in_=g["scr_wq"][2].rearrange("(c p o) -> p c o", p=P, o=D)
                        [:, 16 * half:16 * half + 16, 512 * m2:512 * m2 + 512]),
                        waits=[(S["pe"], S["pe"].val)]))
                for si, sub in enumerate(subs):
                    bank = pb[4 + si]

                    def dmm(e, si=si, bank=bank):
                        last = None
                        for c in range(32):
                            base = 8192 * (c // 16)
                            last = e.matmul(bank[:, :], qhT[:, 4096 * si + 128 * c:4096 * si + 128 * c + 128],
                                            Wg[:, base + 512 * (c % 16):base + 512 * (c % 16) + 512],
                                            start=(c == 0), stop=(c == 31))
                        return last
                    kb.op("pe", dmm, waits=dtoks + htk + [(S["dve"], S["dve"].val)], inc=S["pe"])
                    kb.op("dve", lambda e, si=si, sub=sub, m2=m2, bank=bank: e.tensor_scalar(
                        SM[:, 1024 * si + 512 * m2:1024 * si + 512 * m2 + 512],
                        bank[:, :], CS[:, 96 + sub:97 + sub], None, ALU.mult),
                        waits=[(S["pe"], S["pe"].val)], inc=S["dve"])
                    kb.op("dve", lambda e, si=si, sub=sub, m2=m2: e.tensor_tensor(
                        SM[:, 1024 * si + 512 * m2:1024 * si + 512 * m2 + 512],
                        SM[:, 1024 * si + 512 * m2:1024 * si + 512 * m2 + 512],
                        x5T[:, sub, 512 * m2:512 * m2 + 512], ALU.add), inc=S["dve"])
            for si, sub in enumerate(subs):
                row = SM[:, 1024 * si:1024 * si + 1024]
                kb.op("act", lambda e, row=row, si=si: e.activation(
                    SMb[:, 4096 + 2048 * si:4096 + 2048 * si + 1024], row, AF.Square,
                    accum_out=CS[:, 40 + si:41 + si]),
                    waits=[(S["dve"], S["dve"].val)], inc=S["act"])
                kb.op("dve", lambda e, row=row, si=si: e.reduce_sum(CS[:, 42 + si:43 + si], row, AX.X),
                      waits=[(S["act"], S["act"].val)], inc=S["dve"])
                kb.op("dve", lambda e, si=si: e.tensor_scalar(CS[:, 42 + si:43 + si],
                                                              CS[:, 42 + si:43 + si], 1.0 / D, None,
                                                              ALU.mult), inc=S["dve"])
                kb.op("dve", lambda e, si=si: e.tensor_tensor(CS[:, 44 + si:45 + si],
                                                              CS[:, 42 + si:43 + si],
                                                              CS[:, 42 + si:43 + si], ALU.mult),
                      inc=S["dve"])
                kb.op("dve", lambda e, si=si: e.scalar_tensor_tensor(
                    CS[:, 40 + si:41 + si], CS[:, 40 + si:41 + si], 1.0 / D, CS[:, 44 + si:45 + si],
                    ALU.mult, ALU.subtract), inc=S["dve"])
                kb.op("dve", lambda e, si=si: e.tensor_scalar(CS[:, 40 + si:41 + si],
                                                              CS[:, 40 + si:41 + si], EPS, None,
                                                              ALU.add), inc=S["dve"])
                kb.op("act", lambda e, si=si: e.activation(CS[:, 40 + si:41 + si],
                                                           CS[:, 40 + si:41 + si], AF.Ln),
                      waits=[(S["dve"], S["dve"].val)], inc=S["act"])
                kb.op("act", lambda e, si=si: e.activation(CS[:, 40 + si:41 + si],
                                                           CS[:, 40 + si:41 + si], AF.Exp, scale=-0.5),
                      inc=S["act"])
                kb.op("dve", lambda e, si=si: e.tensor_tensor(CS[:, 46 + si:47 + si],
                                                              CS[:, 42 + si:43 + si],
                                                              CS[:, 40 + si:41 + si], ALU.mult),
                      waits=[(S["act"], S["act"].val)], inc=S["dve"])
                kb.op("dve", lambda e, si=si: e.tensor_scalar(CS[:, 46 + si:47 + si],
                                                              CS[:, 46 + si:47 + si], -1.0, None,
                                                              ALU.mult), inc=S["dve"])
                slot = si + 2 * (tt % 2)
                outb = SB2.ap().bitcast(F32)[:, 1024 * slot:1024 * slot + 1024]
                wprev = [self._y_toks[slot]] if slot in self._y_toks else []
                kb.op("act", lambda e, row=row, outb=outb, si=si: e.activation(
                    outb, row, AF.Identity, scale=CS[:, 40 + si:41 + si],
                    bias=CS[:, 46 + si:47 + si]),
                    waits=[(S["dve"], S["dve"].val)] + wprev, inc=S["act"])
                self._y_toks[slot] = S["qs"].start(lambda e, sub=sub, outb=outb: e.dma_start(
                    out=g["y"][s, 128 * sub:128 * sub + 128, :], in_=outb),
                    waits=[(S["act"], S["act"].val)])
            self.mark()

def prep_inputs(inputs, core):
    i = inputs
    s0 = 2 * core
    x = np.ascontiguousarray(i["L0_attn_hidden_states"][s0:s0 + 2])
    x_f = np.ascontiguousarray(x.reshape(NSEQ, SEQ, DC, P).transpose(0, 2, 3, 1))
    d = {"x_f": x_f}
    d["l0_wq"] = i["L0_attn_Wq"]; d["l0_wk"] = i["L0_attn_Wk"]; d["l0_wv"] = i["L0_attn_Wv"]
    d["l0_wo"] = i["L0_attn_Wo"]
    d["l0_attn_norm_w"] = np.ascontiguousarray(i["L0_attn_attn_norm_w"].reshape(DC, P).T)
    d["l0_mlp_norm_w"] = np.ascontiguousarray(i["L0_attn_mlp_norm_w"].reshape(DC, P).T)
    d["l0_out_norm_w"] = np.ascontiguousarray(i["L0_attn_attn_out_norm_w"].reshape(P, 1))
    d["l0_wgate"] = i["L0_attn_Wgate"]; d["l0_wup"] = i["L0_attn_Wup"]; d["l0_wdown"] = i["L0_attn_Wdown"]
    d["f_router_w1"] = i["L0_ffn_router_w1"]
    d["f_router_b1"] = np.ascontiguousarray(i["L0_ffn_router_b1"].reshape(2, P).T)
    d["f_router_w2"] = i["L0_ffn_router_w2"]
    d["f_router_b2"] = np.ascontiguousarray(i["L0_ffn_router_b2"].reshape(1, 8))
    d["f_halt_w1"] = i["L0_ffn_halt_w1"]
    d["f_halt_b1"] = np.ascontiguousarray(i["L0_ffn_halt_b1"].reshape(2, P).T)
    d["f_halt_w2"] = np.ascontiguousarray(i["L0_ffn_halt_w2"].reshape(2, P).T)
    d["f_halt_b2"] = np.ascontiguousarray(i["L0_ffn_halt_b2"].reshape(1, 1))
    d["f_w1all"] = np.ascontiguousarray(i["L0_ffn_expert_w1"].transpose(1, 0, 2).reshape(2048, 512))
    d["f_b1all"] = np.ascontiguousarray(i["L0_ffn_expert_b1"].reshape(512).reshape(4, P).T)
    d["f_w2cat"] = np.ascontiguousarray(
        np.concatenate([i["L0_ffn_expert_w2"].reshape(512, 3072), i["L0_ffn_expert_b2"]], axis=0))
    d["f_shared_w1"] = i["L0_ffn_shared_w1"]
    b1 = np.zeros((P, 1), np.float32); b1[0:64, 0] = i["L0_ffn_shared_b1"]
    d["f_shared_b1"] = b1
    d["f_shared_w2"] = i["L0_ffn_shared_w2"]
    d["f_shared_b2"] = np.ascontiguousarray(i["L0_ffn_shared_b2"].reshape(24, P).T)
    d["f_shared_weight"] = np.ascontiguousarray(np.float32(i["L0_ffn_shared_weight"]).reshape(1, 1))
    d["f_refine_w1"] = i["L0_ffn_refine_w1"]
    d["f_refine_b1"] = np.ascontiguousarray(i["L0_ffn_refine_b1"].reshape(16, P).T)
    d["f_refine_w2"] = i["L0_ffn_refine_w2"]
    d["f_refine_b2"] = np.ascontiguousarray(i["L0_ffn_refine_b2"].reshape(DC, P).T)
    d["f_mu"] = np.ascontiguousarray(i["L0_ffn_mu"].reshape(DC, P).T)
    d["f_iw"] = np.ascontiguousarray(i["L0_ffn_integration_weight"].reshape(DC, P).T)
    for nm, key in [("a_wq", "L1_attn_Wq"), ("a_wk", "L1_attn_Wk"),
                    ("a_wv", "L1_attn_Wv"), ("a_wu", "L1_attn_Wu")]:
        d[nm] = np.ascontiguousarray(i[key].T)
    for nm, key in [("a_qln_w", "L1_attn_qln_w"), ("a_qln_b", "L1_attn_qln_b"),
                    ("a_kln_w", "L1_attn_kln_w"), ("a_kln_b", "L1_attn_kln_b")]:
        d[nm] = np.ascontiguousarray(np.tile(i[key], 16).reshape(1, D))
    d["b_wgate"] = np.ascontiguousarray(i["L1_ffn_w_gate"].T)
    d["b_wup"] = np.ascontiguousarray(i["L1_ffn_w_up"].T)
    d["b_wdown"] = np.ascontiguousarray(i["L1_ffn_w_down"].T)
    d["c_ident"] = np.eye(P, dtype=np.float32)
    d["c_triu"] = np.triu(np.ones((P, P), np.float32))
    d["c_trilq"] = np.tril(np.ones((P, P), np.float32))
    d["c_trilneg"] = np.where(d["c_trilq"] > 0, np.float32(0), np.float32(-1e38))
    e8 = np.zeros((8, 512), np.float32)
    for e in range(8):
        e8[e, 64 * e:64 * e + 64] = 1.0
    d["c_e8"] = e8
    return {k: np.ascontiguousarray(v, dtype=np.float32) for k, v in d.items()}


def kernel(**inputs):
    inputs = {k: np.asarray(v) for k, v in inputs.items()}
    nc, g, kb = build()
    in_maps = [prep_inputs(inputs, core) for core in range(8)]
    res = run_bass_kernel_spmd(nc, in_maps, core_ids=list(range(8)))
    out = np.empty((16, SEQ, D), np.float32)
    for core in range(8):
        out[2 * core:2 * core + 2] = res.results[core]["y"]
    return out
